# version 12
# baseline (speedup 1.0000x reference)
import numpy as np
import ml_dtypes
import concourse.bass as bass
import concourse.mybir as mybir
from concourse.bass_utils import run_bass_kernel_spmd

F32 = mybir.dt.float32
BF16 = mybir.dt.bfloat16
F8 = mybir.dt.float8e4
I32 = mybir.dt.int32
AF = mybir.ActivationFunctionType
ALU = mybir.AluOpType
AXL = mybir.AxisListType

NCORES = 8
B = 2
L = 8192
DM = 1024
T = 2048
MAGIC = 1.5 * 2.0 ** 23

DSSM = 2048
DST = 128
NH0 = 32
HD0 = 64
CONV_DIM = DSSM + 2 * DST
D_IN_PROJ = 2 * DSSM + 2 * DST + NH0
CH0 = 256
NCH0 = T // CH0

H1 = 8
KD1 = 128
VD1 = 256
KEY_DIM = H1 * KD1
VAL_DIM = H1 * VD1
CH1 = 64
NCH1 = T // CH1

bf16 = ml_dtypes.bfloat16


def cdiv(a, b):
    return (a + b - 1) // b


class Ctx:

    ENGS = ('v', 's', 't', 'g', 'y')
    PHASE_ATTRS = ('_pv', '_gelu_scr', '_gelu_guard', '_g16cc', '_scr', '_g', '_r2use0',
                   '_r2use1', '_sres2', '_r2ap', '_hl', '_po', '_pc', '_last', '_so2')

    def __init__(self, nc):
        self.nc = nc
        self.sem = {e: nc.alloc_semaphore('ctr_' + e) for e in self.ENGS}
        self.cnt = {e: 0 for e in self.ENGS}
        self.script = {e: [] for e in self.ENGS}
        self.dsem = {}
        self.dcnt = {}
        import contextlib
        self._stack = contextlib.ExitStack()
        self.pfx = ''

    def sb(self, name, shape, dtype):
        return self._stack.enter_context(self.nc.sbuf_tensor(self.pfx + name, shape, dtype))

    def ps(self, name, shape, dtype):
        return self._stack.enter_context(self.nc.psum_tensor(self.pfx + name, shape, dtype))

    def barrier(self):
        toks = [(self.sem[e], self.cnt[e]) for e in self.ENGS if self.cnt[e] > 0]
        toks += [(self.dsem[n], self.dcnt[n]) for n in self.dsem]
        for e in self.ENGS:
            def fn(eng, toks=toks):
                for (sem, val) in toks:
                    eng.wait_ge(sem, val)
            self.script[e].append(fn)

    def end_phase(self):
        self.barrier()
        self._stack.close()
        import contextlib
        self._stack = contextlib.ExitStack()
        for a in self.PHASE_ATTRS:
            if hasattr(self, a):
                delattr(self, a)

    def has_dma(self, name):
        return ('g.' + name) in self.dsem or ('h.' + name) in self.dsem

    def _dma_sem(self, name):
        if name not in self.dsem:
            self.dsem[name] = self.nc.alloc_semaphore('dma_' + name)
            self.dcnt[name] = 0
        return self.dsem[name]

    def _wrap(self, eng_key, emit, waits, inc):
        def fn(eng):
            for (sem, val) in waits:
                eng.wait_ge(sem, val)
            instr = emit(eng)
            if instr is not None and inc:
                instr.then_inc(self.sem[eng_key], 1)
        return fn

    def op(self, e, emit, waits=(), inc=True):
        wl = [w for w in waits if w is not None]
        self.script[e].append(self._wrap(e, emit, wl, inc))
        if inc:
            self.cnt[e] += 1
        return (self.sem[e], self.cnt[e])

    def dma(self, e, sem_name, out_ap, in_ap, waits=(), serial=False):
        sem_name = ('g.' if e == 'g' else 'h.') + sem_name
        sem = self._dma_sem(sem_name)
        wl = [w for w in waits if w is not None]
        if serial and self.dcnt[sem_name] > 0:
            wl = wl + [(sem, self.dcnt[sem_name])]
        def emit(eng):
            eng.dma_start(out=out_ap, in_=in_ap).then_inc(sem, 16)
            return None
        self.script[e].append(self._wrap(e, emit, wl, False))
        self.dcnt[sem_name] += 16
        return (sem, self.dcnt[sem_name])

    def dtok(self, name):
        if name in self.dsem:
            return (self.dsem[name], self.dcnt[name])
        g, h = 'g.' + name in self.dsem, 'h.' + name in self.dsem
        if g and h:
            raise KeyError('ambiguous dma sem ' + name)
        key = ('g.' if g else 'h.') + name
        return (self.dsem[key], self.dcnt[key])

    def tok(self, e):
        return (self.sem[e], self.cnt[e])

    def finish(self):
        for name in self.dsem:
            sem, cnt = self.dsem[name], self.dcnt[name]
            self.script['y'].append(lambda eng, s=sem, c=cnt: eng.wait_ge(s, c))

    def const_col(self, value, name=None):
        t = self.sb(name or ('c_%d' % len(self.script['v'])), [128, 1], F32)
        tok = self.op('v', lambda v: v.memset(t[:], value))
        return t, tok

    def emit(self):
        self.finish()
        with self.nc.Block() as block:
            for e, attr in (('y', 'sync'), ('g', 'gpsimd'), ('v', 'vector'),
                            ('s', 'scalar'), ('t', 'tensor')):
                fns = self.script[e]
                if not fns:
                    continue
                def body(eng, fns=fns):
                    for fn in fns:
                        fn(eng)
                getattr(block, attr)(body)



def build_linear(K, OD, Tn, bias=False, residual=False, wscale=False,
                 out_dtype=F32, x_hbm_dtype=F32, res_dtype=F32,
                 stage_dtype=None, post_od=None, residual2=False,
                 nc=None, ctx=None, io=None):
    own = nc is None
    if own:
        nc = bass.Bass('TRN2', target_bir_lowering=False)
    if ctx is None:
        ctx = Ctx(nc)
    x = io['x'] if io is not None else nc.dram_tensor('x', [K, Tn], x_hbm_dtype, kind='ExternalInput').ap()
    w = io['w'] if io is not None else nc.dram_tensor('w', [K, OD], F32, kind='ExternalInput').ap()
    b_ap = (io['bias'] if io is not None else nc.dram_tensor('bias', [OD, 1], F32, kind='ExternalInput').ap()) if bias else None
    r_ap = (io['resid'] if io is not None else nc.dram_tensor('resid', [OD, Tn], res_dtype, kind='ExternalInput').ap()) if residual else None
    ws_ap = (io['wscale'] if io is not None else nc.dram_tensor('wscale', [K, 1], F32, kind='ExternalInput').ap()) if wscale else None
    r2_ap = (io['resid2'] if io is not None else nc.dram_tensor('resid2', [OD, Tn], F32, kind='ExternalInput').ap()) if residual2 else None
    out = io['out'] if io is not None else nc.dram_tensor('out', [OD, Tn], out_dtype, kind='ExternalOutput').ap()

    nk = K // 128
    od_tiles = []
    od0 = 0
    while od0 < OD:
        odp = min(128, OD - od0)
        od_tiles.append((od0, odp))
        od0 += odp
    TOKBLK = 512
    ntok = cdiv(Tn, TOKBLK)

    sx = ctx.sb('sx', [128, nk * Tn], BF16)
    sw = ctx.sb('sw', [128, nk * OD], BF16)
    sbias = ctx.sb('sbias', [128, len(od_tiles)], F32) if bias else None
    sws = ctx.sb('sws', [128, nk], F32) if wscale else None
    sres = ctx.sb('sres', [128, nk_res * Tn], res_dtype) \
        if residual and (nk_res := len(od_tiles)) else None
    sres2 = ctx.sb('sres2', [128, 2 * Tn], F32) if residual2 else None
    sout = ctx.sb('sout', [128, 2 * Tn], stage_dtype or out_dtype)
    pz = [ctx.ps('pz%d' % i, [128, TOKBLK], F32) for i in range(2)]


    for k in range(nk):
        ctx.dma('g', 'ld', sx[:, k * Tn:(k + 1) * Tn], x[k * 128:(k + 1) * 128, :])
    for k in range(nk):
        ctx.dma('g', 'ld', sw[:, k * OD:(k + 1) * OD], w[k * 128:(k + 1) * 128, :])
    if bias:
        for i, (o0, op_) in enumerate(od_tiles):
            ctx.dma('y', 'ldh', sbias[0:op_, i:i + 1], b_ap[o0:o0 + op_, :])
    if wscale:
        for k in range(nk):
            ctx.dma('y', 'ldh', sws[:, k:k + 1], ws_ap[k * 128:(k + 1) * 128, :])
    if residual:
        for i, (o0, op_) in enumerate(od_tiles):
            ctx.dma('y', 'ldh', sres[0:op_, i * Tn:(i + 1) * Tn], r_ap[o0:o0 + op_, :])

    gl = ctx.dtok('ld')
    hl = ctx.dtok('ldh') if ctx.has_dma('ldh') else None

    wtok = None
    if wscale:
        for k in range(nk):
            wtok = ctx.op('v', lambda v, k=k: v.tensor_scalar(
                sw[:, k * OD:(k + 1) * OD], sw[:, k * OD:(k + 1) * OD],
                sws[:, k:k + 1], None, ALU.mult), waits=[gl, hl])

    prev_evac = {}
    store_cnt = {0: 0, 1: 0}
    for i, (o0, odp) in enumerate(od_tiles):
        par = i % 2
        ot = sout[0:odp, par * Tn:par * Tn + Tn]
        last_ev = None
        for j in range(ntok):
            t0 = j * TOKBLK
            tn = min(TOKBLK, Tn - t0)
            bank = (i * ntok + j) % 2
            ps = pz[bank]
            for k in range(nk):
                def mm(te, k=k, o0=o0, odp=odp, t0=t0, tn=tn, ps=ps):
                    return te.matmul(ps[0:odp, 0:tn],
                                     sw[:, k * OD + o0:k * OD + o0 + odp],
                                     sx[:, k * Tn + t0:k * Tn + t0 + tn],
                                     start=(k == 0), stop=(k == nk - 1))
                w8 = [gl, wtok, prev_evac.get(bank)] if k == 0 else []
                tk = ctx.op('t', mm, waits=w8)
            if False:
                pass
            else:
                def ev_fn(v, i=i, odp=odp, t0=t0, tn=tn, ps=ps, ot=ot):
                    if residual:
                        return v.scalar_tensor_tensor(
                            ot[:, t0:t0 + tn], ps[0:odp, 0:tn],
                            sbias[0:odp, i:i + 1] if bias else 0.0,
                            sres[0:odp, i * Tn + t0:i * Tn + t0 + tn],
                            ALU.add, ALU.add)
                    elif bias:
                        return v.tensor_scalar(ot[:, t0:t0 + tn], ps[0:odp, 0:tn],
                                               sbias[0:odp, i:i + 1], None, ALU.add)
                    else:
                        return v.tensor_copy(ot[:, t0:t0 + tn], ps[0:odp, 0:tn])
                wpar = ctx.dtok('st%d' % par) if ctx.has_dma('st%d' % par) else None
                ev = ctx.op('v', ev_fn, waits=[tk, hl, wpar])
            prev_evac[bank] = ev
            last_ev = ev
        if post_od is not None:
            ctx._sres2 = sres2
            ctx._r2ap = r2_ap
            ctx._hl = hl
            src_ap, last_ev = post_od(nc, ctx, i, o0, odp, ot, par, last_ev)
        else:
            src_ap = ot
        ctx.dma('y', 'st%d' % par, out[o0:o0 + odp, :], src_ap, waits=[last_ev],
                serial=True)

    if own:
        ctx.emit()
    return nc



def build_ln(R, D, eps=1e-5, out_dtype=F32, nc=None, ctx=None, io=None):
    own = nc is None
    if own:
        nc = bass.Bass('TRN2', target_bir_lowering=False)
    if ctx is None:
        ctx = Ctx(nc)
    x = io['x'] if io is not None else nc.dram_tensor('x', [R, D], F32, kind='ExternalInput').ap()
    out = io['out'] if io is not None else nc.dram_tensor('out', [R, D], out_dtype, kind='ExternalOutput').ap()
    nr = R // 128
    sx = ctx.sb('sx', [128, 2 * D], F32)
    sq = ctx.sb('sq', [128, 2 * D], F32)
    sm = ctx.sb('sm', [128, 2 * 8], F32)
    so = ctx.sb('so', [128, 2 * D], out_dtype)
    eps_ap, eps_tok = ctx.const_col(eps, 'eps')
    prev_use = {}
    for r in range(nr):
        par = r % 2
        xt = sx[:, par * D:(par + 1) * D]
        qt = sq[:, par * D:(par + 1) * D]
        mt = sm[:, par * 8:(par + 1) * 8]
        otile = so[:, par * D:(par + 1) * D]
        lt = ctx.dma('y', 'x%d' % par, xt, x[r * 128:(r + 1) * 128, :],
                     waits=[prev_use.get(par)], serial=True)
        t1 = ctx.op('v', lambda v, xt=xt, mt=mt: v.reduce_sum(mt[:, 0:1], xt, axis=AXL.X),
                    waits=[lt])
        t2 = ctx.op('s', lambda s, xt=xt, qt=qt, mt=mt: s.activation(
            qt, xt, AF.Square, accum_out=mt[:, 1:2]), waits=[lt])
        t3 = ctx.op('v', lambda v, mt=mt: v.tensor_scalar(mt[:, 0:1], mt[:, 0:1],
                                                          1.0 / D, None, ALU.mult),
                    waits=[t1])
        t4 = ctx.op('v', lambda v, mt=mt: v.tensor_tensor(mt[:, 2:3], mt[:, 0:1],
                                                          mt[:, 0:1], ALU.mult), waits=[t3])
        t5 = ctx.op('v', lambda v, mt=mt: v.tensor_scalar(mt[:, 1:2], mt[:, 1:2],
                                                          1.0 / D, None, ALU.mult),
                    waits=[t2])
        t6 = ctx.op('v', lambda v, mt=mt: v.tensor_tensor(mt[:, 3:4], mt[:, 1:2],
                                                          mt[:, 2:3], ALU.subtract),
                    waits=[t4, t5])
        t7 = ctx.op('s', lambda s, mt=mt: s.activation(mt[:, 4:5], mt[:, 3:4], AF.Sqrt,
                                                       bias=eps_ap[:, 0:1]), waits=[t6, eps_tok])
        t8 = ctx.op('v', lambda v, mt=mt: v.reciprocal(mt[:, 5:6], mt[:, 4:5]), waits=[t7])
        stw = ctx.dtok('o%d' % par) if ctx.has_dma('o%d' % par) else None
        t9 = ctx.op('v', lambda v, xt=xt, otile=otile, mt=mt: v.tensor_scalar(
            otile, xt, mt[:, 0:1], mt[:, 5:6], ALU.subtract, ALU.mult),
            waits=[t8, stw])
        prev_use[par] = t9
        ctx.dma('y', 'o%d' % par, out[r * 128:(r + 1) * 128, :], otile,
                waits=[t9], serial=True)
    if own:
        ctx.emit()
    return nc



def build_mamba_pre(Tn=T, nch=None, ch=CH0, nc=None, ctx=None, io=None, halo_mask=False):
    if nch is None:
        nch = Tn // ch
    own = nc is None
    if own:
        nc = bass.Bass('TRN2', target_bir_lowering=False)
    if ctx is None:
        ctx = Ctx(nc)
    xbc = io['xbc_pad'] if io is not None else nc.dram_tensor('xbc_pad', [CONV_DIM, Tn + 3], F32, kind='ExternalInput').ap()
    cw = io['conv_w'] if io is not None else nc.dram_tensor('conv_w', [CONV_DIM, 4], F32, kind='ExternalInput').ap()
    cb = io['conv_b'] if io is not None else nc.dram_tensor('conv_b', [CONV_DIM, 1], F32, kind='ExternalInput').ap()
    dtr = io['dt_raw'] if io is not None else nc.dram_tensor('dt_raw', [NH0, Tn], F32, kind='ExternalInput').ap()
    dtb = io['dt_bias'] if io is not None else nc.dram_tensor('dt_bias', [NH0, 1], F32, kind='ExternalInput').ap()
    alog = io['A_log'] if io is not None else nc.dram_tensor('A_log', [NH0, 1], F32, kind='ExternalInput').ap()
    convout = io['convout'] if io is not None else nc.dram_tensor('convout', [CONV_DIM, Tn], F32, kind='ExternalOutput').ap()
    dt_o = io['dt'] if io is not None else nc.dram_tensor('dt', [NH0, Tn], F32, kind='ExternalOutput').ap()
    cs_o = io['cs'] if io is not None else nc.dram_tensor('cs', [NH0, Tn], F32, kind='ExternalOutput').ap()
    expcs_o = io['expcs'] if io is not None else nc.dram_tensor('expcs', [NH0, Tn], F32, kind='ExternalOutput').ap()
    dsdt_o = io['dsdt'] if io is not None else nc.dram_tensor('dsdt', [NH0, Tn], F32, kind='ExternalOutput').ap()

    cslast_h = io.get('cslast') if io is not None else None
    hm_ap = io.get('hmask') if io is not None else None
    nct = CONV_DIM // 128
    sx = ctx.sb('sx', [128, 2 * (Tn + 3)], F32)
    swc = ctx.sb('swc', [128, nct * 4], F32)
    sbc = ctx.sb('sbc', [128, nct], F32)
    sacc = ctx.sb('sacc', [128, 2 * Tn], F32)
    sco = ctx.sb('sco', [128, 2 * Tn], F32)
    sdt = ctx.sb('sdt', [NH0, 8 * Tn], F32)
    scol = ctx.sb('scol', [NH0, 4], F32)

    one_ap, one_tok = ctx.const_col(1.0, 'one')

    U = sdt[:, 0 * Tn:1 * Tn]
    AB = sdt[:, 1 * Tn:2 * Tn]
    DT = sdt[:, 2 * Tn:3 * Tn]
    CS = sdt[:, 3 * Tn:4 * Tn]
    EC = sdt[:, 4 * Tn:5 * Tn]
    DS = sdt[:, 5 * Tn:6 * Tn]
    TMP = sdt[:, 6 * Tn:7 * Tn]
    ZERO = sdt[:, 7 * Tn:8 * Tn]

    for i in range(nct):
        ctx.dma('y', 'cw', swc[:, 4 * i:4 * i + 4], cw[i * 128:(i + 1) * 128, :])
        ctx.dma('y', 'cw', sbc[:, i:i + 1], cb[i * 128:(i + 1) * 128, :])
    ctx.dma('y', 'dt', sdt[:, 0:Tn], dtr[:, :])
    ctx.dma('y', 'dt', scol[:, 0:1], dtb[:, :])
    ctx.dma('y', 'dt', scol[:, 1:2], alog[:, :])
    shm = ctx.sb('shm', [128, 1], F32) if hm_ap is not None else None
    if hm_ap is not None:
        ctx.dma('y', 'cw', shm[:, :], hm_ap[:, :])
    cwl = ctx.dtok('cw')
    dtl = ctx.dtok('dt')

    z0 = ctx.op('v', lambda v: v.memset(ZERO, 0.0))
    t = ctx.op('v', lambda v: v.tensor_scalar(U, U, scol[:, 0:1], None, ALU.add),
               waits=[dtl])
    t = ctx.op('s', lambda s: s.activation(AB, U, AF.Abs), waits=[t])
    t = ctx.op('s', lambda s: s.activation(AB, AB, AF.Exp, scale=-1.0), waits=[t])
    t = ctx.op('s', lambda s: s.activation(AB, AB, AF.Ln, bias=one_ap[0:NH0, 0:1]), waits=[t, one_tok])
    t2 = ctx.op('v', lambda v: v.tensor_scalar(TMP, U, 0.0, None, ALU.max), waits=[t])
    tdt = ctx.op('v', lambda v: v.tensor_tensor(DT, AB, TMP, ALU.add), waits=[t, t2])
    t = ctx.op('s', lambda s: s.activation(scol[:, 2:3], scol[:, 1:2], AF.Exp), waits=[dtl])
    t = ctx.op('v', lambda v: v.tensor_scalar(scol[:, 2:3], scol[:, 2:3], -1.0, None,
                                              ALU.mult), waits=[t])
    tda = ctx.op('v', lambda v: v.tensor_scalar(TMP, DT, scol[:, 2:3], None, ALU.mult),
                 waits=[t, tdt])
    tcs = None
    for c in range(nch):
        sl = slice(c * ch, (c + 1) * ch)
        tcs = ctx.op('v', lambda v, sl=sl: v.tensor_tensor_scan(
            CS[:, sl], TMP[:, sl], ZERO[:, sl], 0.0, ALU.add, ALU.add),
            waits=[tda, z0])
    tec = ctx.op('s', lambda s: s.activation(EC, CS, AF.Exp), waits=[tcs])
    tds = None
    for c in range(nch):
        sl = slice(c * ch, (c + 1) * ch)
        last = slice((c + 1) * ch - 1, (c + 1) * ch)
        tt = ctx.op('s', lambda s, sl=sl, last=last: s.activation(
            DS[:, sl], CS[:, sl], AF.Exp, scale=-1.0, bias=CS[:, last]), waits=[tcs])
        tds = tt
    tds2 = ctx.op('v', lambda v: v.tensor_tensor(DS, DS, DT, ALU.mult), waits=[tds, tdt])
    ctx.dma('y', 'odt', dt_o[:, :], DT, waits=[tdt])
    ctx.dma('y', 'ocs', cs_o[:, :], CS, waits=[tcs])
    if cslast_h is not None:
        tcl = ctx.op('v', lambda v: v.tensor_copy(
            TMP[:, 0:nch],
            CS.rearrange('h (c w) -> h c w', w=ch)[:, :, ch - 1]), waits=[tcs])
        ctx.dma('y', 'ocs', cslast_h[:, :], TMP[:, 0:nch], waits=[tcl])
    ctx.dma('y', 'oec', expcs_o[:, :], EC, waits=[tec])
    ctx.dma('y', 'ods', dsdt_o[:, :], DS, waits=[tds2])

    prev_use = {}
    for i in range(nct):
        par = i % 2
        xt = sx[:, par * (Tn + 3):par * (Tn + 3) + Tn + 3]
        at = sacc[:, par * Tn:(par + 1) * Tn]
        ct = sco[:, par * Tn:(par + 1) * Tn]
        stw = ctx.dtok('co%d' % par) if ctx.has_dma('co%d' % par) else None
        lt = ctx.dma('g', 'x%d' % par, xt, xbc[i * 128:(i + 1) * 128, :], serial=True,
                     waits=[prev_use.get(par)])
        if hm_ap is not None:
            lt = ctx.op('v', lambda v, xt=xt: v.tensor_scalar(
                xt[:, 0:3], xt[:, 0:3], shm[:, 0:1], None, ALU.mult), waits=[lt, cwl])
        t = ctx.op('v', lambda v, xt=xt, at=at, i=i: v.tensor_scalar(
            at, xt[:, 0:Tn], swc[:, 4 * i:4 * i + 1], sbc[:, i:i + 1],
            ALU.mult, ALU.add), waits=[lt, cwl, prev_use.get(('silu', par))])
        for k in range(1, 4):
            t = ctx.op('v', lambda v, xt=xt, at=at, i=i, k=k: v.scalar_tensor_tensor(
                at, xt[:, k:k + Tn], swc[:, 4 * i + k:4 * i + k + 1], at,
                ALU.mult, ALU.add), waits=[t])
        prev_use[par] = t
        t = ctx.op('s', lambda s, at=at, ct=ct: s.activation(ct, at, AF.Silu),
                   waits=[t, stw])
        prev_use[('silu', par)] = t
        ctx.dma('y', 'co%d' % par, convout[i * 128:(i + 1) * 128, :], ct,
                waits=[t], serial=True)

    if own:
        ctx.emit()
    return nc



def build_ssd_local(Tn=T, ch=CH0, nc=None, ctx=None, io=None):
    nch = Tn // ch
    njt = ch // 128
    ncx = DSSM // 128
    own = nc is None
    if own:
        nc = bass.Bass('TRN2', target_bir_lowering=False)
    if ctx is None:
        ctx = Ctx(nc)
    x_cm = io['x_cm'] if io is not None else nc.dram_tensor('x_cm', [DSSM, Tn], F32, kind='ExternalInput').ap()
    b_cm = io['B_cm'] if io is not None else nc.dram_tensor('B_cm', [DST, Tn], F32, kind='ExternalInput').ap()
    c_cm = io['C_cm'] if io is not None else nc.dram_tensor('C_cm', [DST, Tn], F32, kind='ExternalInput').ap()
    dt_tm = io['dt_tm'] if io is not None else nc.dram_tensor('dt_tm', [Tn, NH0], F32, kind='ExternalInput').ap()
    cs_row = io['cs_row'] if io is not None else nc.dram_tensor('cs_row', [NH0, Tn], F32, kind='ExternalInput').ap()
    cs_tm = io['cs_tm'] if io is not None else nc.dram_tensor('cs_tm', [Tn, NH0], F32, kind='ExternalInput').ap()
    dsdt_tm = io['dsdt_tm'] if io is not None else nc.dram_tensor('dsdt_tm', [Tn, NH0], F32, kind='ExternalInput').ap()
    d_bc = io['D_bc'] if io is not None else nc.dram_tensor('D_bc', [128, NH0], F32, kind='ExternalInput').ap()
    mask_h = io['mask'] if io is not None else nc.dram_tensor('mask', [ch, ch], F32, kind='ExternalInput').ap()
    ones1_h = io['ones1'] if io is not None else nc.dram_tensor('ones1', [1, 128], F32, kind='ExternalInput').ap()
    ident_h = io['ident'] if io is not None else nc.dram_tensor('ident', [128, 128], BF16, kind='ExternalInput').ap()
    ydx_o = io['ydx'] if io is not None else nc.dram_tensor('ydx', [Tn, DSSM], F32, kind='ExternalOutput').ap()
    st_o = io['states'] if io is not None else nc.dram_tensor('states', [nch * DSSM, DST], F32, kind='ExternalOutput').ap()

    sxu = ctx.sb('sxu', [128, ncx * ch], BF16)
    sxt = ctx.sb('sxt', [128, njt * DSSM], BF16)
    sbu = ctx.sb('sbu', [128, ch], BF16)
    sbt = ctx.sb('sbt', [128, njt * DST], BF16)
    scu = ctx.sb('scu', [128, ch], BF16)
    s_dt = ctx.sb('s_dt', [128, njt * NH0], F32)
    s_cst = ctx.sb('s_cst', [128, njt * NH0], F32)
    s_dsd = ctx.sb('s_dsd', [128, njt * NH0], F32)
    s_csr = ctx.sb('s_csr', [1, NH0 * ch], F32)
    s_db = ctx.sb('s_db', [128, NH0], F32)
    s_mask = ctx.sb('s_mask', [128, njt * ch], F32)
    s_ones = ctx.sb('s_ones', [1, 128], F32)
    s_id = ctx.sb('s_id', [128, 128], BF16)
    s_sm = ctx.sb('s_sm', [128, njt * ch], BF16)
    s_csb = ctx.sb('s_csb', [128, ch], F32)
    s_g = ctx.sb('s_g', [128, njt * ch], F32)
    s_e = ctx.sb('s_e', [128, njt * ch], BF16)
    s_m = ctx.sb('s_m', [128, njt * ch], BF16)
    s_xdt = ctx.sb('s_xdt', [128, njt * HD0], BF16)
    s_bt2 = ctx.sb('s_bt2', [128, njt * DST], BF16)
    s_y = ctx.sb('s_y', [128, njt * DSSM], F32)
    s_st = ctx.sb('s_st', [128, ncx * DST], F32)

    p_tr = [ctx.ps('ptr%d' % i, [128, 128], BF16) for i in range(2)]
    p_s = [ctx.ps('p_s%d' % i, [128, ch], F32) for i in range(njt)]
    p_bc = ctx.ps('p_bc', [128, ch], F32)
    p_y = [ctx.ps('p_y%d' % i, [128, HD0], F32) for i in range(njt)]
    p_st = ctx.ps('p_st', [64, DST], F32)


    ctx.dma('y', 'const', s_db[:, :], d_bc[:, :])
    for jt in range(njt):
        ctx.dma('y', 'const', s_mask[:, jt * ch:(jt + 1) * ch],
                mask_h[jt * 128:(jt + 1) * 128, :])
    ctx.dma('y', 'const', s_ones[:, :], ones1_h[:, :])
    ctx.dma('y', 'const', s_id[:, :], ident_h[:, :])
    ctok = ctx.dtok('const')

    for u in range(nch):
        t0 = u * ch
        ub = [ctx.tok('v'), ctx.tok('t'), ctx.tok('s'),
              ctx.dtok('oy') if ctx.has_dma('oy') else None,
              ctx.dtok('ost') if ctx.has_dma('ost') else None]
        for i in range(ncx):
            ctx.dma('g', 'ld', sxu[:, i * ch:(i + 1) * ch],
                    x_cm[i * 128:(i + 1) * 128, t0:t0 + ch], waits=ub, serial=True)
        ctx.dma('g', 'ld', sbu[:, :], b_cm[:, t0:t0 + ch], waits=ub, serial=True)
        ctx.dma('g', 'ld', scu[:, :], c_cm[:, t0:t0 + ch], waits=ub, serial=True)
        for jt in range(njt):
            r0 = t0 + jt * 128
            ctx.dma('y', 'ldh', s_dt[:, jt * NH0:(jt + 1) * NH0], dt_tm[r0:r0 + 128, :],
                    waits=ub, serial=True)
            ctx.dma('y', 'ldh', s_cst[:, jt * NH0:(jt + 1) * NH0], cs_tm[r0:r0 + 128, :],
                    waits=ub, serial=True)
            ctx.dma('y', 'ldh', s_dsd[:, jt * NH0:(jt + 1) * NH0], dsdt_tm[r0:r0 + 128, :],
                    waits=ub, serial=True)
        ctx.dma('y', 'ldh', s_csr[:, :].rearrange('o (h w) -> o h w', w=ch),
                cs_row[:, t0:t0 + ch].unsqueeze(0), waits=ub, serial=True)
        ld = ctx.dtok('ld')
        ldh = ctx.dtok('ldh')

        prev_ptr = {0: None, 1: None}
        for idx in range(ncx * njt + njt):
            if idx < ncx * njt:
                i, jt = divmod(idx, njt)
                src_ap = sxu[:, i * ch + jt * 128:i * ch + (jt + 1) * 128]
                dst_ap = sxt[:, jt * DSSM + i * 128:jt * DSSM + (i + 1) * 128]
            else:
                jt = idx - ncx * njt
                src_ap = sbu[:, jt * 128:(jt + 1) * 128]
                dst_ap = sbt[:, jt * DST:(jt + 1) * DST]
            pk = idx % 2
            tk = ctx.op('t', lambda te, s=src_ap, pk=pk: te.transpose(
                p_tr[pk][:, :], s, s_id[:, :]), waits=[ld, ctok, prev_ptr[pk]])
            ev = ctx.op('v', lambda v, d=dst_ap, pk=pk: v.tensor_copy(d, p_tr[pk][:, :]),
                        waits=[tk])
            prev_ptr[pk] = ev
        ttr = ctx.tok('v')

        tsm = []
        for jt in range(njt):
            tk = ctx.op('t', lambda te, jt=jt: te.matmul(
                p_s[jt][:, :], sbu[:, jt * 128:(jt + 1) * 128], scu[:, :],
                start=True, stop=True), waits=[ld, ub[0]])
            ev = ctx.op('v', lambda v, jt=jt: v.scalar_tensor_tensor(
                s_sm[:, jt * ch:(jt + 1) * ch], p_s[jt][:, :], 0.0,
                s_mask[:, jt * ch:(jt + 1) * ch], ALU.add, ALU.mult),
                waits=[tk, ctok])
            tsm.append(ev)

        prev = {}
        for h in range(NH0):
            tbc = ctx.op('t', lambda te, h=h: te.matmul(
                p_bc[:, :], s_ones[:, :], s_csr[0:1, h * ch:(h + 1) * ch],
                start=True, stop=True),
                waits=[ldh, ctok, prev.get('csb')])
            ebc = ctx.op('v', lambda v: v.tensor_copy(s_csb[:, :], p_bc[:, :]),
                         waits=[tbc, prev.get('csb_use')])
            tg = []
            for jt in range(njt):
                t1 = ctx.op('v', lambda v, jt=jt, h=h: v.tensor_scalar(
                    s_g[:, jt * ch:(jt + 1) * ch], s_csb[:, :],
                    s_cst[:, jt * NH0 + h:jt * NH0 + h + 1], 0.0,
                    ALU.subtract, ALU.min), waits=[ebc, ldh, prev.get(('g', jt))])
                t2 = ctx.op('s', lambda s, jt=jt: s.activation(
                    s_e[:, jt * ch:(jt + 1) * ch], s_g[:, jt * ch:(jt + 1) * ch], AF.Exp),
                    waits=[t1, prev.get(('e', jt))])
                prev[('g', jt)] = t2
                t3 = ctx.op('v', lambda v, jt=jt: v.tensor_tensor(
                    s_m[:, jt * ch:(jt + 1) * ch], s_e[:, jt * ch:(jt + 1) * ch],
                    s_sm[:, jt * ch:(jt + 1) * ch], ALU.mult),
                    waits=[t2, tsm[jt], prev.get(('m', jt))])
                prev[('e', jt)] = t3
                tg.append(t3)
            prev['csb_use'] = tg[0]
            prev['csb'] = tg[njt - 1]
            txd = []
            for jt in range(njt):
                t4 = ctx.op('v', lambda v, jt=jt, h=h: v.tensor_scalar(
                    s_xdt[:, jt * HD0:(jt + 1) * HD0],
                    sxt[:, jt * DSSM + h * HD0:jt * DSSM + (h + 1) * HD0],
                    s_dt[:, jt * NH0 + h:jt * NH0 + h + 1], None, ALU.mult),
                    waits=[ttr, ldh, prev.get(('xdt', jt))])
                t5 = ctx.op('v', lambda v, jt=jt, h=h: v.tensor_scalar(
                    s_bt2[:, jt * DST:(jt + 1) * DST], sbt[:, jt * DST:(jt + 1) * DST],
                    s_dsd[:, jt * NH0 + h:jt * NH0 + h + 1], None, ALU.mult),
                    waits=[ttr, ldh, prev.get(('bt2', jt))])
                txd.append((t4, t5))
            tmy = []
            for it in range(njt):
                tk = None
                for jt in range(njt):
                    tk = ctx.op('t', lambda te, it=it, jt=jt: te.matmul(
                        p_y[it][:, :],
                        s_m[:, jt * ch + it * 128:jt * ch + (it + 1) * 128],
                        s_xdt[:, jt * HD0:(jt + 1) * HD0],
                        start=(jt == 0), stop=(jt == njt - 1)),
                        waits=[tg[jt], txd[jt][0],
                               prev.get(('py', it)) if jt == 0 else None])
                tmy.append(tk)
            tst = None
            for jt in range(njt):
                tst = ctx.op('t', lambda te, jt=jt, h=h: te.matmul(
                    p_st[0:64, :], sxt[:, jt * DSSM + h * HD0:jt * DSSM + (h + 1) * HD0],
                    s_bt2[:, jt * DST:(jt + 1) * DST],
                    start=(jt == 0), stop=(jt == njt - 1)),
                    waits=[txd[jt][0], txd[jt][1],
                           prev.get('pst') if jt == 0 else None])
            for jt in range(njt):
                prev[('m', jt)] = tmy[njt - 1]
                prev[('xdt', jt)] = ctx.tok('t')
                prev[('bt2', jt)] = tst
            yevs = []
            for it in range(njt):
                ev = ctx.op('v', lambda v, it=it, h=h: v.scalar_tensor_tensor(
                    s_y[:, it * DSSM + h * HD0:it * DSSM + (h + 1) * HD0],
                    sxt[:, it * DSSM + h * HD0:it * DSSM + (h + 1) * HD0],
                    s_db[:, h:h + 1], p_y[it][:, :], ALU.mult, ALU.add),
                    waits=[tmy[it], ctok, ub[3]])
                prev[('py', it)] = ev
                yevs.append(ev)
            hb = (h % 2) * 64
            hc = (h // 2) * DST
            ev2 = ctx.op('v', lambda v, hb=hb, hc=hc: v.tensor_copy(
                s_st[hb:hb + 64, hc:hc + DST], p_st[0:64, :]),
                waits=[tst, ub[4]])
            prev['pst'] = ev2
        vtok = ctx.tok('v')
        for it in range(njt):
            ctx.dma('y', 'oy', ydx_o[t0 + it * 128:t0 + (it + 1) * 128, :],
                    s_y[:, it * DSSM:(it + 1) * DSSM], waits=[vtok], serial=True)
        ctx.dma('y', 'ost', st_o[u * DSSM:(u + 1) * DSSM, :].rearrange(
            '(a p) d -> p a d', p=128),
            s_st[:, :].rearrange('p (a d) -> p a d', d=DST),
            waits=[vtok], serial=True)

    if own:
        ctx.emit()
    return nc



def build_state_scan(nchunks, rows, ndst=DST, nc=None, ctx=None, io=None):
    own = nc is None
    if own:
        nc = bass.Bass('TRN2', target_bir_lowering=False)
    if ctx is None:
        ctx = Ctx(nc)
    st_h = io['states'] if io is not None else nc.dram_tensor('states', [nchunks, rows, ndst], F32, kind='ExternalInput').ap()
    dc_h = io['decay_bc'] if io is not None else nc.dram_tensor(
        'decay_bc', [nchunks, 128, rows // 128], F32, kind='ExternalInput').ap()
    pv_h = io['prev'] if io is not None else nc.dram_tensor('prev', [nchunks, rows, ndst], F32, kind='ExternalOutput').ap()
    fin_h = io.get('final') if io is not None else None
    nrt = rows // 128
    s_pv = ctx.sb('s_pv', [128, nrt * ndst], F32)
    s_in = ctx.sb('s_in', [128, 2 * nrt * ndst], F32)
    s_dc = ctx.sb('s_dc', [128, 2 * nrt], F32)
    tz = ctx.op('v', lambda v: v.memset(s_pv[:, :], 0.0))
    upd = {i: tz for i in range(nrt)}
    for c in range(nchunks):
        par = c % 2
        ib = s_in[:, par * nrt * ndst:(par + 1) * nrt * ndst]
        db = s_dc[:, par * nrt:(par + 1) * nrt]
        lw = [upd[nrt - 1]] if c >= 2 else []
        lt = ctx.dma('g', 'ld%d' % par, ib.rearrange('p (a d) -> p a d', d=ndst),
                     st_h[c, :, :].rearrange('(a p) d -> p a d', p=128),
                     waits=lw, serial=True)
        lt2 = ctx.dma('g', 'ld%d' % par, db, dc_h[c, :, :], serial=True)
        stok = ctx.dma('y', 'st', pv_h[c, :, :].rearrange('(a p) d -> p a d', p=128),
                       s_pv[:, :].rearrange('p (a d) -> p a d', d=ndst),
                       waits=[upd[nrt - 1]], serial=True)
        for a in range(nrt):
            upd[a] = ctx.op('v', lambda v, a=a, ib=ib, db=db: v.scalar_tensor_tensor(
                s_pv[:, a * ndst:(a + 1) * ndst], s_pv[:, a * ndst:(a + 1) * ndst],
                db[:, a:a + 1], ib[:, a * ndst:(a + 1) * ndst], ALU.mult, ALU.add),
                waits=[lt2, stok, upd[a]])
    if fin_h is not None:
        ctx.dma('y', 'fin', fin_h[:, :].rearrange('(a p) d -> p a d', p=128),
                s_pv[:, :].rearrange('p (a d) -> p a d', d=ndst),
                waits=[upd[nrt - 1]], serial=True)
    if own:
        ctx.emit()
    return nc



def build_ssd_final(Tn=T, ch=CH0, nc=None, ctx=None, io=None):
    nch = Tn // ch
    njt = ch // 128
    own = nc is None
    if own:
        nc = bass.Bass('TRN2', target_bir_lowering=False)
    if ctx is None:
        ctx = Ctx(nc)
    ydx_h = io['ydx'] if io is not None else nc.dram_tensor('ydx', [Tn, DSSM], F32, kind='ExternalInput').ap()
    c_cm = io['C_cm'] if io is not None else nc.dram_tensor('C_cm', [DST, Tn], F32, kind='ExternalInput').ap()
    pvt_h = io['prevT'] if io is not None else nc.dram_tensor('prevT', [nch * DST, DSSM], F32, kind='ExternalInput').ap()
    ecs_h = io['expcs_tm'] if io is not None else nc.dram_tensor('expcs_tm', [Tn, NH0], F32, kind='ExternalInput').ap()
    z_h = (io.get('z_tm') if io is not None else None) if io is not None else None
    if io is None:
        z_h = nc.dram_tensor('z_tm', [Tn, DSSM], F32, kind='ExternalInput').ap()
    zcm_h = io.get('z_cm') if io is not None else None
    sint_h = io.get('s_inT') if io is not None else None
    sdr_h = io.get('sd_row') if io is not None else None
    on_h = io.get('ones1') if io is not None else None
    idf_h = io.get('identF') if io is not None else None
    yn_o = io['yn'] if io is not None else nc.dram_tensor('yn', [Tn, DSSM], BF16, kind='ExternalOutput').ap()

    s_y = ctx.sb('s_y', [128, njt * DSSM], F32)
    s_cu = ctx.sb('s_cu', [128, ch], BF16)
    s_pv = ctx.sb('s_pv', [128, DSSM], BF16)
    s_ec = ctx.sb('s_ec', [128, njt * NH0], F32)
    s_z = ctx.sb('s_z', [128, njt * DSSM], F32)
    s_sq = ctx.sb('s_sq', [128, DSSM], BF16)
    s_ms = ctx.sb('s_ms', [128, 8], F32)
    s_o = ctx.sb('s_o', [128, njt * DSSM], BF16)
    p_yo = [ctx.ps('p_yo%d' % i, [128, HD0], F32) for i in range(njt)]
    if zcm_h is not None:
        s_zc = ctx.sb('s_zc', [128, (DSSM // 128) * ch], F32)
        s_idf = ctx.sb('s_idf', [128, 128], F32)
        p_zt = ctx.ps('p_zt', [128, 128], F32)
    if sint_h is not None:
        s_sin = ctx.sb('s_sin', [DST, DSSM], F32)
        s_sdr = ctx.sb('s_sdr', [1, nch * NH0], F32)
        s_on1 = ctx.sb('s_on1', [1, 128], F32)
        s_sd = ctx.sb('s_sd', [128, NH0], F32)
        p_sd = ctx.ps('p_sd', [128, NH0], F32)

    eps_ap, eps_tok = ctx.const_col(1e-12, 'eps')
    corr0 = None
    if sint_h is not None:
        ctx.dma('y', 'corr', s_sin[:, :], sint_h[:, :])
        ctx.dma('y', 'corr', s_sdr[:, :], sdr_h[:, :])
        ctx.dma('y', 'corr', s_on1[:, :], on_h[:, :])
        corr0 = ctx.dtok('corr')
    if zcm_h is not None:
        ctx.dma('y', 'zid', s_idf[:, :], idf_h[:, :])
        corrz = ctx.dtok('zid')

    for u in range(nch):
        t0 = u * ch
        ub = [ctx.tok('v'), ctx.tok('t'), ctx.tok('s'),
              ctx.dtok('oy') if ctx.has_dma('oy') else None]
        for jt in range(njt):
            r0 = t0 + jt * 128
            ctx.dma('y', 'ldy', s_y[:, jt * DSSM:(jt + 1) * DSSM], ydx_h[r0:r0 + 128, :],
                    waits=ub, serial=True)
            ctx.dma('y', 'ldy', s_ec[:, jt * NH0:(jt + 1) * NH0], ecs_h[r0:r0 + 128, :],
                    waits=ub, serial=True)
            if z_h is not None:
                ctx.dma('y', 'ldy', s_z[:, jt * DSSM:(jt + 1) * DSSM],
                        z_h[r0:r0 + 128, :], waits=ub, serial=True)
        ctx.dma('g', 'ldg', s_cu[:, :], c_cm[:, t0:t0 + ch], waits=ub, serial=True)
        ctx.dma('g', 'ldg', s_pv[:, :], pvt_h[u * DST:(u + 1) * DST, :], waits=ub,
                serial=True)
        if zcm_h is not None:
            ctx.dma('g', 'ldg', s_zc[:, :].rearrange('p (a w) -> p a w', w=ch),
                    zcm_h[:, t0:t0 + ch].rearrange('(a p) w -> p a w', p=128),
                    waits=ub, serial=True)
        ld = ctx.dtok('ldy')
        ldg = ctx.dtok('ldg')
        if zcm_h is not None:
            pzt = None
            for a in range(DSSM // 128):
                for jt in range(njt):
                    tkz = ctx.op('t', lambda te, a=a, jt=jt: te.transpose(
                        p_zt[:, :], s_zc[:, a * ch + jt * 128:a * ch + (jt + 1) * 128],
                        s_idf[:, :]), waits=[ldg, corrz, pzt])
                    pzt = ctx.op('v', lambda v, a=a, jt=jt: v.tensor_copy(
                        s_z[:, jt * DSSM + a * 128:jt * DSSM + (a + 1) * 128],
                        p_zt[:, :]), waits=[tkz] + ub)
            ztok = pzt
        corr_w = None
        if sint_h is not None:
            msd = ctx.op('t', lambda te, u=u: te.matmul(
                p_sd[:, :], s_on1[0:1, :], s_sdr[0:1, u * NH0:(u + 1) * NH0],
                start=True, stop=True), waits=[corr0, ctx.tok('v')])
            esd = ctx.op('v', lambda v: v.tensor_copy(s_sd[:, :], p_sd[:, :]),
                         waits=[msd])
            ctok2 = esd
            for h in range(NH0):
                ctok2 = ctx.op('v', lambda v, h=h: v.scalar_tensor_tensor(
                    s_pv[:, h * HD0:(h + 1) * HD0], s_sin[:, h * HD0:(h + 1) * HD0],
                    s_sd[:, h:h + 1], s_pv[:, h * HD0:(h + 1) * HD0],
                    ALU.mult, ALU.add), waits=[ctok2, ldg, corr0])
        corr_w = ctx.tok('v') if sint_h is not None else None

        prev = {}
        for h in range(NH0):
            for it in range(njt):
                tk = ctx.op('t', lambda te, it=it, h=h: te.matmul(
                    p_yo[it][:, :], s_cu[:, it * 128:(it + 1) * 128],
                    s_pv[:, h * HD0:(h + 1) * HD0], start=True, stop=True),
                    waits=[ldg, corr_w, prev.get(('py', it))])
                ev = ctx.op('v', lambda v, it=it, h=h: v.scalar_tensor_tensor(
                    s_y[:, it * DSSM + h * HD0:it * DSSM + (h + 1) * HD0],
                    p_yo[it][:, :], s_ec[:, it * NH0 + h:it * NH0 + h + 1],
                    s_y[:, it * DSSM + h * HD0:it * DSSM + (h + 1) * HD0],
                    ALU.mult, ALU.add), waits=[tk, ld])
                prev[('py', it)] = ev
        for jt in range(njt):
            yt = s_y[:, jt * DSSM:(jt + 1) * DSSM]
            zt = s_z[:, jt * DSSM:(jt + 1) * DSSM]
            ot = s_o[:, jt * DSSM:(jt + 1) * DSSM]
            vt = ctx.tok('v')
            t1 = ctx.op('s', lambda s, zt=zt: s.activation(zt, zt, AF.Silu), waits=[vt, ld])
            t2 = ctx.op('v', lambda v, yt=yt, zt=zt: v.tensor_tensor(yt, yt, zt, ALU.mult),
                        waits=[t1, vt])
            t3 = ctx.op('s', lambda s, yt=yt: s.activation(s_sq[:, :], yt, AF.Square,
                                                           accum_out=s_ms[:, 0:1]),
                        waits=[t2])
            t4 = ctx.op('s', lambda s: s.activation(s_ms[:, 1:2], s_ms[:, 0:1], AF.Sqrt,
                                                    scale=1.0 / DSSM, bias=eps_ap[:, 0:1]),
                        waits=[t3, eps_tok])
            t5 = ctx.op('v', lambda v: v.reciprocal(s_ms[:, 2:3], s_ms[:, 1:2]), waits=[t4])
            t6 = ctx.op('v', lambda v, yt=yt, ot=ot: v.tensor_scalar(
                ot, yt, s_ms[:, 2:3], None, ALU.mult), waits=[t5, ub[3]])
            ctx.dma('y', 'oy', yn_o[t0 + jt * 128:t0 + (jt + 1) * 128, :], ot,
                    waits=[t6], serial=True)
    if own:
        ctx.emit()
    return nc


import math as _math

def _pwl_table():
    x = np.linspace(-4.0, 4.0, 33).astype(np.float32)
    erf = np.array([_math.erf(float(v) / _math.sqrt(2.0)) for v in x], np.float64)
    y = (np.float32(0.5) * x * (np.float32(1.0) + erf.astype(np.float32))).astype(np.float32)
    m = ((y[1:] - y[:-1]) / np.float32(0.25)).astype(np.float32)
    m = np.concatenate([m, np.float32([1.0])])
    c = np.empty(33, np.float32)
    c[0] = m[0]
    c[1:] = m[1:] - m[:-1]
    return x, y, c

_PWLX, _PWLY, _PWLC = _pwl_table()


def _pwl_table16():
    x = np.linspace(-4.0, 4.0, 33).astype(np.float32)
    erf = np.array([_math.erf(float(v) / _math.sqrt(2.0)) for v in x], np.float64)
    y = (np.float32(0.5) * x * (np.float32(1.0) + erf.astype(np.float32))).astype(np.float32)
    s = ((y[1:17] - y[:16]) / np.float32(0.25)).astype(np.float32)
    c = np.empty(16, np.float32)
    c[0] = s[0]
    c[1:] = s[1:] - s[:-1]
    kxu = (np.float32(256.0) * x[:16]).astype(np.float32)
    y0u = np.float32(256.0) * y[0]
    return kxu, c, y0u

_PWL16_KXU, _PWL16_C, _PWL16_Y0U = _pwl_table16()


def emit_round256(ctx, src_ap, dst_ap, tmp_ap, waits, tmp_guard=None):
    t1 = ctx.op('v', lambda v: v.tensor_scalar(tmp_ap, src_ap, 256.0, MAGIC,
                                               ALU.mult, ALU.add),
                waits=list(waits) + [tmp_guard])
    t2 = ctx.op('v', lambda v: v.tensor_scalar(dst_ap, tmp_ap, MAGIC, 1.0 / 256.0,
                                               ALU.subtract, ALU.mult), waits=[t1])
    return t2


GELU_MODE = 'pwl'


def emit_gelu_q(ctx, h_ap, out_ap, scr, waits):
    t1 = ctx.op('v', lambda v: v.tensor_scalar(scr['t'], h_ap, 256.0, MAGIC,
                                               ALU.mult, ALU.add),
                waits=list(waits) + [scr.get('guard')])
    if GELU_MODE == 'hw':
        t2 = ctx.op('s', lambda s: s.activation(scr['u'], scr['t'], AF.Gelu,
                                                scale=1.0 / 256.0,
                                                bias=-MAGIC / 256.0), waits=[t1])
        acc_tok = t2
        acc_ap = scr['u']
    else:
        if not hasattr(ctx, '_g16cc'):
            bias_aps = []
            for i in range(16):
                sc = abs(float(_PWL16_C[i])) if i % 2 else 1.0
                bap, cctok = ctx.const_col(float(-sc * _PWL16_KXU[i]), 'g16b%d' % i)
                bias_aps.append(bap)
            ctx._g16cc = (bias_aps, cctok)
        bias_aps, cctok = ctx._g16cc
        t2 = ctx.op('v', lambda v: v.tensor_scalar(scr['u'], scr['t'], MAGIC, None,
                                                   ALU.subtract), waits=[t1])
        ta = ctx.op('s', lambda s: s.activation(scr['t'], scr['u'], AF.Abs),
                    waits=[t2])
        P_ = scr['acc'].shape[0]
        z0 = ctx.op('v', lambda v: v.tensor_scalar(
            scr['acc'], scr['u'], 0.0, float(_PWL16_Y0U), ALU.max, ALU.add),
            waits=[t2])
        rbuf = [scr['r'], scr['r2']]
        rfree = [None, None]
        acc_tok = z0
        acc2_tok = ctx.op('v', lambda v: v.memset(scr['acc2'], 0.0),
                          waits=[scr.get('guard')])
        for i in range(16):
            b = i % 2
            ci = float(_PWL16_C[i])
            sc = -abs(ci) if b else -1.0
            tr = ctx.op('s', lambda s, b=b, i=i, sc=sc: s.activation(
                rbuf[b], scr['t'], AF.Relu, bias=bias_aps[i][0:P_, :], scale=sc),
                waits=[ta, cctok, rfree[b]])
            if b == 0:
                acc_tok = ctx.op('v', lambda v, b=b, ci=ci: v.scalar_tensor_tensor(
                    scr['acc'], rbuf[b], ci, scr['acc'], ALU.mult, ALU.add),
                    waits=[tr, acc_tok])
                rfree[b] = acc_tok
            else:
                aop = ALU.add if ci > 0 else ALU.subtract
                acc2_tok = ctx.op('g', lambda g, b=b, aop=aop: g.tensor_tensor(
                    scr['acc2'], scr['acc2'], rbuf[b], aop),
                    waits=[tr, acc2_tok])
                rfree[b] = acc2_tok
        acc_tok = ctx.op('v', lambda v: v.tensor_tensor(
            scr['acc'], scr['acc'], scr['acc2'], ALU.add), waits=[acc_tok, acc2_tok])
        t3 = ctx.op('v', lambda v: v.tensor_scalar(scr['t'], scr['acc'], MAGIC, None,
                                                   ALU.add), waits=[acc_tok])
        t4 = ctx.op('v', lambda v: v.tensor_scalar(out_ap, scr['t'], MAGIC, 1.0 / 256.0,
                                                   ALU.subtract, ALU.mult), waits=[t3])
        return t4
    t3 = ctx.op('v', lambda v: v.tensor_scalar(scr['t'], acc_ap, 256.0, MAGIC,
                                               ALU.mult, ALU.add), waits=[acc_tok])
    t4 = ctx.op('v', lambda v: v.tensor_scalar(out_ap, scr['t'], MAGIC, 1.0 / 256.0,
                                               ALU.subtract, ALU.mult), waits=[t3])
    return t4


def build_ffn1(Tn=T, nc=None, ctx=None, io=None):
    FF = 4096
    def post(nc, ctx, i, o0, odp, ot, par, tok):
        if not hasattr(ctx, '_gelu_scr'):
            scr_t = ctx.sb('g_t', [128, Tn], F32)
            scr_u = ctx.sb('g_u', [128, Tn], F32)
            scr_a = ctx.sb('g_a', [128, Tn], F32)
            scr_r = ctx.sb('g_r', [128, Tn], F32)
            scr_r2 = ctx.sb('g_r2', [128, Tn], F32)
            scr_a2 = ctx.sb('g_a2', [128, Tn], F32)
            gout = ctx.sb('g_o', [128, 2 * Tn], BF16)
            ctx._gelu_scr = (scr_t, scr_u, scr_a, scr_r, scr_r2, scr_a2, gout)
            ctx._gelu_guard = None
        scr_t, scr_u, scr_a, scr_r, scr_r2, scr_a2, gout = ctx._gelu_scr
        ob = gout[:, par * Tn:(par + 1) * Tn]
        stw = ctx.dtok('st%d' % par) if ctx.has_dma('st%d' % par) else None
        scr = {'t': scr_t[:, :], 'u': scr_u[:, :], 'acc': scr_a[:, :], 'r': scr_r[:, :],
               'r2': scr_r2[:, :], 'acc2': scr_a2[:, :], 'guard': ctx._gelu_guard}
        tk = emit_gelu_q(ctx, ot, ob, scr, waits=[tok, stw])
        ctx._gelu_guard = tk
        return ob, tk
    return build_linear(DM, FF, Tn, bias=True, out_dtype=BF16, stage_dtype=F32,
                        post_od=post, nc=nc, ctx=ctx, io=io)


def build_f2b(Tn=T, nc=None, ctx=None, io=None):
    def post(nc, ctx, i, o0, odp, ot, par, tok):
        if not hasattr(ctx, '_scr'):
            ctx._scr = ctx.sb('r_t', [128, Tn], F32)
            ctx._g = None
        r2t = ctx._sres2[0:odp, par * Tn:par * Tn + Tn]
        lr = ctx.dma('y', 'r2%d' % par, r2t, ctx._r2ap[o0:o0 + odp, :], serial=True,
                     waits=[getattr(ctx, '_r2use%d' % par, None)])
        tk = emit_round256(ctx, ot, ot, ctx._scr[:, :], waits=[tok], tmp_guard=ctx._g)
        ctx._g = tk
        tk2 = ctx.op('v', lambda v, ot=ot, r2t=r2t: v.tensor_tensor(
            ot, ot, r2t, ALU.add), waits=[tk, lr])
        setattr(ctx, '_r2use%d' % par, tk2)
        return ot, tk2
    return build_linear(2048, DM, Tn, bias=True, residual=True, residual2=True,
                        x_hbm_dtype=BF16, post_od=post, nc=nc, ctx=ctx, io=io)



def build_conv_silu(C_, Tn=T, nc=None, ctx=None, io=None, halo_mask=False):
    own = nc is None
    if own:
        nc = bass.Bass('TRN2', target_bir_lowering=False)
    if ctx is None:
        ctx = Ctx(nc)
    xh = io['x_pad'] if io is not None else nc.dram_tensor('x_pad', [C_, Tn + 3], F32, kind='ExternalInput').ap()
    wh = io['w'] if io is not None else nc.dram_tensor('w', [C_, 4], F32, kind='ExternalInput').ap()
    oh = io['out'] if io is not None else nc.dram_tensor('out', [C_, Tn], F32, kind='ExternalOutput').ap()
    nct = C_ // 128
    sx = ctx.sb('sx', [128, 2 * (Tn + 3)], F32)
    sw = ctx.sb('sw', [128, nct * 4], F32)
    sa = ctx.sb('sa', [128, 2 * Tn], F32)
    so = ctx.sb('so', [128, 2 * Tn], F32)
    shm = ctx.sb('shm', [128, 1], F32) if halo_mask else None
    for i in range(nct):
        ctx.dma('y', 'w', sw[:, 4 * i:4 * i + 4], wh[i * 128:(i + 1) * 128, :])
    if halo_mask:
        hm_ap = io['hmask'] if io is not None else nc.dram_tensor(
            'hmask', [128, 1], F32, kind='ExternalInput').ap()
        ctx.dma('y', 'w', shm[:, :], hm_ap[:, :])
    wl = ctx.dtok('w')
    prev = {}
    for i in range(nct):
        par = i % 2
        xt = sx[:, par * (Tn + 3):par * (Tn + 3) + Tn + 3]
        at = sa[:, par * Tn:(par + 1) * Tn]
        ot = so[:, par * Tn:(par + 1) * Tn]
        stw = ctx.dtok('o%d' % par) if ctx.has_dma('o%d' % par) else None
        lt = ctx.dma('g', 'x%d' % par, xt, xh[i * 128:(i + 1) * 128, :], serial=True,
                     waits=[prev.get(('x', par))])
        if halo_mask:
            lt = ctx.op('v', lambda v, xt=xt: v.tensor_scalar(
                xt[:, 0:3], xt[:, 0:3], shm[:, 0:1], None, ALU.mult), waits=[lt, wl])
        t = ctx.op('v', lambda v, xt=xt, at=at, i=i: v.tensor_scalar(
            at, xt[:, 0:Tn], sw[:, 4 * i:4 * i + 1], None, ALU.mult),
            waits=[lt, wl, prev.get(('a', par))])
        for k in range(1, 4):
            t = ctx.op('v', lambda v, xt=xt, at=at, i=i, k=k: v.scalar_tensor_tensor(
                at, xt[:, k:k + Tn], sw[:, 4 * i + k:4 * i + k + 1], at,
                ALU.mult, ALU.add), waits=[t])
        prev[('x', par)] = t
        t2 = ctx.op('s', lambda s, at=at, ot=ot: s.activation(ot, at, AF.Silu),
                    waits=[t, stw])
        prev[('a', par)] = t2
        ctx.dma('y', 'o%d' % par, oh[i * 128:(i + 1) * 128, :], ot, waits=[t2],
                serial=True)
    if own:
        ctx.emit()
    return nc



def build_l2norm_qk(Tn=T, nc=None, ctx=None, io=None):
    own = nc is None
    if own:
        nc = bass.Bass('TRN2', target_bir_lowering=False)
    if ctx is None:
        ctx = Ctx(nc)
    qh = io['q_tm'] if io is not None else nc.dram_tensor('q_tm', [Tn, KEY_DIM], F32, kind='ExternalInput').ap()
    kh = io['k_tm'] if io is not None else nc.dram_tensor('k_tm', [Tn, KEY_DIM], F32, kind='ExternalInput').ap()
    qo = io['qn'] if io is not None else nc.dram_tensor('qn', [Tn, KEY_DIM], BF16, kind='ExternalOutput').ap()
    ko = io['kn'] if io is not None else nc.dram_tensor('kn', [Tn, KEY_DIM], BF16, kind='ExternalOutput').ap()
    nr = Tn // 128
    sx = ctx.sb('sx', [128, 2 * KEY_DIM], F32)
    sq = ctx.sb('sq', [128, KEY_DIM], BF16)
    sm = ctx.sb('sm', [128, 3 * H1], F32)
    so = ctx.sb('so', [128, 2 * KEY_DIM], BF16)
    eps_ap, ept = ctx.const_col(1e-6, 'epsn')
    prev = {}
    for idx in range(2 * nr):
        tensor_i = idx % 2
        r = idx // 2
        src_h = qh if tensor_i == 0 else kh
        dst_h = qo if tensor_i == 0 else ko
        par = idx % 2
        xt = sx[:, par * KEY_DIM:(par + 1) * KEY_DIM]
        ot = so[:, par * KEY_DIM:(par + 1) * KEY_DIM]
        stw = ctx.dtok('o%d' % par) if ctx.has_dma('o%d' % par) else None
        lt = ctx.dma('y', 'x%d' % par, xt, src_h[r * 128:(r + 1) * 128, :], serial=True,
                     waits=[prev.get(('u', par))])
        ts_ = []
        for h in range(H1):
            hs = slice(h * KD1, (h + 1) * KD1)
            t1 = ctx.op('s', lambda s, xt=xt, h=h, hs=hs: s.activation(
                sq[:, hs], xt[:, hs], AF.Square, accum_out=sm[:, h:h + 1]),
                waits=[lt, prev.get('sq')])
            t2 = ctx.op('s', lambda s, h=h: s.activation(
                sm[:, H1 + h:H1 + h + 1], sm[:, h:h + 1], AF.Sqrt), waits=[t1])
            t3 = ctx.op('v', lambda v, h=h: v.tensor_scalar(
                sm[:, H1 + h:H1 + h + 1], sm[:, H1 + h:H1 + h + 1], eps_ap[:, 0:1], None,
                ALU.max), waits=[t2, ept])
            t4 = ctx.op('v', lambda v, h=h: v.reciprocal(
                sm[:, 2 * H1 + h:2 * H1 + h + 1], sm[:, H1 + h:H1 + h + 1]), waits=[t3])
            t5 = ctx.op('v', lambda v, xt=xt, ot=ot, h=h, hs=hs: v.tensor_scalar(
                ot[:, hs], xt[:, hs], sm[:, 2 * H1 + h:2 * H1 + h + 1], None, ALU.mult),
                waits=[t4, stw])
            ts_.append(t5)
        prev[('u', par)] = ts_[-1]
        prev['sq'] = ts_[-1]
        ctx.dma('y', 'o%d' % par, dst_h[r * 128:(r + 1) * 128, :], ot,
                waits=[ts_[-1]], serial=True)
    if own:
        ctx.emit()
    return nc



def build_ab_rows(Tn=T, ch=CH1, nc=None, ctx=None, io=None):
    nch = Tn // ch
    own = nc is None
    if own:
        nc = bass.Bass('TRN2', target_bir_lowering=False)
    if ctx is None:
        ctx = Ctx(nc)
    ar = io['a_raw'] if io is not None else nc.dram_tensor('a_raw', [H1, Tn], F32, kind='ExternalInput').ap()
    br = io['b_raw'] if io is not None else nc.dram_tensor('b_raw', [H1, Tn], F32, kind='ExternalInput').ap()
    dtb = io['dt_bias'] if io is not None else nc.dram_tensor('dt_bias', [H1, 1], F32, kind='ExternalInput').ap()
    alog = io['A_log'] if io is not None else nc.dram_tensor('A_log', [H1, 1], F32, kind='ExternalInput').ap()
    lg_o = io['lg'] if io is not None else nc.dram_tensor('lg', [H1, Tn], F32, kind='ExternalOutput').ap()
    be_o = io['beta'] if io is not None else nc.dram_tensor('beta', [H1, Tn], F32, kind='ExternalOutput').ap()
    s = ctx.sb('s', [H1, 6 * Tn], F32)
    sc = ctx.sb('sc', [H1, 4], F32)
    U, AB, G, LG, BE, ZERO = (s[:, i * Tn:(i + 1) * Tn] for i in range(6))
    one_ap, one_tok = ctx.const_col(1.0, 'one')
    ctx.dma('y', 'ldy', U, ar[:, :])
    ctx.dma('y', 'ldy', BE, br[:, :])
    ctx.dma('y', 'ldy', sc[:, 0:1], dtb[:, :])
    ctx.dma('y', 'ldy', sc[:, 1:2], alog[:, :])
    ld = ctx.dtok('ldy')
    z0 = ctx.op('v', lambda v: v.memset(ZERO, 0.0))
    t = ctx.op('v', lambda v: v.tensor_scalar(U, U, sc[:, 0:1], None, ALU.add), waits=[ld])
    t = ctx.op('s', lambda s_: s_.activation(AB, U, AF.Abs), waits=[t])
    t = ctx.op('s', lambda s_: s_.activation(AB, AB, AF.Exp, scale=-1.0), waits=[t])
    t = ctx.op('s', lambda s_: s_.activation(AB, AB, AF.Ln, bias=one_ap[0:H1, 0:1]),
               waits=[t, one_tok])
    t2 = ctx.op('v', lambda v: v.tensor_scalar(G, U, 0.0, None, ALU.max), waits=[t])
    tsp = ctx.op('v', lambda v: v.tensor_tensor(G, AB, G, ALU.add), waits=[t, t2])
    te = ctx.op('s', lambda s_: s_.activation(sc[:, 2:3], sc[:, 1:2], AF.Exp), waits=[ld])
    tn = ctx.op('v', lambda v: v.tensor_scalar(sc[:, 2:3], sc[:, 2:3], -1.0, None,
                                               ALU.mult), waits=[te])
    tg = ctx.op('v', lambda v: v.tensor_scalar(G, G, sc[:, 2:3], None, ALU.mult),
                waits=[tsp, tn])
    tcs = None
    for c in range(nch):
        sl = slice(c * ch, (c + 1) * ch)
        tcs = ctx.op('v', lambda v, sl=sl: v.tensor_tensor_scan(
            LG[:, sl], G[:, sl], ZERO[:, sl], 0.0, ALU.add, ALU.add), waits=[tg, z0])
    tb = ctx.op('s', lambda s_: s_.activation(BE, BE, AF.Sigmoid), waits=[ld])
    tb = ctx.op('v', lambda v: v.tensor_scalar(BE, BE, 2.0, None, ALU.mult), waits=[tb])
    ctx.dma('y', 'o', lg_o[:, :], LG, waits=[tcs])
    ctx.dma('y', 'o', be_o[:, :], BE, waits=[tb])
    if own:
        ctx.emit()
    return nc



def build_delta_local(Tn=T, ch=CH1, nc=None, ctx=None, io=None):
    nch = Tn // ch
    own = nc is None
    if own:
        nc = bass.Bass('TRN2', target_bir_lowering=False)
    if ctx is None:
        ctx = Ctx(nc)
    qcm_h = io['q_cm'] if io is not None else nc.dram_tensor('q_cm', [KEY_DIM, Tn], BF16, kind='ExternalInput').ap()
    kcm_h = io['k_cm'] if io is not None else nc.dram_tensor('k_cm', [KEY_DIM, Tn], BF16, kind='ExternalInput').ap()
    qtm_h = io['q_tm'] if io is not None else nc.dram_tensor('q_tm', [Tn, KEY_DIM], BF16, kind='ExternalInput').ap()
    ktm_h = io['k_tm'] if io is not None else nc.dram_tensor('k_tm', [Tn, KEY_DIM], BF16, kind='ExternalInput').ap()
    vtm_h = io['v_tm'] if io is not None else nc.dram_tensor('v_tm', [Tn, VAL_DIM], F32, kind='ExternalInput').ap()
    lgr_h = io['lg_row'] if io is not None else nc.dram_tensor('lg_row', [H1, Tn], F32, kind='ExternalInput').ap()
    lgt_h = io['lg_tm'] if io is not None else nc.dram_tensor('lg_tm', [Tn, H1], F32, kind='ExternalInput').ap()
    bet_h = io['beta_tm'] if io is not None else nc.dram_tensor('beta_tm', [Tn, H1], F32, kind='ExternalInput').ap()
    idf_h = io['identF'] if io is not None else nc.dram_tensor('identF', [128, 128], F32, kind='ExternalInput').ap()
    id2_h = io['ident2'] if io is not None else nc.dram_tensor('ident2', [ch, ch], F32, kind='ExternalInput').ap()
    mi_h = io['mask_incl'] if io is not None else nc.dram_tensor('mask_incl', [ch, ch], F32, kind='ExternalInput').ap()
    ms_h = io['mask_strict'] if io is not None else nc.dram_tensor('mask_strict', [ch, ch], F32, kind='ExternalInput').ap()
    on_h = io['ones1'] if io is not None else nc.dram_tensor('ones1', [1, 128], F32, kind='ExternalInput').ap()
    olo_o = io['O_loc'] if io is not None else nc.dram_tensor('O_loc', [Tn, VAL_DIM], F32, kind='ExternalOutput').ap()
    chain = io is not None and 'PLT' in io
    mt_o = (io.get('Mt') if io is not None else None) if io is not None else None
    sl_o = (io.get('SlocT') if io is not None else None) if io is not None else None
    if io is None:
        mt_o = nc.dram_tensor('Mt', [nch * H1 * KD1, KD1], F32, kind='ExternalOutput').ap()
        sl_o = nc.dram_tensor('SlocT', [nch * H1 * KD1, VD1], F32, kind='ExternalOutput').ap()
    gt_o = io['GT'] if io is not None else nc.dram_tensor('GT', [nch * H1 * KD1, ch], F32, kind='ExternalOutput').ap()
    gc_o = (io.get('gamc') if io is not None else None) if io is not None else None
    if io is None:
        gc_o = nc.dram_tensor('gamc', [nch, H1], F32, kind='ExternalOutput').ap()
    plt_o = io.get('PLT') if io is not None else None
    ags_o = io.get('ag_S') if io is not None else None
    agp_o = io.get('ag_PL') if io is not None else None

    sq_cm = ctx.sb('sq_cm', [128, H1 * ch], BF16)
    sk_cm = ctx.sb('sk_cm', [128, H1 * ch], BF16)
    sq_tm = ctx.sb('sq_tm', [ch, KEY_DIM], F32)
    sk_tm = ctx.sb('sk_tm', [ch, KEY_DIM], F32)
    sv_tm = ctx.sb('sv_tm', [ch, VAL_DIM], F32)
    s_lgr = ctx.sb('s_lgr', [1, H1 * ch], F32)
    s_lgc = ctx.sb('s_lgc', [ch, H1], F32)
    s_bet = ctx.sb('s_bet', [ch, H1], F32)
    s_id = ctx.sb('s_id', [128, 128], F32)
    s_id2 = ctx.sb('s_id2', [ch, ch], F32)
    s_mi = ctx.sb('s_mi', [ch, ch], F32)
    s_ms = ctx.sb('s_ms', [ch, ch], F32)
    s_on = ctx.sb('s_on', [1, 128], F32)
    s_lgb = ctx.sb('s_lgb', [ch, ch], F32)
    s_E = ctx.sb('s_E', [ch, ch], F32)
    s_C = ctx.sb('s_C', [ch, ch], F32)
    s_A = ctx.sb('s_A', [ch, ch], F32)
    s_Dn = ctx.sb('s_Dn', [ch, ch], F32)
    s_DT = ctx.sb('s_DT', [ch, ch], F32)
    s_X = ctx.sb('s_X', [ch, ch], F32)
    s_XT = ctx.sb('s_XT', [ch, ch], F32)
    s_Z = ctx.sb('s_Z', [ch, ch], F32)
    s_Rv = ctx.sb('s_Rv', [ch, VD1], F32)
    s_Kp = ctx.sb('s_Kp', [ch, KD1], F32)
    s_Wl = ctx.sb('s_Wl', [ch, VD1], F32)
    s_Pl = ctx.sb('s_Pl', [ch, KD1], F32)
    s_Qt = ctx.sb('s_Qt', [ch, KD1], F32)
    s_G = ctx.sb('s_G', [ch, KD1], F32)
    s_GT = ctx.sb('s_GT', [KD1, ch], F32)
    s_LK = ctx.sb('s_LK', [ch, KD1], F32)
    s_Sl = ctx.sb('s_Sl', [KD1, VD1], F32)
    s_Mt = ctx.sb('s_Mt', [KD1, KD1], F32)
    s_col = ctx.sb('s_col', [ch, 4], F32)
    s_gr = ctx.sb('s_gr', [1, ch], F32)
    s_gc = ctx.sb('s_gc', [1, H1], F32)
    s_O = ctx.sb('s_O', [ch, VAL_DIM], F32)

    p1 = ctx.ps('p1', [ch, ch], F32)
    p2 = ctx.ps('p2', [ch, ch], F32)
    pP = ctx.ps('pP', [ch, KD1], F32)
    pT = ctx.ps('pT', [KD1, ch], F32)
    pW = ctx.ps('pW', [ch, VD1], F32)
    pS = ctx.ps('pS', [KD1, VD1], F32)
    pM = ctx.ps('pM', [KD1, KD1], F32)
    if chain:
        s_S0 = ctx.sb('s_S0', [128, H1 * VD1], F32)
        s_PLn = ctx.sb('s_PLn', [128, H1 * KD1], F32)
        s_PLt = ctx.sb('s_PLt', [128, H1 * KD1], F32)
        s_gI = ctx.sb('s_gI', [128, 128], F32)
        s_Lt = ctx.sb('s_Lt', [128, 128], F32)
        s_gcol = ctx.sb('s_gcol', [128, 1], F32)
        s_tmp3 = ctx.sb('s_tmp3', [128, VD1], F32)

    for src_h, dst in [(idf_h, s_id), (id2_h, s_id2), (mi_h, s_mi), (ms_h, s_ms),
                       (on_h, s_on)]:
        ctx.dma('y', 'const', dst[:, :], src_h[:, :])
    ctok = ctx.dtok('const')
    if chain:
        ztok = ctx.op('v', lambda v: v.memset(s_S0[:, :], 0.0))
        for h in range(H1):
            ztok = ctx.op('v', lambda v, h=h: v.tensor_copy(
                s_PLn[:, h * KD1:(h + 1) * KD1], s_id[:, :]), waits=[ctok, ztok])
            ztok = ctx.op('v', lambda v, h=h: v.tensor_copy(
                s_PLt[:, h * KD1:(h + 1) * KD1], s_id[:, :]), waits=[ztok])

    for u in range(nch):
        t0 = u * ch
        ub = [ctx.tok('v'), ctx.tok('t'), ctx.tok('s')] +              [ctx.dtok(n) for n in ('oO', 'oM', 'oS', 'oG', 'ogc') if ctx.has_dma(n)]
        ctx.dma('g', 'ld', sq_cm[:, :].rearrange('p (a w) -> p a w', w=ch),
                qcm_h[:, t0:t0 + ch].rearrange('(a p) w -> p a w', p=128),
                waits=ub, serial=True)
        ctx.dma('g', 'ld', sk_cm[:, :].rearrange('p (a w) -> p a w', w=ch),
                kcm_h[:, t0:t0 + ch].rearrange('(a p) w -> p a w', p=128),
                waits=ub, serial=True)
        ctx.dma('g', 'ld', sq_tm[:, :], qtm_h[t0:t0 + ch, :], waits=ub, serial=True)
        ctx.dma('g', 'ld', sk_tm[:, :], ktm_h[t0:t0 + ch, :], waits=ub, serial=True)
        ctx.dma('y', 'ldh', sv_tm[:, :], vtm_h[t0:t0 + ch, :], waits=ub, serial=True)
        ctx.dma('y', 'ldh', s_lgr[:, :].rearrange('o (a w) -> o a w', w=ch),
                lgr_h[:, t0:t0 + ch].unsqueeze(0), waits=ub, serial=True)
        ctx.dma('y', 'ldh', s_lgc[:, :], lgt_h[t0:t0 + ch, :], waits=ub, serial=True)
        ctx.dma('y', 'ldh', s_bet[:, :], bet_h[t0:t0 + ch, :], waits=ub, serial=True)
        ld = ctx.dtok('ld')
        ldh = ctx.dtok('ldh')

        def emit_head(h, u=u, ld=ld, ldh=ldh, ub=ub):
            hs = slice(h * ch, (h + 1) * ch)
            kcm = sk_cm[:, hs]
            qcm = sq_cm[:, hs]
            ktm = sk_tm[:, h * KD1:(h + 1) * KD1]
            qtm = sq_tm[:, h * KD1:(h + 1) * KD1]
            vtm = sv_tm[:, h * VD1:(h + 1) * VD1]
            lgc = s_lgc[:, h:h + 1]
            bet = s_bet[:, h:h + 1]

            def op(e, f, w=()):
                return ctx.op(e, f, waits=[x for x in w if x is not None])

            m1 = op('t', lambda te: te.matmul(p1[:, :], s_on[0:1, 0:ch],
                                              s_lgr[0:1, hs], start=True, stop=True),
                    [ldh, ctok, ctx._pv.get('p1') if hasattr(ctx, '_pv') else None])
            e1 = op('v', lambda v: v.tensor_copy(s_lgb[:, :], p1[:, :]),
                    [m1, ctx._pv.get('lgb') if hasattr(ctx, '_pv') else None,
                     ctx._pv.get('lgb_s') if hasattr(ctx, '_pv') else None])
            g1 = op('s', lambda s_: s_.activation(s_gr[:, :], s_lgb[0:1, :], AF.Exp), [e1])
            g2 = op('v', lambda v, h=h: v.tensor_copy(s_gc[:, h:h + 1], s_gr[:, ch - 1:ch]),
                    [g1])
            c1 = op('s', lambda s_: s_.activation(s_col[:, 0:1], lgc, AF.Exp), [ldh])
            c2 = op('s', lambda s_: s_.activation(s_col[:, 1:2], lgc, AF.Exp, scale=-1.0,
                                                  bias=s_lgb[:, ch - 1:ch]), [ldh, e1])
            c3 = op('v', lambda v: v.tensor_tensor(s_col[:, 2:3], s_col[:, 0:1], bet,
                                                   ALU.mult), [c1, ldh])
            e2 = op('v', lambda v: v.tensor_scalar(s_E[:, :], s_lgb[:, :], lgc, 0.0,
                                                   ALU.subtract, ALU.min), [e1, ldh])
            e3 = op('s', lambda s_: s_.activation(s_E[:, :], s_E[:, :], AF.Exp), [e2])
            m2 = op('t', lambda te: te.matmul(p2[:, :], kcm, kcm, start=True, stop=True),
                    [ld, ctx._pv.get('p2') if hasattr(ctx, '_pv') else None])
            e4 = op('v', lambda v: v.scalar_tensor_tensor(s_C[:, :], p2[:, :], 0.0,
                                                          s_ms[:, :], ALU.add, ALU.mult),
                    [m2, ctok])
            e5 = op('v', lambda v: v.tensor_tensor(s_C[:, :], s_C[:, :], s_E[:, :],
                                                   ALU.mult), [e4, e3])
            m3 = op('t', lambda te: te.matmul(p2[:, :], kcm, qcm, start=True, stop=True),
                    [ld, e4])
            e6 = op('v', lambda v: v.scalar_tensor_tensor(s_A[:, :], p2[:, :], 0.0,
                                                          s_mi[:, :], ALU.add, ALU.mult),
                    [m3, ctok])
            e7 = op('v', lambda v: v.tensor_tensor(s_A[:, :], s_A[:, :], s_E[:, :],
                                                   ALU.mult), [e6, e3])
            m4 = op('t', lambda te: te.transpose(p1[:, :], s_C[:, :], s_id[0:ch, 0:ch]),
                    [e5, e1, ctok])
            e8 = op('v', lambda v: v.scalar_tensor_tensor(s_Dn[:, :], p1[:, :], bet,
                                                          s_id[0:ch, 0:ch], ALU.mult,
                                                          ALU.add), [m4, ctok])
            m5 = op('t', lambda te: te.transpose(p1[:, :], s_Dn[:, :], s_id[0:ch, 0:ch]),
                    [e8])
            e9 = op('v', lambda v: v.tensor_copy(s_DT[:, :], p1[:, :]), [m5])
            x0 = op('v', lambda v: v.scalar_tensor_tensor(s_X[:, :], s_Dn[:, :], -1.0,
                                                          s_id2[:, :], ALU.mult, ALU.add),
                    [e8])
            xt_tok = None
            for itn in range(5):
                mA = op('t', lambda te: te.matmul(p2[:, :], s_DT[:, :], s_X[:, :],
                                                  start=True, stop=True),
                        [e9, x0 if itn == 0 else xt_tok, e6])
                zA = op('v', lambda v: v.scalar_tensor_tensor(s_Z[:, :], p2[:, :], -1.0,
                                                              s_id2[:, :], ALU.mult,
                                                              ALU.add), [mA])
                mB = op('t', lambda te: te.transpose(p1[:, :], s_X[:, :],
                                                     s_id[0:ch, 0:ch]), [zA])
                zB = op('s', lambda s_: s_.activation(s_XT[:, :], p1[:, :], AF.Copy),
                        [mB])
                mC = op('t', lambda te: te.matmul(p2[:, :], s_XT[:, :], s_Z[:, :],
                                                  start=True, stop=True), [zB, zA])
                xt_tok = op('s', lambda s_: s_.activation(s_X[:, :], p2[:, :], AF.Copy),
                           [mC])
            mD = op('t', lambda te: te.transpose(p1[:, :], s_X[:, :], s_id[0:ch, 0:ch]),
                    [xt_tok])
            xT = op('v', lambda v: v.tensor_copy(s_XT[:, :], p1[:, :]), [mD])
            r1 = op('v', lambda v: v.tensor_scalar(s_Rv[:, :], vtm, bet, None, ALU.mult),
                    [ldh])
            r2 = op('v', lambda v: v.tensor_scalar(s_Kp[:, :], ktm, s_col[:, 2:3], None,
                                                   ALU.mult), [ld, c3])
            mW = op('t', lambda te: te.matmul(pW[:, :], s_XT[:, :], s_Rv[:, :],
                                              start=True, stop=True), [xT, r1])
            eW = op('v', lambda v: v.tensor_copy(s_Wl[:, :], pW[:, :]), [mW])
            mP = op('t', lambda te: te.matmul(pP[:, :], s_XT[:, :], s_Kp[:, :],
                                              start=True, stop=True), [xT, r2])
            eP = op('v', lambda v: v.tensor_copy(s_Pl[:, :], pP[:, :]), [mP])
            q1 = op('v', lambda v: v.tensor_scalar(s_Qt[:, :], qtm, s_col[:, 0:1], None,
                                                   ALU.mult), [ld, c1])
            mG = op('t', lambda te: te.matmul(pP[:, :], s_A[:, :], s_Pl[:, :],
                                              start=True, stop=True), [e7, eP])
            eG = op('v', lambda v: v.scalar_tensor_tensor(s_G[:, :], pP[:, :], -1.0,
                                                          s_Qt[:, :], ALU.mult, ALU.add),
                    [mG, q1])
            mGT = op('t', lambda te: te.transpose(pT[:, :], s_G[:, :], s_id[0:ch, 0:ch]),
                     [eG])
            eGT = op('v', lambda v: v.tensor_copy(s_GT[:, :], pT[:, :]),
                     [mGT, ctx.dtok('oG') if ctx.has_dma('oG') else None])
            mO = op('t', lambda te: te.matmul(pW[:, :], s_A[:, :], s_Wl[:, :],
                                              start=True, stop=not chain), [e7, eW])
            if chain:
                mO = op('t', lambda te, h=h: te.matmul(
                    pW[:, :], s_GT[:, :], s_S0[:, h * VD1:(h + 1) * VD1],
                    start=False, stop=True), [mO, eGT])
            eO = op('v', lambda v, h=h: v.tensor_copy(s_O[:, h * VD1:(h + 1) * VD1],
                                                      pW[:, :]), [mO] + ub)
            l1 = op('v', lambda v: v.tensor_scalar(s_LK[:, :], ktm, s_col[:, 1:2], None,
                                                   ALU.mult), [ld, c2])
            mS = op('t', lambda te: te.matmul(pS[:, :], s_LK[:, :], s_Wl[:, :],
                                              start=True, stop=True), [l1, eW, mO])
            eS = op('v', lambda v: v.tensor_copy(s_Sl[:, :], pS[:, :]),
                    [mS, ctx.dtok('oS') if ctx.has_dma('oS') else None])
            mM = op('t', lambda te: te.matmul(pM[:, :], s_Pl[:, :], s_LK[:, :],
                                              start=True, stop=True), [eP, l1])
            eM = op('v', lambda v: v.tensor_copy(s_Mt[:, :], pM[:, :]),
                    [mM, ctx.dtok('oM') if ctx.has_dma('oM') else None])
            r0 = (u * H1 + h) * KD1
            if sl_o is not None:
                ctx.dma('y', 'oS', sl_o[r0:r0 + KD1, :], s_Sl[:, :], waits=[eS],
                        serial=True)
            if mt_o is not None:
                ctx.dma('y', 'oM', mt_o[r0:r0 + KD1, :], s_Mt[:, :], waits=[eM],
                        serial=True)
            ctx.dma('y', 'oG', gt_o[r0:r0 + KD1, :], s_GT[:, :], waits=[eGT], serial=True)
            if chain:
                stP = ctx.dma('y', 'oP', plt_o[r0:r0 + KD1, :],
                              s_PLt[:, h * KD1:(h + 1) * KD1],
                              waits=[ctx.tok('v')], serial=True)
                mgc = op('t', lambda te: te.matmul(pT[:, 0:1], s_on[0:1, :],
                                                   s_gr[0:1, ch - 1:ch],
                                                   start=True, stop=True), [g1, eGT])
                egc = op('v', lambda v: v.tensor_copy(s_gcol[:, :], pT[:, 0:1]), [mgc])
                mC = op('t', lambda te, h=h: te.matmul(
                    pS[:, :], s_Mt[:, :], s_S0[:, h * VD1:(h + 1) * VD1],
                    start=True, stop=True), [eM, eS, mO])
                t1c = op('v', lambda v, h=h: v.scalar_tensor_tensor(
                    s_tmp3[:, :], s_S0[:, h * VD1:(h + 1) * VD1], s_gcol[:, 0:1],
                    pS[:, :], ALU.mult, ALU.subtract), [mC, egc])
                t2c = op('v', lambda v, h=h: v.tensor_tensor(
                    s_S0[:, h * VD1:(h + 1) * VD1], s_tmp3[:, :], s_Sl[:, :], ALU.add),
                    [t1c])
                gi = op('v', lambda v: v.tensor_scalar(s_gI[:, :], s_id[:, :],
                                                       s_gcol[:, 0:1], None, ALU.mult),
                        [egc])
                lt_ = op('v', lambda v: v.scalar_tensor_tensor(
                    s_Lt[:, :], s_Mt[:, :], -1.0, s_gI[:, :], ALU.mult, ALU.add),
                    [gi, eM])
                mPA = op('t', lambda te, h=h: te.matmul(
                    pM[:, :], s_Lt[:, :], s_PLn[:, h * KD1:(h + 1) * KD1],
                    start=True, stop=True), [lt_, eM])
                mPB = op('t', lambda te, h=h: te.matmul(
                    pS[:, 0:KD1], s_PLn[:, h * KD1:(h + 1) * KD1], s_Lt[:, :],
                    start=True, stop=True), [lt_, t1c, mPA])
                ePA = op('v', lambda v, h=h: v.tensor_copy(
                    s_PLn[:, h * KD1:(h + 1) * KD1], pM[:, :]), [mPA, mPB])
                ePB = op('v', lambda v, h=h: v.tensor_copy(
                    s_PLt[:, h * KD1:(h + 1) * KD1], pS[:, 0:KD1]),
                    [mPB, stP])
            if not hasattr(ctx, '_pv'):
                ctx._pv = {}
            ctx._pv['p1'] = xT
            ctx._pv['p2'] = xt_tok
            ctx._pv['lgb'] = ctx.tok('v')
            ctx._pv['lgb_s'] = ctx.tok('s')

        for h in range(H1):
            emit_head(h)
        vt = ctx.tok('v')
        ctx.dma('y', 'oO', olo_o[t0:t0 + ch, :], s_O[:, :], waits=[vt], serial=True)
        if gc_o is not None:
            ctx.dma('y', 'ogc', gc_o[u:u + 1, :], s_gc[:, :], waits=[vt], serial=True)

    if chain:
        vt = ctx.tok('v')
        for h in range(H1):
            ctx.dma('y', 'oAS', ags_o[h * KD1:(h + 1) * KD1, :],
                    s_S0[:, h * VD1:(h + 1) * VD1], waits=[vt], serial=True)
            ctx.dma('y', 'oAP', agp_o[h * KD1:(h + 1) * KD1, :],
                    s_PLt[:, h * KD1:(h + 1) * KD1], waits=[vt], serial=True)
    if own:
        ctx.emit()
    return nc



def build_delta_chain(nunits, nchg, ch=CH1, nc=None, ctx=None, io=None):
    own = nc is None
    if own:
        nc = bass.Bass('TRN2', target_bir_lowering=False)
    if ctx is None:
        ctx = Ctx(nc)
    mt_h = io['Mt'] if io is not None else nc.dram_tensor('Mt', [nunits * nchg * KD1, KD1], F32, kind='ExternalInput').ap()
    sl_h = io['SlocT'] if io is not None else nc.dram_tensor('SlocT', [nunits * nchg * KD1, VD1], F32, kind='ExternalInput').ap()
    gt_h = io['GT'] if io is not None else nc.dram_tensor('GT', [nunits * nchg * KD1, ch], F32, kind='ExternalInput').ap()
    gm_h = io['gam_bc'] if io is not None else nc.dram_tensor('gam_bc', [nunits * nchg, 128], F32, kind='ExternalInput').ap()
    oc_o = io['O_corr'] if io is not None else nc.dram_tensor('O_corr', [nunits * nchg * ch, VD1], F32, kind='ExternalOutput').ap()
    s_S0 = ctx.sb('s_S0', [128, VD1], F32)
    s_tmp = ctx.sb('s_tmp', [128, VD1], F32)
    s_mt = ctx.sb('s_mt', [128, 2 * KD1], F32)
    s_sl = ctx.sb('s_sl', [128, 2 * VD1], F32)
    s_gt = ctx.sb('s_gt', [128, 2 * ch], F32)
    s_gm = ctx.sb('s_gm', [128, 2], F32)
    s_oc = ctx.sb('s_oc', [ch, 2 * VD1], F32)
    pC = ctx.ps('pC', [128, VD1], F32)
    pO = ctx.ps('pO', [ch, VD1], F32)
    for un in range(nunits):
        z = ctx.op('v', lambda v: v.memset(s_S0[:, :], 0.0),
                   waits=[ctx._last] if hasattr(ctx, '_last') else [])
        upd = z
        for c in range(nchg):
            par = c % 2
            r0 = (un * nchg + c) * KD1
            lw = [ctx.tok('v')] if c >= 2 or un > 0 else []
            l1 = ctx.dma('g', 'ld%d' % par, s_mt[:, par * KD1:(par + 1) * KD1],
                         mt_h[r0:r0 + KD1, :], waits=lw, serial=True)
            l2 = ctx.dma('g', 'ld%d' % par, s_sl[:, par * VD1:(par + 1) * VD1],
                         sl_h[r0:r0 + KD1, :], serial=True)
            l3 = ctx.dma('g', 'ld%d' % par, s_gt[:, par * ch:(par + 1) * ch],
                         gt_h[r0:r0 + KD1, :], serial=True)
            l4 = ctx.dma('y', 'lh%d' % par, s_gm[:, par:par + 1],
                         gm_h[un * nchg + c:un * nchg + c + 1, :].transpose([1, 0]),
                         serial=True)
            mO = ctx.op('t', lambda te, par=par: te.matmul(
                pO[:, :], s_gt[:, par * ch:(par + 1) * ch], s_S0[:, :],
                start=True, stop=True),
                waits=[l3, upd, ctx._po if hasattr(ctx, '_po') else None])
            eO = ctx.op('v', lambda v, par=par: v.tensor_copy(
                s_oc[:, par * VD1:(par + 1) * VD1], pO[:, :]),
                waits=[mO, ctx.dtok('oc%d' % par) if ctx.has_dma('oc%d' % par) else None])
            ctx._po = eO
            ro = (un * nchg + c) * ch
            ctx.dma('y', 'oc%d' % par, oc_o[ro:ro + ch, :],
                    s_oc[:, par * VD1:(par + 1) * VD1], waits=[eO], serial=True)
            mC = ctx.op('t', lambda te, par=par: te.matmul(
                pC[:, :], s_mt[:, par * KD1:(par + 1) * KD1], s_S0[:, :],
                start=True, stop=True),
                waits=[l1, upd, ctx._pc if hasattr(ctx, '_pc') else None])
            t1 = ctx.op('v', lambda v, par=par: v.scalar_tensor_tensor(
                s_tmp[:, :], s_S0[:, :], s_gm[:, par:par + 1], pC[:, :],
                ALU.mult, ALU.subtract), waits=[mC, l4, mO])
            ctx._pc = t1
            upd = ctx.op('v', lambda v, par=par: v.tensor_tensor(
                s_S0[:, :], s_tmp[:, :], s_sl[:, par * VD1:(par + 1) * VD1], ALU.add),
                waits=[t1, l2])
        ctx._last = upd
    if own:
        ctx.emit()
    return nc



def build_gate_out(Tn=T, nc=None, ctx=None, io=None):
    own = nc is None
    if own:
        nc = bass.Bass('TRN2', target_bir_lowering=False)
    if ctx is None:
        ctx = Ctx(nc)
    ol_h = io['O_loc'] if io is not None else nc.dram_tensor('O_loc', [Tn, VAL_DIM], F32, kind='ExternalInput').ap()
    oc_h = io['O_corr'] if io is not None else nc.dram_tensor('O_corr', [Tn, VAL_DIM], F32, kind='ExternalInput').ap()
    g_h = io['g_tm'] if io is not None else nc.dram_tensor('g_tm', [Tn, VAL_DIM], F32, kind='ExternalInput').ap()
    og_o = io['og'] if io is not None else nc.dram_tensor('og', [Tn, VAL_DIM], BF16, kind='ExternalOutput').ap()
    nr = Tn // 128
    sA = ctx.sb('sA', [128, 2 * VAL_DIM], F32)
    sB = ctx.sb('sB', [128, VAL_DIM], F32)
    sG = ctx.sb('sG', [128, VAL_DIM], F32)
    sq = ctx.sb('sq', [128, VAL_DIM], BF16)
    sm = ctx.sb('sm', [128, 3 * H1], F32)
    so = ctx.sb('so', [128, 2 * VAL_DIM], BF16)
    eps_ap, ept = ctx.const_col(1e-5, 'epsr')
    prev = {}
    for r in range(nr):
        par = r % 2
        at = sA[:, par * VAL_DIM:(par + 1) * VAL_DIM]
        ot = so[:, par * VAL_DIM:(par + 1) * VAL_DIM]
        l1 = ctx.dma('y', 'lA%d' % par, at, ol_h[r * 128:(r + 1) * 128, :],
                     waits=[prev.get(('a', par))], serial=True)
        l2 = ctx.dma('y', 'lB', sB[:, :], oc_h[r * 128:(r + 1) * 128, :],
                     waits=[prev.get('b')], serial=True)
        l3 = ctx.dma('y', 'lG', sG[:, :], g_h[r * 128:(r + 1) * 128, :],
                     waits=[prev.get('g2')], serial=True)
        t1 = ctx.op('v', lambda v, at=at: v.tensor_tensor(at, at, sB[:, :], ALU.add),
                    waits=[l1, l2])
        prev['b'] = t1
        t2 = ctx.op('s', lambda s_: s_.activation(sG[:, :], sG[:, :], AF.Silu),
                    waits=[l3, prev.get('g2')])
        prev['g'] = t2
        prev['g2'] = None
        hts = []
        for h in range(H1):
            hs = slice(h * VD1, (h + 1) * VD1)
            q1 = ctx.op('s', lambda s_, at=at, hs=hs, h=h: s_.activation(
                sq[:, hs], at[:, hs], AF.Square, accum_out=sm[:, h:h + 1]),
                waits=[t1, prev.get('sq')])
            q2 = ctx.op('s', lambda s_, h=h: s_.activation(
                sm[:, H1 + h:H1 + h + 1], sm[:, h:h + 1], AF.Sqrt, scale=1.0 / VD1,
                bias=eps_ap[:, 0:1]), waits=[q1, ept])
            q3 = ctx.op('v', lambda v, h=h: v.reciprocal(
                sm[:, 2 * H1 + h:2 * H1 + h + 1], sm[:, H1 + h:H1 + h + 1]), waits=[q2])
            stw = ctx.dtok('o%d' % par) if ctx.has_dma('o%d' % par) else None
            q4 = ctx.op('v', lambda v, at=at, ot=ot, hs=hs, h=h: v.tensor_scalar(
                at[:, hs], at[:, hs], sm[:, 2 * H1 + h:2 * H1 + h + 1], None, ALU.mult),
                waits=[q3])
            q5 = ctx.op('v', lambda v, at=at, ot=ot, hs=hs: v.tensor_tensor(
                ot[:, hs], at[:, hs], sG[:, hs], ALU.mult), waits=[q4, t2, stw])
            hts.append(q5)
        prev['sq'] = hts[-1]
        prev[('a', par)] = hts[-1]
        prev['g2'] = hts[-1]
        ctx.dma('y', 'o%d' % par, og_o[r * 128:(r + 1) * 128, :], ot, waits=[hts[-1]],
                serial=True)
    if own:
        ctx.emit()
    return nc



def build_amax_part(C_, Tn=T, nc=None, ctx=None, io=None):
    own = nc is None
    if own:
        nc = bass.Bass('TRN2', target_bir_lowering=False)
    if ctx is None:
        ctx = Ctx(nc)
    x_h = io['x'] if io is not None else nc.dram_tensor('x', [C_, Tn], F32, kind='ExternalInput').ap()
    idf_h = io['identF'] if io is not None else nc.dram_tensor('identF', [128, 128], F32, kind='ExternalInput').ap()
    p_o = io['part'] if io is not None else nc.dram_tensor('part', [1, 1], F32, kind='ExternalOutput').ap()
    nk = C_ // 128
    sx = ctx.sb('sx', [128, 2 * Tn], F32)
    sc = ctx.sb('sc', [128, nk + 2], F32)
    sr = ctx.sb('sr', [1, 129], F32)
    s_id = ctx.sb('s_id', [128, 128], F32)
    pt = ctx.ps('pt', [1, 128], F32)
    ctx.dma('y', 'c', s_id[:, :], idf_h[:, :])
    ctok = ctx.dtok('c')
    prev = {}
    toks = []
    for k in range(nk):
        par = k % 2
        xt = sx[:, par * Tn:(par + 1) * Tn]
        lt = ctx.dma('g', 'x%d' % par, xt, x_h[k * 128:(k + 1) * 128, :],
                     waits=[prev.get(par)], serial=True)
        tk = ctx.op('v', lambda v, xt=xt, k=k: v.reduce_max(
            sc[:, k:k + 1], xt, axis=AXL.X, apply_absolute_value=True), waits=[lt])
        prev[par] = tk
        toks.append(tk)
    t1 = ctx.op('v', lambda v: v.reduce_max(sc[:, nk:nk + 1], sc[:, 0:nk], axis=AXL.X),
                waits=[toks[-1]])
    m1 = ctx.op('t', lambda te: te.transpose(pt[:, :], sc[:, nk:nk + 1], s_id[:, :]),
                waits=[t1, ctok])
    e1 = ctx.op('v', lambda v: v.tensor_copy(sr[:, 0:128], pt[:, :]), waits=[m1])
    t2 = ctx.op('v', lambda v: v.reduce_max(sr[:, 128:129], sr[:, 0:128], axis=AXL.X),
                waits=[e1])
    ctx.dma('y', 'o', p_o[:, :], sr[:, 128:129], waits=[t2])
    if own:
        ctx.emit()
    return nc



def build_ffn_fp8(Tn=T, nc=None, ctx=None, io=None):
    own = nc is None
    if own:
        nc = bass.Bass('TRN2', target_bir_lowering=False)
    if ctx is None:
        ctx = Ctx(nc)
    parts_h = io['parts'] if io is not None else nc.dram_tensor('parts', [1, NCORES], F32, kind='ExternalInput').ap()
    x_h = io['x'] if io is not None else nc.dram_tensor('x', [DM, Tn], F32, kind='ExternalInput').ap()
    w_h = io['w'] if io is not None else nc.dram_tensor('w', [DM, DM], F32, kind='ExternalInput').ap()
    b_h = io['bias'] if io is not None else nc.dram_tensor('bias', [DM, 1], F32, kind='ExternalInput').ap()
    r_h = io['resid'] if io is not None else nc.dram_tensor('resid', [DM, Tn], F32, kind='ExternalInput').ap()
    on_h = io['ones1'] if io is not None else nc.dram_tensor('ones1', [1, 128], F32, kind='ExternalInput').ap()
    idf_h = io['identF'] if io is not None else nc.dram_tensor('identF', [128, 128], F32, kind='ExternalInput').ap()
    out_o = io['out'] if io is not None else nc.dram_tensor('out', [DM, Tn], F32, kind='ExternalOutput').ap()
    nk = DM // 128
    sxf = ctx.sb('sxf', [128, 2 * Tn], F32)
    sx8 = ctx.sb('sx8', [128, Tn], F8)
    sxq = ctx.sb('sxq', [128, nk * Tn], BF16)
    swf = ctx.sb('swf', [128, nk * DM], F32)
    sw8 = ctx.sb('sw8', [128, DM], F8)
    swq = ctx.sb('swq', [128, nk * DM], BF16)
    sbias = ctx.sb('sbias', [128, nk], F32)
    sres = ctx.sb('sres', [128, nk * Tn], F32)
    s_sc = ctx.sb('s_sc', [1, 32], F32)
    s_cx = ctx.sb('s_cx', [128, 3], F32)
    s_id = ctx.sb('s_id', [128, 128], F32)
    s_on = ctx.sb('s_on', [1, 128], F32)
    s_mx = ctx.sb('s_mx', [128, nk + 2], F32)
    sr = ctx.sb('sr', [1, 129], F32)
    sout = ctx.sb('sout', [128, 2 * Tn], F32)
    pz = [ctx.ps('pz%d' % i, [128, 512], F32) for i in range(2)]
    pt = ctx.ps('pt', [1, 128], F32)
    pc = ctx.ps('pc', [128, 3], F32)
    ctx.dma('y', 'c', s_id[:, :], idf_h[:, :])
    ctx.dma('y', 'c', s_on[:, :], on_h[:, :])
    ctx.dma('y', 'c', s_sc[:, 0:NCORES], parts_h[:, :])
    for k in range(nk):
        ctx.dma('y', 'c', swf[:, k * DM:(k + 1) * DM], w_h[k * 128:(k + 1) * 128, :])
        ctx.dma('y', 'c', sbias[:, k:k + 1], b_h[k * 128:(k + 1) * 128, :])
        ctx.dma('y', 'c', sres[:, k * Tn:(k + 1) * Tn], r_h[k * 128:(k + 1) * 128, :])
    ctok = ctx.dtok('c')
    t1 = ctx.op('v', lambda v: v.reduce_max(s_sc[:, 8:9], s_sc[:, 0:NCORES], axis=AXL.X),
                waits=[ctok])
    wts = []
    for k in range(nk):
        wts.append(ctx.op('v', lambda v, k=k: v.reduce_max(
            s_mx[:, k:k + 1], swf[:, k * DM:(k + 1) * DM], axis=AXL.X,
            apply_absolute_value=True), waits=[ctok]))
    t2 = ctx.op('v', lambda v: v.reduce_max(s_mx[:, nk:nk + 1], s_mx[:, 0:nk],
                                            axis=AXL.X), waits=[wts[-1]])
    m1 = ctx.op('t', lambda te: te.transpose(pt[:, :], s_mx[:, nk:nk + 1], s_id[:, :]),
                waits=[t2, ctok])
    e1 = ctx.op('v', lambda v: v.tensor_copy(sr[:, 0:128], pt[:, :]), waits=[m1])
    t3 = ctx.op('v', lambda v: v.reduce_max(s_sc[:, 9:10], sr[:, 0:128], axis=AXL.X),
                waits=[e1])
    t4 = ctx.op('v', lambda v: v.tensor_scalar(s_sc[:, 10:12], s_sc[:, 8:10], 1e-12, None,
                                               ALU.max), waits=[t3, t1])
    t5 = ctx.op('v', lambda v: v.reciprocal(s_sc[:, 12:14], s_sc[:, 10:12]), waits=[t4])
    t6 = ctx.op('v', lambda v: v.tensor_scalar(s_sc[:, 14:16], s_sc[:, 12:14], 224.0,
                                               None, ALU.mult), waits=[t5])
    t7 = ctx.op('v', lambda v: v.tensor_tensor(s_sc[:, 16:17], s_sc[:, 14:15],
                                               s_sc[:, 15:16], ALU.mult), waits=[t6])
    t8 = ctx.op('v', lambda v: v.reciprocal(s_sc[:, 17:18], s_sc[:, 16:17]), waits=[t7])
    m2 = ctx.op('t', lambda te: te.matmul(pc[:, 0:3], s_on[0:1, :],
                                          s_sc[0:1, 14:17].copy() if False else
                                          s_sc[0:1, 14:17], start=True, stop=True),
                waits=[t8, ctok])
    e2 = ctx.op('v', lambda v: v.tensor_copy(s_cx[:, 0:3], pc[:, 0:3]), waits=[m2])
    e2b = ctx.op('v', lambda v: v.reciprocal(s_cx[:, 2:3], s_cx[:, 2:3]), waits=[e2])
    wq_toks = []
    for k in range(nk):
        a = ctx.op('v', lambda v, k=k: v.tensor_scalar(
            swf[:, k * DM:(k + 1) * DM], swf[:, k * DM:(k + 1) * DM], s_cx[:, 1:2], None,
            ALU.mult), waits=[e2, t2 if k == 0 else None])
        b_ = ctx.op('v', lambda v, k=k: v.tensor_copy(sw8[:, 0:DM],
                                                      swf[:, k * DM:(k + 1) * DM]),
                    waits=[a, wq_toks[-1] if wq_toks else None])
        c_ = ctx.op('v', lambda v, k=k: v.tensor_copy(swq[:, k * DM:(k + 1) * DM],
                                                      sw8[:, 0:DM]), waits=[b_])
        wq_toks.append(c_)
    xq_toks = []
    prevx = {}
    for k in range(nk):
        par = k % 2
        xt = sxf[:, par * Tn:(par + 1) * Tn]
        lt = ctx.dma('g', 'x%d' % par, xt, x_h[k * 128:(k + 1) * 128, :],
                     waits=[prevx.get(par)], serial=True)
        a = ctx.op('v', lambda v, xt=xt: v.tensor_scalar(xt, xt, s_cx[:, 0:1], None,
                                                         ALU.mult), waits=[lt, e2])
        b_ = ctx.op('v', lambda v, xt=xt: v.tensor_copy(sx8[:, :], xt),
                    waits=[a, xq_toks[-1] if xq_toks else None])
        prevx[par] = b_
        c_ = ctx.op('v', lambda v, k=k: v.tensor_copy(sxq[:, k * Tn:(k + 1) * Tn],
                                                      sx8[:, :]), waits=[b_])
        xq_toks.append(c_)
    xq = ctx.tok('v')
    prev_ev = {}
    for i in range(nk):
        par = i % 2
        ot = sout[:, par * Tn:(par + 1) * Tn]
        evs = []
        tbk = min(512, Tn)
        for j in range(cdiv(Tn, tbk)):
            t0j = j * tbk
            tn = min(tbk, Tn - t0j)
            bank = (i * cdiv(Tn, tbk) + j) % 2
            ps = pz[bank]
            for k in range(nk):
                def mm(te, k=k, i=i, t0j=t0j, tn=tn, ps=ps):
                    return te.matmul(ps[:, 0:tn],
                                     swq[:, k * DM + i * 128:k * DM + (i + 1) * 128],
                                     sxq[:, k * Tn + t0j:k * Tn + t0j + tn],
                                     start=(k == 0), stop=(k == nk - 1))
                w8 = [xq, prev_ev.get(bank)] if k == 0 else []
                tk = ctx.op('t', mm, waits=w8)
            stw = ctx.dtok('st%d' % par) if ctx.has_dma('st%d' % par) else None
            ev = ctx.op('v', lambda v, i=i, t0j=t0j, tn=tn, ps=ps, ot=ot: v.tensor_scalar(
                ot[:, t0j:t0j + tn], ps[:, 0:tn], s_cx[:, 2:3], sbias[:, i:i + 1],
                ALU.mult, ALU.add), waits=[tk, e2b, stw])
            ev = ctx.op('v', lambda v, i=i, t0j=t0j, tn=tn, ot=ot: v.tensor_tensor(
                ot[:, t0j:t0j + tn], ot[:, t0j:t0j + tn],
                sres[:, i * Tn + t0j:i * Tn + t0j + tn], ALU.add), waits=[ev])
            prev_ev[bank] = ev
            evs.append(ev)
        ctx.dma('y', 'st%d' % par, out_o[i * 128:(i + 1) * 128, :], ot,
                waits=[evs[-1]], serial=True)
    if own:
        ctx.emit()
    return nc



def emit_transpose(nc, ctx, src_ap, dst_ap, C_, Tn, dtype, ident_ap, itok):
    nr = cdiv(C_, 128)
    ntt = cdiv(Tn, 128)
    st_in = ctx.sb('tp_in', [128, Tn], dtype)
    st_out = ctx.sb('tp_out', [128, min(C_, 8192)], dtype)
    ptr = [ctx.ps('tp_ps%d' % i, [128, 128], dtype) for i in range(2)]
    prev = {0: None, 1: None}
    prev_row = {}
    for tt in range(ntt):
        tp = min(128, Tn - tt * 128)
        evs = []
        for cc in range(nr):
            cp = min(128, C_ - cc * 128)
            par = cc % 2
            lt = ctx.dma('g', 'tp%d' % par, st_in[0:cp, 0:Tn],
                         src_ap[cc * 128:cc * 128 + cp, :], serial=True,
                         waits=[prev.get(('l', par)), prev_row.get('store')])
            tk = ctx.op('t', lambda te, cp=cp, tt=tt, tp=tp, par=par: te.transpose(
                ptr[par][0:tp, 0:cp], st_in[0:cp, tt * 128:tt * 128 + tp],
                ident_ap[0:cp, 0:cp]), waits=[lt, itok, prev.get(par)])
            ev = ctx.op('v', lambda v, cp=cp, cc=cc, tp=tp, par=par: v.tensor_copy(
                st_out[0:tp, cc * 128:cc * 128 + cp], ptr[par][0:tp, 0:cp]),
                waits=[tk, prev_row.get('store')])
            prev[par] = ev
            prev[('l', par)] = tk
            evs.append(ev)
        prev_row['store'] = ctx.dma('y', 'tpo',
                                    dst_ap[tt * 128:tt * 128 + tp, :],
                                    st_out[0:tp, 0:C_], serial=True, waits=[evs[-1]])


def emit_transpose_blocked(nc, ctx, src_ap, dst_ap, C_, Tn, dtype, ident_ap, itok):
    nr = cdiv(C_, 128)
    ntt = cdiv(Tn, 128)
    st_in = ctx.sb('tq_in', [128, 2 * 128], dtype)
    st_out = ctx.sb('tq_out', [128, C_], dtype)
    ptr = [ctx.ps('tq_ps%d' % i, [128, 128], dtype) for i in range(2)]
    prev = {}
    for tt in range(ntt):
        tp = min(128, Tn - tt * 128)
        evs = []
        for cc in range(nr):
            cp = min(128, C_ - cc * 128)
            par = cc % 2
            xt = st_in[:, par * 128:par * 128 + 128]
            lt = ctx.dma('g', 'tq%d' % par, xt[0:cp, 0:tp],
                         src_ap[cc * 128:cc * 128 + cp, tt * 128:tt * 128 + tp],
                         serial=True, waits=[prev.get(('t', par))])
            tk = ctx.op('t', lambda te, xt=xt, cp=cp, tp=tp, par=par: te.transpose(
                ptr[par][0:tp, 0:cp], xt[0:cp, 0:tp], ident_ap[0:cp, 0:cp]),
                waits=[lt, itok, prev.get(par)])
            prev[('t', par)] = tk
            ev = ctx.op('v', lambda v, cc=cc, cp=cp, tp=tp, par=par: v.tensor_copy(
                st_out[0:tp, cc * 128:cc * 128 + cp], ptr[par][0:tp, 0:cp]),
                waits=[tk, prev.get('store')])
            prev[par] = ev
            evs.append(ev)
        prev['store'] = ctx.dma('y', 'tqo', dst_ap[tt * 128:tt * 128 + tp, :],
                                st_out[0:tp, 0:C_], serial=True, waits=[evs[-1]])


def emit_allgather(nc, ctx, name, in_ap, out_ap, groups):
    sem = nc.alloc_semaphore('cc_' + name)
    def fn(eng):
        eng.collective_compute('AllGather', ALU.bypass, replica_groups=groups,
                               ins=[in_ap.opt()], outs=[out_ap.opt()]).then_inc(sem, 1)
    ctx.script['g'].append(fn)
    ctx.dsem['cc_' + name] = sem
    ctx.dcnt['cc_' + name] = 1
    return (sem, 1)


def build_fused(seg=None):
    nc = bass.Bass('TRN2', target_bir_lowering=False)
    ctx = Ctx(nc)
    Tn = T
    nch0 = Tn // CH0
    nch1 = Tn // CH1
    EXI = 'ExternalInput'
    per_b = NCORES // B

    def inp(name, shape, dtype=F32, segs=None):
        if seg is not None and segs is not None and seg not in segs:
            return None
        return nc.dram_tensor(name, shape, dtype, kind=EXI).ap()

    def scr(name, shape, dtype=F32, segs=None):
        if seg is not None and segs is not None and seg not in segs:
            return None
        return nc.dram_tensor('scr_' + name, shape, dtype, kind='Internal').ap()

    def bnd(name, shape, dtype, prod, cons):
        if seg is None:
            return nc.dram_tensor('scr_' + name, shape, dtype, kind='Internal').ap()
        if seg == prod:
            return nc.dram_tensor(name, shape, dtype, kind='ExternalOutput').ap()
        if seg in cons:
            return nc.dram_tensor(name, shape, dtype, kind=EXI).ap()
        return None

    def in_seg(k):
        return seg is None or seg == k

    u_hal = inp('u_hal', [DM, Tn + 3], F32, segs=(1, 2))
    W_inT = inp('W_inT', [DM, D_IN_PROJ], F32, segs=(1,))
    b_in = inp('b_in', [D_IN_PROJ, 1], F32, segs=(1,))
    conv_w = inp('conv_w', [CONV_DIM, 4], F32, segs=(1,))
    conv_b = inp('conv_b', [CONV_DIM, 1], F32, segs=(1,))
    dt_bias = inp('dt_bias', [NH0, 1], F32, segs=(1,))
    A_log = inp('A_log', [NH0, 1], F32, segs=(1,))
    D_bc = inp('D_bc', [128, NH0], F32, segs=(1,))
    mask0 = inp('mask0', [CH0, CH0], F32, segs=(1,))
    ones1 = inp('ones1', [1, 128])
    identB = inp('identB', [128, 128], BF16)
    identF = inp('identF', [128, 128])
    norm_w = inp('norm_w', [DSSM, 1], F32, segs=(2,))
    W_outT = inp('W_outT', [DSSM, DM], F32, segs=(2,))
    b_out = inp('b_out', [DM, 1], F32, segs=(2,))
    w1 = inp('w1', [DM, 4096], F32, segs=(2,))
    b1 = inp('b1', [4096, 1], F32, segs=(2,))
    w2 = inp('w2', [4096, DM], F32, segs=(2,))
    b2 = inp('b2', [DM, 1], F32, segs=(2,))
    Wcat = inp('Wcat', [DM, 6160], F32, segs=(3,))
    convq_w = inp('convq_w', [KEY_DIM, 4], F32, segs=(3,))
    convk_w = inp('convk_w', [KEY_DIM, 4], F32, segs=(3,))
    convv_w = inp('convv_w', [VAL_DIM, 4], F32, segs=(3,))
    dt_bias1 = inp('dt_bias1', [H1, 1], F32, segs=(3,))
    A_log1 = inp('A_log1', [H1, 1], F32, segs=(3,))
    ident2 = inp('ident2', [CH1, CH1], F32, segs=(3,))
    mask_i1 = inp('mask_i1', [CH1, CH1], F32, segs=(3,))
    mask_s1 = inp('mask_s1', [CH1, CH1], F32, segs=(3,))
    onw = inp('onw', [VAL_DIM, 1], F32, segs=(4,))
    WoT = inp('WoT', [VAL_DIM, DM], F32, segs=(4,))
    w1f = inp('w1f', [DM, DM], F32, segs=(5,))
    b1f = inp('b1f', [DM, 1], F32, segs=(5,))
    hmask = inp('hmask', [128, 1], F32, segs=(1, 3))
    mam_mask = inp('mam_mask', [128, per_b], F32, segs=(2, 4))
    halo_sel = inp('halo_sel', [128, per_b], F32, segs=(3,))
    out_ap = (nc.dram_tensor('out', [Tn, DM], F32, kind='ExternalOutput').ap()
              if in_seg(5) else None)

    groups_b = [list(range(b * per_b, (b + 1) * per_b)) for b in range(B)]
    groups_all = [list(range(NCORES))]

    zx = scr('zx', [D_IN_PROJ, Tn + 3], F32, segs=(1,))
    z_x = bnd('z_x', [DSSM, Tn], F32, 1, (2,))
    c_x = bnd('c_x', [DST, Tn], F32, 1, (2,))
    convout = scr('convout', [CONV_DIM, Tn], F32, segs=(1,))
    rows = {n: scr(n, [NH0, Tn], F32, segs=(1,)) for n in ('dt', 'cs', 'expcs', 'dsdt')}
    cslast = scr('cslast', [NH0, nch0], F32, segs=(1,))
    rows_tm = {n: scr(n + '_tm', [Tn, NH0], F32, segs=(1,))
               for n in ('dt', 'cs', 'dsdt')}
    rows_tm['expcs'] = bnd('expcs_tm', [Tn, NH0], F32, 1, (2,))
    xcl8 = bnd('xcl8', [nch0, NH0], F32, 1, (2,))
    ydx = bnd('ydx', [Tn, DSSM], F32, 1, (2,))
    states = scr('states', [nch0 * DSSM, DST], F32, segs=(1,))
    prev_s = scr('prev', [nch0, DSSM, DST], F32, segs=(1,))
    s_fin = scr('s_fin', [DSSM, DST], F32, segs=(1,))
    s_finT = scr('s_finT', [DST, DSSM], F32, segs=(1,))
    prevT = bnd('prevT', [nch0 * DST, DSSM], F32, 1, (2,))
    decay_bc = scr('decay_bc', [nch0, 128, DSSM // 128], F32, segs=(1,))
    drow8 = scr('drow8', [nch0, NH0], F32, segs=(1,))
    ag_m_in = bnd('ag_m_in', [128, DSSM + 32], F32, 1, ())
    ag_m_out = bnd('ag_m_out', [per_b * 128, DSSM + 32], F32, -1, (2,))
    s_inT = scr('s_inT', [DST, DSSM], F32, segs=(2,))
    yn = scr('yn', [Tn, DSSM], BF16, segs=(2,))
    ynT = scr('ynT', [DSSM, Tn], BF16, segs=(2,))
    s1 = scr('s1', [DM, Tn], F32, segs=(2,))
    s1_tm = scr('s1_tm', [Tn, DM], F32, segs=(2,))
    x1_tm = scr('x1_tm', [Tn, DM], F32, segs=(2,))
    x1 = scr('x1', [DM, Tn], F32, segs=(2,))
    g_s = scr('g_s', [4096, Tn], BF16, segs=(2,))
    part = scr('part', [DM, Tn], F32, segs=(2,))
    s2 = scr('s2', [DM, Tn], F32, segs=(2,))
    s2_tm = scr('s2_tm', [Tn, DM], F32, segs=(2,))
    x2_tm = scr('x2_tm', [Tn, DM], F32, segs=(2,))
    x2 = bnd('x2', [DM, Tn], F32, 2, (3, 4))
    halo_pub = bnd('halo_pub', [DM, 3], F32, 2, ())
    halo_ag = bnd('halo_ag', [per_b * DM, 3], F32, -1, (3,))
    x2h = scr('x2h', [DM, Tn + 3], F32, segs=(3,))
    proj = scr('proj', [6160, Tn + 3], F32, segs=(3,))
    qc = scr('qc', [KEY_DIM, Tn], F32, segs=(3,))
    kc = scr('kc', [KEY_DIM, Tn], F32, segs=(3,))
    vc = scr('vc', [VAL_DIM, Tn], F32, segs=(3,))
    q_tm = scr('q_tm', [Tn, KEY_DIM], F32, segs=(3,))
    k_tm = scr('k_tm', [Tn, KEY_DIM], F32, segs=(3,))
    v_tm = scr('v_tm', [Tn, VAL_DIM], F32, segs=(3,))
    qn = scr('qn', [Tn, KEY_DIM], BF16, segs=(3,))
    kn = scr('kn', [Tn, KEY_DIM], BF16, segs=(3,))
    qn_cm = scr('qn_cm', [KEY_DIM, Tn], BF16, segs=(3,))
    kn_cm = scr('kn_cm', [KEY_DIM, Tn], BF16, segs=(3,))
    lg_r = scr('lg_r', [H1, Tn], F32, segs=(3,))
    beta_r = scr('beta_r', [H1, Tn], F32, segs=(3,))
    lg_tm = scr('lg_tm', [Tn, H1], F32, segs=(3,))
    beta_tm = scr('beta_tm', [Tn, H1], F32, segs=(3,))
    O_loc = bnd('O_loc', [Tn, VAL_DIM], F32, 3, (4,))
    GT_s = bnd('GT_s', [nch1 * H1 * KD1, CH1], F32, 3, (4,))
    PLT_s = bnd('PLT_s', [nch1 * H1 * KD1, KD1], F32, 3, (4,))
    ag_d_in = bnd('ag_d_in', [H1 * KD1, VD1 + KD1], F32, 3, ())
    ag_d_out = bnd('ag_d_out', [per_b * H1 * KD1, VD1 + KD1], F32, -1, (4,))
    g_tm_s = bnd('g_tm_s', [Tn, VAL_DIM], F32, 3, (4,))
    s_inT_d = scr('s_inT_d', [H1 * KD1, VD1], F32, segs=(4,))
    O_corr = scr('O_corr', [Tn, VAL_DIM], F32, segs=(4,))
    og = scr('og', [Tn, VAL_DIM], BF16, segs=(4,))
    ogT = scr('ogT', [VAL_DIM, Tn], BF16, segs=(4,))
    s3 = scr('s3', [DM, Tn], F32, segs=(4,))
    s3_tm = scr('s3_tm', [Tn, DM], F32, segs=(4,))
    x3_tm = scr('x3_tm', [Tn, DM], F32, segs=(4,))
    x3 = bnd('x3', [DM, Tn], F32, 4, (5,))
    am_part = bnd('am_part', [1, 1], F32, 4, ())
    am_ag = bnd('am_ag', [NCORES, 1], F32, -1, (5,))
    s4 = scr('s4', [DM, Tn], F32, segs=(5,))
    s4_tm = scr('s4_tm', [Tn, DM], F32, segs=(5,))

    def phase(pfx):
        ctx.end_phase()
        ctx.pfx = pfx + '_'

    def ldid(dtype):
        t = ctx.sb('ident_l', [128, 128], dtype)
        tok = ctx.dma('y', 'idl', t[:, :], identB[:, :] if dtype == BF16 else identF[:, :])
        return t, tok

    if in_seg(1):
        ctx.pfx = 'p00_'
        build_linear(DM, D_IN_PROJ, Tn + 3, bias=True, nc=nc, ctx=ctx,
                     io={'x': u_hal, 'w': W_inT, 'bias': b_in, 'out': zx})
        phase('p01')
        build_mamba_pre(Tn=Tn, nc=nc, ctx=ctx, io={
            'xbc_pad': zx[DSSM:DSSM + CONV_DIM, :],
            'conv_w': conv_w, 'conv_b': conv_b,
            'dt_raw': zx[DSSM + CONV_DIM:, 3:], 'dt_bias': dt_bias, 'A_log': A_log,
            'convout': convout, 'dt': rows['dt'], 'cs': rows['cs'],
            'expcs': rows['expcs'], 'dsdt': rows['dsdt'], 'cslast': cslast,
            'hmask': hmask})
        phase('p01b')
        ctx.dma('y', 'cp1', z_x[:, :], zx[0:DSSM, 3:])
        ctx.dma('y', 'cp1', c_x[:, :], convout[DSSM + DST:, :])
        for ip, n in enumerate(('dt', 'cs', 'expcs', 'dsdt')):
            phase('p02%d' % ip)
            idF, itok = ldid(F32)
            emit_transpose_blocked(nc, ctx, rows[n], rows_tm[n], NH0, Tn, F32,
                                   idF[:, :], itok)
        phase('p03')
        s_cl = ctx.sb('cl', [NH0, nch0], F32)
        s_ecl = ctx.sb('ecl', [NH0, nch0], F32)
        s_xcl = ctx.sb('xcl', [NH0, nch0 + 1], F32)
        s_z8 = ctx.sb('z8', [NH0, nch0], F32)
        s_tr = ctx.sb('tr8', [nch0, NH0], F32)
        s_dx = ctx.sb('dx', [16, 128], F32)
        s_dxt = ctx.sb('dxt', [128, DSSM // 128], F32)
        p_t8 = ctx.ps('pt8', [nch0, NH0], F32)
        p_dx = ctx.ps('pdx', [128, 16], F32)
        idF, itok = ldid(F32)
        l1 = ctx.dma('y', 'cl', s_cl[:, :], cslast[:, :])
        z1 = ctx.op('v', lambda v: v.memset(s_z8[:, :], 0.0))
        t1 = ctx.op('v', lambda v: v.tensor_tensor_scan(
            s_xcl[:, 1:nch0 + 1], s_cl[:, :], s_z8[:, :], 0.0, ALU.add, ALU.add),
            waits=[l1, z1])
        t1b = ctx.op('v', lambda v: v.memset(s_xcl[:, 0:1], 0.0))
        t2 = ctx.op('s', lambda s_: s_.activation(s_ecl[:, :], s_cl[:, :], AF.Exp),
                    waits=[l1])
        t3 = ctx.op('s', lambda s_: s_.activation(s_xcl[:, 0:nch0], s_xcl[:, 0:nch0],
                                                  AF.Exp), waits=[t1, t1b])
        tr1 = ctx.op('t', lambda te, idF=idF: te.transpose(
            p_t8[:, :], s_ecl[:, :], idF[0:NH0, 0:NH0]), waits=[t2, itok])
        e1 = ctx.op('v', lambda v: v.tensor_copy(s_tr[:, :], p_t8[:, :]), waits=[tr1])
        st1 = ctx.dma('y', 'dr8', drow8[:, :], s_tr[:, :], waits=[e1])
        tr2 = ctx.op('t', lambda te, idF=idF: te.transpose(
            p_t8[:, :], s_xcl[:, 0:nch0], idF[0:NH0, 0:NH0]), waits=[t3, itok, e1])
        e2 = ctx.op('v', lambda v: v.tensor_copy(s_tr[:, :], p_t8[:, :]), waits=[tr2, st1])
        st2 = ctx.dma('y', 'dr8', xcl8[:, :], s_tr[:, :], waits=[e2])
        s_r32 = ctx.sb('r32', [1, NH0], F32)
        s_r2k = ctx.sb('r2k', [1, DSSM], F32)
        dscr = scr('d_exp_row', [1, DSSM], F32, segs=(1,))
        pv = None
        for c in range(nch0):
            lr = ctx.dma('y', 'r32', s_r32[:, :], drow8[c:c + 1, :],
                         waits=[st1, pv], serial=True)
            cp = ctx.op('v', lambda v: v.tensor_copy(
                s_r2k[:, :].rearrange('o (h r) -> o h r', r=HD0),
                s_r32[:, :].unsqueeze(2).broadcast_to([1, NH0, HD0])), waits=[lr])
            sd = ctx.dma('y', 'r32o', dscr[:, :], s_r2k[:, :], waits=[cp], serial=True)
            ld2 = ctx.dma('y', 'r32i', s_dx[:, :],
                          dscr[:, :].rearrange('o (a p) -> (o a) p', p=128),
                          waits=[sd], serial=True)
            tr = ctx.op('t', lambda te, idF=idF: te.transpose(
                p_dx[:, 0:16], s_dx[:, :], idF[0:16, 0:16]), waits=[ld2, itok, pv])
            ev = ctx.op('v', lambda v: v.tensor_copy(s_dxt[:, :], p_dx[:, 0:16]),
                        waits=[tr])
            pv = ev
            ctx.dma('y', 'dbc', decay_bc[c, :, :], s_dxt[:, :], waits=[ev], serial=True)

        phase('p04')
        build_ssd_local(Tn=Tn, ch=CH0, nc=nc, ctx=ctx, io={
            'x_cm': convout[0:DSSM, :], 'B_cm': convout[DSSM:DSSM + DST, :],
            'C_cm': convout[DSSM + DST:, :], 'dt_tm': rows_tm['dt'],
            'cs_row': rows['cs'], 'cs_tm': rows_tm['cs'], 'dsdt_tm': rows_tm['dsdt'],
            'D_bc': D_bc, 'mask': mask0, 'ones1': ones1, 'ident': identB,
            'ydx': ydx, 'states': states})
        phase('p05')
        build_state_scan(nch0, DSSM, nc=nc, ctx=ctx, io={
            'states': states.rearrange('(c r) d -> c r d', c=nch0),
            'decay_bc': decay_bc, 'prev': prev_s, 'final': s_fin})
        phase('p06')
        idF, itok = ldid(F32)
        emit_transpose_blocked(nc, ctx, s_fin, s_finT, DSSM, DST, F32, idF[:, :], itok)
        for c in range(nch0):
            phase('p06%d' % c)
            idF, itok = ldid(F32)
            emit_transpose_blocked(nc, ctx, prev_s[c, :, :],
                                   prevT[c * DST:(c + 1) * DST, :], DSSM, DST, F32,
                                   idF[:, :], itok)
        phase('p07')
        s_cl2 = ctx.sb('cl2', [NH0, nch0], F32)
        s_tt = ctx.sb('tt', [NH0, 2], F32)
        s_ttr = ctx.sb('ttr', [1, NH0], F32)
        p_tt = ctx.ps('ptt', [1, NH0], F32)
        idF, itok = ldid(F32)
        l1 = ctx.dma('y', 'cl2', s_cl2[:, :], cslast[:, :])
        t1 = ctx.op('v', lambda v: v.reduce_sum(s_tt[:, 0:1], s_cl2[:, :], axis=AXL.X),
                    waits=[l1])
        t2 = ctx.op('s', lambda s_: s_.activation(s_tt[:, 1:2], s_tt[:, 0:1], AF.Exp),
                    waits=[t1])
        tr = ctx.op('t', lambda te, idF=idF: te.transpose(
            p_tt[:, :], s_tt[:, 1:2], idF[0:NH0, 0:NH0]), waits=[t2, itok])
        e1 = ctx.op('v', lambda v: v.tensor_copy(s_ttr[:, :], p_tt[:, :]), waits=[tr])
        s_zpad = ctx.sb('zpad', [128, 32], F32)
        zp = ctx.op('v', lambda v: v.memset(s_zpad[:, :], 0.0))
        ctx.dma('y', 'agm', ag_m_in[:, DSSM:DSSM + 32], s_zpad[:, :], waits=[zp],
                serial=True)
        ctx.dma('y', 'agm', ag_m_in[0:1, DSSM:DSSM + 32], s_ttr[:, :], waits=[e1],
                serial=True)
        ctx.dma('y', 'agm', ag_m_in[:, 0:DSSM], s_finT[:, :], serial=True)

    cc1 = None
    if seg is None:
        ctx.end_phase()
        ctx.pfx = 'p07b_'
        cc1 = emit_allgather(nc, ctx, 'mam', ag_m_in, ag_m_out, groups_b)

    if in_seg(2):
        ctx.end_phase()
        ctx.pfx = 'p07c_'
        s_ST = ctx.sb('ST', [DST, DSSM], F32)
        s_SJ = ctx.sb('SJ', [DST, DSSM], F32)
        s_dj = ctx.sb('dj', [1, NH0], F32)
        s_dex = ctx.sb('dex', [1, DSSM], F32)
        s_dbc = ctx.sb('dbc', [128, DSSM], F32)
        s_mm = ctx.sb('mmk', [128, per_b], F32)
        s_on2 = ctx.sb('on2', [1, 128], F32)
        s_t1 = ctx.sb('ft1', [DST, DSSM], F32)
        p_bc2 = ctx.ps('pbc2', [128, 512], F32)
        ctx.dma('y', 'fm', s_mm[:, :], mam_mask[:, :])
        ctx.dma('y', 'fm', s_on2[:, :], ones1[:, :])
        lm = lo = ctx.dtok('fm')
        z1 = ctx.op('v', lambda v: v.memset(s_ST[:, :], 0.0))
        fold_tok = z1
        for j in range(per_b):
            lj = ctx.dma('y', 'fj', s_SJ[:, :], ag_m_out[j * 128:j * 128 + DST, 0:DSSM],
                         serial=True, waits=[cc1, fold_tok])
            ldj = ctx.dma('y', 'fj', s_dj[:, :],
                          ag_m_out[j * 128:j * 128 + 1, DSSM:DSSM + 32], serial=True)
            cp = ctx.op('v', lambda v: v.tensor_copy(
                s_dex[:, :].rearrange('o (h r) -> o h r', r=HD0),
                s_dj[:, :].unsqueeze(2).broadcast_to([1, NH0, HD0])),
                waits=[ldj, fold_tok])
            bts = []
            for blk in range(DSSM // 512):
                tb = ctx.op('t', lambda te, blk=blk: te.matmul(
                    p_bc2[:, :], s_on2[0:1, :], s_dex[0:1, blk * 512:(blk + 1) * 512],
                    start=True, stop=True), waits=[cp, lo, bts[-1] if bts else fold_tok])
                eb = ctx.op('v', lambda v, blk=blk: v.tensor_copy(
                    s_dbc[:, blk * 512:(blk + 1) * 512], p_bc2[:, :]), waits=[tb])
                bts.append(eb)
            t1 = ctx.op('v', lambda v: v.tensor_tensor(s_t1[:, :], s_dbc[0:DST, :],
                                                       s_ST[:, :], ALU.mult),
                        waits=[bts[-1], fold_tok])
            t2 = ctx.op('v', lambda v: v.tensor_tensor(s_t1[:, :], s_t1[:, :],
                                                       s_SJ[:, :], ALU.add),
                        waits=[t1, lj])
            t3 = ctx.op('v', lambda v: v.tensor_tensor(s_t1[:, :], s_t1[:, :],
                                                       s_ST[:, :], ALU.subtract),
                        waits=[t2])
            fold_tok = ctx.op('v', lambda v, j=j: v.scalar_tensor_tensor(
                s_ST[:, :], s_t1[:, :], s_mm[0:DST, j:j + 1], s_ST[:, :],
                ALU.mult, ALU.add), waits=[t3, lm])
        ctx.dma('y', 'fst', s_inT[:, :], s_ST[:, :], waits=[fold_tok])

        phase('p08')
        build_ssd_final(Tn=Tn, ch=CH0, nc=nc, ctx=ctx, io={
            'ydx': ydx, 'C_cm': c_x, 'prevT': prevT,
            'expcs_tm': rows_tm['expcs'], 'yn': yn,
            's_inT': s_inT,
            'sd_row': xcl8.rearrange('c h -> (c h)').unsqueeze(0), 'ones1': ones1,
            'z_cm': z_x, 'identF': identF})
        phase('p09')
        idB, itok = ldid(BF16)
        emit_transpose_blocked(nc, ctx, yn, ynT, Tn, DSSM, BF16, idB[:, :], itok)
        phase('p10')
        build_linear(DSSM, DM, Tn, bias=True, residual=True, wscale=True,
                     x_hbm_dtype=BF16, nc=nc, ctx=ctx,
                     io={'x': ynT, 'w': W_outT, 'bias': b_out, 'wscale': norm_w,
                         'resid': u_hal[:, 3:], 'out': s1})
        phase('p11')
        idF, itok = ldid(F32)
        emit_transpose_blocked(nc, ctx, s1, s1_tm, DM, Tn, F32, idF[:, :], itok)
        phase('p12')
        build_ln(Tn, DM, nc=nc, ctx=ctx, io={'x': s1_tm, 'out': x1_tm})
        phase('p13')
        idF, itok = ldid(F32)
        emit_transpose_blocked(nc, ctx, x1_tm, x1, Tn, DM, F32, idF[:, :], itok)
        phase('p14')
        build_ffn1(Tn=Tn, nc=nc, ctx=ctx, io={'x': x1, 'w': w1, 'bias': b1, 'out': g_s})
        phase('p15')
        build_linear(2048, DM, Tn, bias=False, x_hbm_dtype=BF16, nc=nc, ctx=ctx,
                     io={'x': g_s[0:2048, :], 'w': w2[0:2048, :], 'out': part})
        phase('p16')
        build_f2b(Tn=Tn, nc=nc, ctx=ctx, io={'x': g_s[2048:, :], 'w': w2[2048:, :],
                                             'bias': b2, 'resid': part, 'resid2': x1,
                                             'out': s2})
        phase('p17')
        idF, itok = ldid(F32)
        emit_transpose_blocked(nc, ctx, s2, s2_tm, DM, Tn, F32, idF[:, :], itok)
        phase('p18')
        build_ln(Tn, DM, nc=nc, ctx=ctx, io={'x': s2_tm, 'out': x2_tm})
        phase('p19')
        idF, itok = ldid(F32)
        emit_transpose_blocked(nc, ctx, x2_tm, x2, Tn, DM, F32, idF[:, :], itok)
        phase('p20')
        s_hp = ctx.sb('hp', [128, 3], F32)
        prevh = None
        for kk in range(DM // 128):
            lt = ctx.dma('y', 'hpl', s_hp[:, :], x2[kk * 128:(kk + 1) * 128, Tn - 3:Tn],
                         serial=True, waits=[prevh])
            prevh = ctx.dma('y', 'hps', halo_pub[kk * 128:(kk + 1) * 128, :],
                            s_hp[:, :], serial=True, waits=[lt])

    cc2 = None
    if seg is None:
        ctx.end_phase()
        ctx.pfx = 'p20b_'
        cc2 = emit_allgather(nc, ctx, 'halo', halo_pub, halo_ag, groups_b)

    if in_seg(3):
        ctx.end_phase()
        ctx.pfx = 'p20c_'
        s_hs = ctx.sb('hs', [128, per_b], F32)
        s_hj = ctx.sb('hj', [128, 3], F32)
        s_ha = ctx.sb('ha', [128, 3], F32)
        lhs = ctx.dma('y', 'hsl', s_hs[:, :], halo_sel[:, :])
        prevh = None
        for kk in range(DM // 128):
            z = ctx.op('v', lambda v: v.memset(s_ha[:, :], 0.0), waits=[prevh])
            tacc = z
            for j in range(per_b):
                lj = ctx.dma('y', 'hjl', s_hj[:, :],
                             halo_ag[j * DM + kk * 128:j * DM + (kk + 1) * 128, :],
                             serial=True, waits=[cc2, tacc])
                tacc = ctx.op('v', lambda v, j=j: v.scalar_tensor_tensor(
                    s_ha[:, :], s_hj[:, :], s_hs[:, j:j + 1], s_ha[:, :], ALU.mult,
                    ALU.add), waits=[lj, lhs])
            prevh = ctx.dma('y', 'hao', x2h[kk * 128:(kk + 1) * 128, 0:3], s_ha[:, :],
                            waits=[tacc], serial=True)
        ctx.dma('y', 'hax', x2h[:, 3:], x2[:, :])
        phase('p21')
        build_linear(DM, 6160, Tn + 3, bias=False, nc=nc, ctx=ctx,
                     io={'x': x2h, 'w': Wcat, 'out': proj})
        phase('p22')
        build_conv_silu(KEY_DIM, Tn=Tn, nc=nc, ctx=ctx, halo_mask=True,
                        io={'x_pad': proj[0:1024, :], 'w': convq_w, 'out': qc,
                            'hmask': hmask})
        phase('p23')
        build_conv_silu(KEY_DIM, Tn=Tn, nc=nc, ctx=ctx, halo_mask=True,
                        io={'x_pad': proj[1024:2048, :], 'w': convk_w, 'out': kc,
                            'hmask': hmask})
        phase('p24')
        build_conv_silu(VAL_DIM, Tn=Tn, nc=nc, ctx=ctx, halo_mask=True,
                        io={'x_pad': proj[2048:4096, :], 'w': convv_w, 'out': vc,
                            'hmask': hmask})
        phase('p25')
        idF, itok = ldid(F32)
        emit_transpose_blocked(nc, ctx, qc, q_tm, KEY_DIM, Tn, F32, idF[:, :], itok)
        phase('p25b')
        idF, itok = ldid(F32)
        emit_transpose_blocked(nc, ctx, kc, k_tm, KEY_DIM, Tn, F32, idF[:, :], itok)
        phase('p25c')
        idF, itok = ldid(F32)
        emit_transpose_blocked(nc, ctx, vc, v_tm, VAL_DIM, Tn, F32, idF[:, :], itok)
        phase('p26')
        build_l2norm_qk(Tn=Tn, nc=nc, ctx=ctx, io={'q_tm': q_tm, 'k_tm': k_tm,
                                                   'qn': qn, 'kn': kn})
        phase('p27')
        idB, itok = ldid(BF16)
        emit_transpose_blocked(nc, ctx, qn, qn_cm, Tn, KEY_DIM, BF16, idB[:, :], itok)
        phase('p27b')
        idB, itok = ldid(BF16)
        emit_transpose_blocked(nc, ctx, kn, kn_cm, Tn, KEY_DIM, BF16, idB[:, :], itok)
        phase('p28')
        build_ab_rows(Tn=Tn, ch=CH1, nc=nc, ctx=ctx, io={
            'a_raw': proj[4096:4104, 3:], 'b_raw': proj[4104:4112, 3:],
            'dt_bias': dt_bias1, 'A_log': A_log1, 'lg': lg_r, 'beta': beta_r})
        phase('p29')
        idF, itok = ldid(F32)
        emit_transpose_blocked(nc, ctx, lg_r, lg_tm, H1, Tn, F32, idF[:, :], itok)
        phase('p29b')
        idF, itok = ldid(F32)
        emit_transpose_blocked(nc, ctx, beta_r, beta_tm, H1, Tn, F32, idF[:, :], itok)
        phase('p30')
        build_delta_local(Tn=Tn, ch=CH1, nc=nc, ctx=ctx, io={
            'q_cm': qn_cm, 'k_cm': kn_cm, 'q_tm': qn, 'k_tm': kn, 'v_tm': v_tm,
            'lg_row': lg_r, 'lg_tm': lg_tm, 'beta_tm': beta_tm,
            'identF': identF, 'ident2': ident2, 'mask_incl': mask_i1,
            'mask_strict': mask_s1, 'ones1': ones1,
            'O_loc': O_loc, 'GT': GT_s, 'PLT': PLT_s,
            'ag_S': ag_d_in[:, 0:VD1], 'ag_PL': ag_d_in[:, VD1:VD1 + KD1]})
        phase('p33')
        idF, itok = ldid(F32)
        emit_transpose_blocked(nc, ctx, proj[4112:6160, 3:], g_tm_s, VAL_DIM, Tn, F32,
                               idF[:, :], itok)

    cc3 = None
    if seg is None:
        ctx.end_phase()
        ctx.pfx = 'p31_'
        cc3 = emit_allgather(nc, ctx, 'delta', ag_d_in, ag_d_out, groups_b)

    if in_seg(4):
        ctx.end_phase()
        ctx.pfx = 'p31b_'
        s_STd = ctx.sb('STd', [128, H1 * VD1], F32)
        s_SJd = ctx.sb('SJd', [128, VD1], F32)
        s_PJ = ctx.sb('PJ', [128, KD1], F32)
        s_md = ctx.sb('md', [128, per_b], F32)
        s_tmp2 = ctx.sb('tmp2', [128, VD1], F32)
        p_f = ctx.ps('pf', [128, VD1], F32)
        lmd = ctx.dma('y', 'md', s_md[:, :], mam_mask[:, :])
        zz = ctx.op('v', lambda v: v.memset(s_STd[:, :], 0.0))
        fold_tok = zz
        for j in range(per_b):
            for h in range(H1):
                r0 = j * H1 * KD1 + h * KD1
                lj = ctx.dma('y', 'fdj', s_SJd[:, :], ag_d_out[r0:r0 + KD1, 0:VD1],
                             serial=True, waits=[cc3, fold_tok])
                lp = ctx.dma('y', 'fdj', s_PJ[:, :],
                             ag_d_out[r0:r0 + KD1, VD1:VD1 + KD1], serial=True)
                mm = ctx.op('t', lambda te, h=h: te.matmul(
                    p_f[:, :], s_PJ[:, :], s_STd[:, h * VD1:(h + 1) * VD1],
                    start=True, stop=True), waits=[lp, fold_tok])
                t1 = ctx.op('v', lambda v, h=h: v.scalar_tensor_tensor(
                    s_tmp2[:, :], p_f[:, :], 0.0, s_SJd[:, :], ALU.add, ALU.add),
                    waits=[mm, lj])
                t2 = ctx.op('v', lambda v, h=h: v.tensor_tensor(
                    s_tmp2[:, :], s_tmp2[:, :], s_STd[:, h * VD1:(h + 1) * VD1],
                    ALU.subtract), waits=[t1])
                fold_tok = ctx.op('v', lambda v, h=h, j=j: v.scalar_tensor_tensor(
                    s_STd[:, h * VD1:(h + 1) * VD1], s_tmp2[:, :], s_md[:, j:j + 1],
                    s_STd[:, h * VD1:(h + 1) * VD1], ALU.mult, ALU.add),
                    waits=[t2, lmd])
        ctx.dma('y', 'fso', s_inT_d[:, :].rearrange('(h p) d -> p h d', p=128),
                s_STd[:, :].rearrange('p (h d) -> p h d', d=VD1), waits=[fold_tok])
        phase('p32')
        s_PLT = ctx.sb('cPLT', [128, 2 * KD1], F32)
        s_GTl = ctx.sb('cGT', [128, 2 * CH1], F32)
        s_Sc = ctx.sb('cS', [128, H1 * VD1], F32)
        s_tc = ctx.sb('ctmp', [128, VD1], F32)
        s_oc = ctx.sb('cO', [CH1, 2 * VD1], F32)
        p_c1 = ctx.ps('pc1', [128, VD1], F32)
        p_c2 = ctx.ps('pc2', [CH1, VD1], F32)
        lS = ctx.dma('y', 'cS', s_Sc[:, :].rearrange('p (h d) -> p h d', d=VD1),
                     s_inT_d[:, :].rearrange('(h p) d -> p h d', p=128))
        prevc = {}
        for u in range(nch1):
            for h in range(H1):
                par = (u * H1 + h) % 2
                r0 = (u * H1 + h) * KD1
                l1 = ctx.dma('g', 'cl%d' % par, s_PLT[:, par * KD1:(par + 1) * KD1],
                             PLT_s[r0:r0 + KD1, :], serial=True,
                             waits=[prevc.get(('m', par))])
                l2 = ctx.dma('g', 'cl%d' % par, s_GTl[:, par * CH1:(par + 1) * CH1],
                             GT_s[r0:r0 + KD1, :], serial=True)
                m1 = ctx.op('t', lambda te, par=par, h=h: te.matmul(
                    p_c1[:, :], s_PLT[:, par * KD1:(par + 1) * KD1],
                    s_Sc[:, h * VD1:(h + 1) * VD1], start=True, stop=True),
                    waits=[l1, lS, prevc.get('e1')])
                e1 = ctx.op('v', lambda v: v.tensor_copy(s_tc[:, :], p_c1[:, :]),
                            waits=[m1, prevc.get('m2')])
                prevc['e1'] = e1
                m2 = ctx.op('t', lambda te, par=par: te.matmul(
                    p_c2[:, :], s_GTl[:, par * CH1:(par + 1) * CH1], s_tc[:, :],
                    start=True, stop=True), waits=[l2, e1, prevc.get('e2')])
                prevc['m2'] = m2
                e2 = ctx.op('v', lambda v, par=par: v.tensor_copy(
                    s_oc[:, par * VD1:(par + 1) * VD1], p_c2[:, :]),
                    waits=[m2, ctx.dtok('co%d' % par) if ctx.has_dma('co%d' % par)
                           else None])
                prevc['e2'] = e2
                prevc[('m', par)] = e2
                ro = u * CH1
                ctx.dma('y', 'co%d' % par,
                        O_corr[ro:ro + CH1, h * VD1:(h + 1) * VD1],
                        s_oc[:, par * VD1:(par + 1) * VD1], waits=[e2], serial=True)
        phase('p34')
        build_gate_out(Tn=Tn, nc=nc, ctx=ctx, io={'O_loc': O_loc, 'O_corr': O_corr,
                                                  'g_tm': g_tm_s, 'og': og})
        phase('p35')
        idB, itok = ldid(BF16)
        emit_transpose_blocked(nc, ctx, og, ogT, Tn, VAL_DIM, BF16, idB[:, :], itok)
        phase('p36')
        build_linear(VAL_DIM, DM, Tn, residual=True, wscale=True, x_hbm_dtype=BF16,
                     nc=nc, ctx=ctx, io={'x': ogT, 'w': WoT, 'wscale': onw,
                                         'resid': x2, 'out': s3})
        phase('p37')
        idF, itok = ldid(F32)
        emit_transpose_blocked(nc, ctx, s3, s3_tm, DM, Tn, F32, idF[:, :], itok)
        phase('p38')
        build_ln(Tn, DM, nc=nc, ctx=ctx, io={'x': s3_tm, 'out': x3_tm})
        phase('p39')
        idF, itok = ldid(F32)
        emit_transpose_blocked(nc, ctx, x3_tm, x3, Tn, DM, F32, idF[:, :], itok)
        phase('p40')
        build_amax_part(DM, Tn=Tn, nc=nc, ctx=ctx, io={'x': x3, 'identF': identF,
                                                       'part': am_part})

    cc4 = None
    if seg is None:
        ctx.end_phase()
        ctx.pfx = 'p40b_'
        cc4 = emit_allgather(nc, ctx, 'amax', am_part, am_ag, groups_all)

    if in_seg(5):
        ctx.end_phase()
        ctx.pfx = 'p41_'
        build_ffn_fp8(Tn=Tn, nc=nc, ctx=ctx, io={
            'parts': am_ag.rearrange('a b -> (a b)').unsqueeze(0), 'x': x3, 'w': w1f,
            'bias': b1f, 'resid': x3, 'ones1': ones1, 'identF': identF, 'out': s4})
        phase('p42')
        idF, itok = ldid(F32)
        emit_transpose_blocked(nc, ctx, s4, s4_tm, DM, Tn, F32, idF[:, :], itok)
        phase('p43')
        build_ln(Tn, DM, nc=nc, ctx=ctx, io={'x': s4_tm, 'out': out_ap})
    ctx.end_phase()
    ctx.emit()
    return nc


def _input_names(nc):
    import concourse.mybir as _mb
    return [a.memorylocations[0].name for a in nc.m.functions[0].allocations
            if isinstance(a, _mb.MemoryLocationSet) and a.kind == 'ExternalInput'
            and a.memorylocations[0].name != 'partition_id']


def _output_names(nc):
    import concourse.mybir as _mb
    return [a.memorylocations[0].name for a in nc.m.functions[0].allocations
            if isinstance(a, _mb.MemoryLocationSet) and a.kind == 'ExternalOutput']


def run8(nc, in_maps):
    return run_bass_kernel_spmd(nc, in_maps, core_ids=list(range(NCORES)))


def asc(x):
    return np.ascontiguousarray(x)


def _cores():
    per_b = NCORES // B
    return [(c, c // per_b, (c % per_b) * T) for c in range(NCORES)]


def shard_cm(xg):
    return [asc(xg[b, :, t0:t0 + T]) for (_, b, t0) in _cores()]


def shard_tm(xg):
    return [asc(xg[b, t0:t0 + T, :]) for (_, b, t0) in _cores()]


def gather_cm(outs, C_):
    per_b = NCORES // B
    return np.stack([np.concatenate(outs[b * per_b:(b + 1) * per_b], axis=1)
                     for b in range(B)])


def gather_tm(outs, C_):
    per_b = NCORES // B
    return np.stack([np.concatenate(outs[b * per_b:(b + 1) * per_b], axis=0)
                     for b in range(B)])


def _run(nc, in_maps, out_names):
    res = run8(nc, in_maps)
    return [[np.asarray(r[nm]) for nm in out_names] for r in res.results]


def _forward(inputs, dbg=None):
    inp = {k: np.asarray(v, dtype=np.float32) if np.asarray(v).dtype == np.float32
           else np.asarray(v) for k, v in inputs.items()}
    nch_tot = L // CH0
    per_b = NCORES // B

    u = np.asarray(inp['L0_attn_u'], np.float32)
    uT = asc(u.transpose(0, 2, 1))
    W_inT = asc(np.asarray(inp['L0_attn_W_in'], np.float32).T)
    b_in = np.asarray(inp['L0_attn_b_in'], np.float32).reshape(-1, 1)

    nc = build_linear(DM, D_IN_PROJ, T, bias=True)
    outs = _run(nc, [{'x': xc, 'w': W_inT, 'bias': b_in} for xc in shard_cm(uT)], ['out'])
    zx = gather_cm([o[0] for o in outs], D_IN_PROJ)
    if dbg is not None:
        dbg['zx'] = zx

    xbc = zx[:, DSSM:DSSM + CONV_DIM, :]
    dt_raw = zx[:, DSSM + CONV_DIM:, :]
    conv_w = np.asarray(inp['L0_attn_conv_w'], np.float32)
    conv_b = np.asarray(inp['L0_attn_conv_b'], np.float32).reshape(-1, 1)
    dt_bias = np.asarray(inp['L0_attn_dt_bias'], np.float32).reshape(-1, 1)
    A_log = np.asarray(inp['L0_attn_A_log'], np.float32).reshape(-1, 1)
    xbc_pad_g = np.concatenate([np.zeros((B, CONV_DIM, 3), np.float32), xbc], axis=2)
    maps = []
    for (_, b, t0) in _cores():
        maps.append({'xbc_pad': asc(xbc_pad_g[b, :, t0:t0 + T + 3]),
                     'conv_w': conv_w, 'conv_b': conv_b,
                     'dt_raw': asc(dt_raw[b, :, t0:t0 + T]),
                     'dt_bias': dt_bias, 'A_log': A_log})
    nc = build_mamba_pre(Tn=T)
    outs = _run(nc, maps, ['convout', 'dt', 'cs', 'expcs', 'dsdt'])
    convout = gather_cm([o[0] for o in outs], CONV_DIM)
    dt_g = gather_cm([o[1] for o in outs], NH0)
    cs_g = gather_cm([o[2] for o in outs], NH0)
    expcs_g = gather_cm([o[3] for o in outs], NH0)
    dsdt_g = gather_cm([o[4] for o in outs], NH0)
    if dbg is not None:
        dbg.update(convout=convout, dt=dt_g, cs=cs_g, expcs=expcs_g, dsdt=dsdt_g)

    x_cm_g = convout[:, :DSSM, :]
    B_cm_g = convout[:, DSSM:DSSM + DST, :]
    C_cm_g = convout[:, DSSM + DST:, :]
    dt_tm_g = asc(dt_g.transpose(0, 2, 1))
    cs_tm_g = asc(cs_g.transpose(0, 2, 1))
    dsdt_tm_g = asc(dsdt_g.transpose(0, 2, 1))
    D_bc = np.broadcast_to(np.asarray(inp['L0_attn_D'], np.float32), (128, NH0)).copy()
    mask_ji = asc(np.tril(np.ones((CH0, CH0), np.float32)).T)
    ones1 = np.ones((1, 128), np.float32)
    ident = np.eye(128).astype(bf16)
    maps = []
    for (_, b, t0) in _cores():
        maps.append({'x_cm': asc(x_cm_g[b, :, t0:t0 + T]),
                     'B_cm': asc(B_cm_g[b, :, t0:t0 + T]),
                     'C_cm': asc(C_cm_g[b, :, t0:t0 + T]),
                     'dt_tm': asc(dt_tm_g[b, t0:t0 + T]),
                     'cs_row': asc(cs_g[b, :, t0:t0 + T]),
                     'cs_tm': asc(cs_tm_g[b, t0:t0 + T]),
                     'dsdt_tm': asc(dsdt_tm_g[b, t0:t0 + T]),
                     'D_bc': D_bc, 'mask': mask_ji, 'ones1': ones1, 'ident': ident})
    nc = build_ssd_local(Tn=T, ch=CH0)
    outs = _run(nc, maps, ['ydx', 'states'])
    ydx_g = gather_tm([o[0] for o in outs], DSSM)
    states_g = np.stack([np.concatenate(
        [outs[b * per_b + i][1].reshape(NCH0, DSSM, DST) for i in range(per_b)], axis=0)
        for b in range(B)])
    if dbg is not None:
        dbg['ydx'] = ydx_g
        dbg['states'] = states_g

    HQ = NH0 // (NCORES // B)
    rows = HQ * HD0
    decay = expcs_g[:, :, CH0 - 1::CH0]
    maps = []
    for c in range(NCORES):
        b, hq = c // per_b, c % per_b
        st = asc(states_g[b, :, hq * rows:(hq + 1) * rows, :])
        dcr = np.repeat(decay[b, hq * HQ:(hq + 1) * HQ, :].T, HD0, axis=1)
        dc = asc(dcr.reshape(nch_tot, rows // 128, 128).transpose(0, 2, 1))
        maps.append({'states': st, 'decay_bc': dc})
    nc = build_state_scan(nch_tot, rows)
    outs = _run(nc, maps, ['prev'])
    prev_g = np.stack([np.concatenate([outs[b * per_b + i][0] for i in range(per_b)],
                                      axis=1) for b in range(B)])
    prevT_g = asc(prev_g.transpose(0, 1, 3, 2))
    if dbg is not None:
        dbg['prev'] = prev_g

    expcs_tm_g = asc(expcs_g.transpose(0, 2, 1))
    z_tm_g = asc(zx[:, :DSSM, :].transpose(0, 2, 1))
    maps = []
    for (_, b, t0) in _cores():
        c0 = t0 // CH0
        maps.append({'ydx': asc(ydx_g[b, t0:t0 + T]),
                     'C_cm': asc(C_cm_g[b, :, t0:t0 + T]),
                     'prevT': asc(prevT_g[b, c0:c0 + NCH0].reshape(NCH0 * DST, DSSM)),
                     'expcs_tm': asc(expcs_tm_g[b, t0:t0 + T]),
                     'z_tm': asc(z_tm_g[b, t0:t0 + T])})
    nc = build_ssd_final(Tn=T, ch=CH0)
    outs = _run(nc, maps, ['yn'])
    yn_g = gather_tm([o[0] for o in outs], DSSM)
    if dbg is not None:
        dbg['yn'] = yn_g

    ynT = asc(yn_g.transpose(0, 2, 1))
    W_outT = asc(np.asarray(inp['L0_attn_W_out'], np.float32).T)
    b_out = np.asarray(inp['L0_attn_b_out'], np.float32).reshape(-1, 1)
    nw = np.asarray(inp['L0_attn_norm_w'], np.float32).reshape(-1, 1)
    maps = []
    for i, (_, b, t0) in enumerate(_cores()):
        maps.append({'x': asc(ynT[b, :, t0:t0 + T]), 'w': W_outT, 'bias': b_out,
                     'resid': asc(uT[b, :, t0:t0 + T]), 'wscale': nw})
    nc = build_linear(DSSM, DM, T, bias=True, residual=True, wscale=True,
                      x_hbm_dtype=BF16)
    outs = _run(nc, maps, ['out'])
    s1 = gather_cm([o[0] for o in outs], DM)
    if dbg is not None:
        dbg['s1'] = s1

    s1_tm = asc(s1.transpose(0, 2, 1))
    nc = build_ln(T, DM)
    outs = _run(nc, [{'x': xc} for xc in shard_tm(s1_tm)], ['out'])
    x1_tm = gather_tm([o[0] for o in outs], DM)
    if dbg is not None:
        dbg['x1'] = x1_tm

    x1T = asc(x1_tm.transpose(0, 2, 1))
    w1 = np.asarray(inp['L0_ffn_w1'], np.float32)
    b1 = np.asarray(inp['L0_ffn_b1'], np.float32).reshape(-1, 1)
    w2 = np.asarray(inp['L0_ffn_w2'], np.float32)
    b2 = np.asarray(inp['L0_ffn_b2'], np.float32).reshape(-1, 1)
    nc = build_ffn1(Tn=T)
    outs = _run(nc, [{'x': xc, 'w': w1, 'bias': b1} for xc in shard_cm(x1T)], ['out'])
    g_g = gather_cm([o[0] for o in outs], 4096)
    if dbg is not None:
        dbg['g'] = g_g

    nc = build_linear(2048, DM, T, bias=False, x_hbm_dtype=BF16)
    outs = _run(nc, [{'x': asc(g_g[b, :2048, t0:t0 + T]), 'w': asc(w2[:2048])}
                     for (_, b, t0) in _cores()], ['out'])
    part = gather_cm([o[0] for o in outs], DM)
    nc = build_f2b(Tn=T)
    outs = _run(nc, [{'x': asc(g_g[b, 2048:, t0:t0 + T]), 'w': asc(w2[2048:]),
                      'bias': b2, 'resid': asc(part[b, :, t0:t0 + T]),
                      'resid2': asc(x1T[b, :, t0:t0 + T])}
                     for (_, b, t0) in _cores()], ['out'])
    s2 = gather_cm([o[0] for o in outs], DM)
    s2_tm = asc(s2.transpose(0, 2, 1))
    nc = build_ln(T, DM)
    outs = _run(nc, [{'x': xc} for xc in shard_tm(s2_tm)], ['out'])
    x2_tm = gather_tm([o[0] for o in outs], DM)
    if dbg is not None:
        dbg['x2'] = x2_tm

    x2T = asc(x2_tm.transpose(0, 2, 1))
    Wq = np.asarray(inp['L1_attn_Wq'], np.float32)
    Wk = np.asarray(inp['L1_attn_Wk'], np.float32)
    Wv = np.asarray(inp['L1_attn_Wv'], np.float32)
    Wa = np.asarray(inp['L1_attn_Wa'], np.float32)
    Wb = np.asarray(inp['L1_attn_Wb'], np.float32)
    Wg = np.asarray(inp['L1_attn_Wg'], np.float32)
    Wcat = asc(np.concatenate([Wq, Wk, Wv, Wa, Wb, Wg], axis=1))
    nc = build_linear(DM, 6160, T)
    outs = _run(nc, [{'x': xc, 'w': Wcat} for xc in shard_cm(x2T)], ['out'])
    proj = gather_cm([o[0] for o in outs], 6160)
    q_raw = proj[:, 0:1024, :]
    k_raw = proj[:, 1024:2048, :]
    v_raw = proj[:, 2048:4096, :]
    a_raw = proj[:, 4096:4104, :]
    b_raw = proj[:, 4104:4112, :]
    g_raw = proj[:, 4112:6160, :]
    if dbg is not None:
        dbg['proj'] = proj

    convs = {}
    for nm, arr, wkey in [('q', q_raw, 'L1_attn_conv_q'), ('k', k_raw, 'L1_attn_conv_k'),
                          ('v', v_raw, 'L1_attn_conv_v')]:
        C_ = arr.shape[1]
        pad = np.concatenate([np.zeros((B, C_, 3), np.float32), arr], axis=2)
        wc = np.asarray(inp[wkey], np.float32)
        nc = build_conv_silu(C_, Tn=T)
        outs = _run(nc, [{'x_pad': asc(pad[b, :, t0:t0 + T + 3]), 'w': wc}
                         for (_, b, t0) in _cores()], ['out'])
        convs[nm] = gather_cm([o[0] for o in outs], C_)
    if dbg is not None:
        dbg.update(qc=convs['q'], kc=convs['k'], vc=convs['v'])

    q_tm = asc(convs['q'].transpose(0, 2, 1))
    k_tm = asc(convs['k'].transpose(0, 2, 1))
    nc = build_l2norm_qk(Tn=T)
    outs = _run(nc, [{'q_tm': asc(q_tm[b, t0:t0 + T]), 'k_tm': asc(k_tm[b, t0:t0 + T])}
                     for (_, b, t0) in _cores()], ['qn', 'kn'])
    qn_g = gather_tm([o[0] for o in outs], KEY_DIM)
    kn_g = gather_tm([o[1] for o in outs], KEY_DIM)
    if dbg is not None:
        dbg.update(qn=qn_g, kn=kn_g)

    dtb1 = np.asarray(inp['L1_attn_dt_bias'], np.float32).reshape(-1, 1)
    alog1 = np.asarray(inp['L1_attn_A_log'], np.float32).reshape(-1, 1)
    nc = build_ab_rows(Tn=T, ch=CH1)
    outs = _run(nc, [{'a_raw': asc(a_raw[b, :, t0:t0 + T]),
                      'b_raw': asc(b_raw[b, :, t0:t0 + T]),
                      'dt_bias': dtb1, 'A_log': alog1} for (_, b, t0) in _cores()],
                ['lg', 'beta'])
    lg_g = gather_cm([o[0] for o in outs], H1)
    beta_g = gather_cm([o[1] for o in outs], H1)
    if dbg is not None:
        dbg.update(lg=lg_g, beta=beta_g)

    qn_cm = asc(qn_g.transpose(0, 2, 1))
    kn_cm = asc(kn_g.transpose(0, 2, 1))
    v_tm_g = asc(convs['v'].transpose(0, 2, 1))
    lg_tm_g = asc(lg_g.transpose(0, 2, 1))
    beta_tm_g = asc(beta_g.transpose(0, 2, 1))
    identF = np.eye(128, dtype=np.float32)
    ident2 = (2 * np.eye(CH1)).astype(np.float32)
    mask_incl = asc(np.tril(np.ones((CH1, CH1), np.float32)).T)
    mask_strict = asc(np.tril(np.ones((CH1, CH1), np.float32), -1).T)
    ones1 = np.ones((1, 128), np.float32)
    nc = build_delta_local(Tn=T, ch=CH1)
    maps = []
    for (_, b, t0) in _cores():
        maps.append({'q_cm': asc(qn_cm[b, :, t0:t0 + T]), 'k_cm': asc(kn_cm[b, :, t0:t0 + T]),
                     'q_tm': asc(qn_g[b, t0:t0 + T]), 'k_tm': asc(kn_g[b, t0:t0 + T]),
                     'v_tm': asc(v_tm_g[b, t0:t0 + T]),
                     'lg_row': asc(lg_g[b, :, t0:t0 + T]),
                     'lg_tm': asc(lg_tm_g[b, t0:t0 + T]),
                     'beta_tm': asc(beta_tm_g[b, t0:t0 + T]),
                     'identF': identF, 'ident2': ident2, 'mask_incl': mask_incl,
                     'mask_strict': mask_strict, 'ones1': ones1})
    outs = _run(nc, maps, ['O_loc', 'Mt', 'SlocT', 'GT', 'gamc'])
    O_loc_g = gather_tm([o[0] for o in outs], VAL_DIM)
    nchg = L // CH1
    Mt_g = np.stack([np.concatenate(
        [outs[b * per_b + i][1].reshape(NCH1, H1, KD1, KD1) for i in range(per_b)], axis=0)
        for b in range(B)])
    Sl_g = np.stack([np.concatenate(
        [outs[b * per_b + i][2].reshape(NCH1, H1, KD1, VD1) for i in range(per_b)], axis=0)
        for b in range(B)])
    GT_g = np.stack([np.concatenate(
        [outs[b * per_b + i][3].reshape(NCH1, H1, KD1, CH1) for i in range(per_b)], axis=0)
        for b in range(B)])
    gamc_g = np.stack([np.concatenate(
        [outs[b * per_b + i][4] for i in range(per_b)], axis=0) for b in range(B)])
    if dbg is not None:
        dbg.update(O_loc=O_loc_g, Mt=Mt_g, Sl=Sl_g, GT=GT_g, gamc=gamc_g)

    units = [(b, h) for b in range(B) for h in range(H1)]
    upc = len(units) // NCORES
    maps = []
    for c in range(NCORES):
        us = units[c * upc:(c + 1) * upc]
        maps.append({
            'Mt': asc(np.concatenate([Mt_g[b, :, h] for (b, h) in us], axis=0)
                      ).reshape(upc * nchg * KD1, KD1),
            'SlocT': asc(np.concatenate([Sl_g[b, :, h] for (b, h) in us], axis=0)
                         ).reshape(upc * nchg * KD1, VD1),
            'GT': asc(np.concatenate([GT_g[b, :, h] for (b, h) in us], axis=0)
                      ).reshape(upc * nchg * KD1, CH1),
            'gam_bc': asc(np.broadcast_to(
                np.concatenate([gamc_g[b, :, h] for (b, h) in us])[:, None],
                (upc * nchg, 128)).copy()),
        })
    nc = build_delta_chain(upc, nchg, ch=CH1)
    outs = _run(nc, maps, ['O_corr'])
    O_corr_g = np.zeros((B, L, VAL_DIM), np.float32)
    for c in range(NCORES):
        oc = outs[c][0].reshape(upc, nchg, CH1, VD1)
        for iu, (b, h) in enumerate(units[c * upc:(c + 1) * upc]):
            O_corr_g[b, :, h * VD1:(h + 1) * VD1] = oc[iu].reshape(L, VD1)
    if dbg is not None:
        dbg['O_corr'] = O_corr_g

    g_tm_g = asc(g_raw.transpose(0, 2, 1))
    nc = build_gate_out(Tn=T)
    outs = _run(nc, [{'O_loc': asc(O_loc_g[b, t0:t0 + T]),
                      'O_corr': asc(O_corr_g[b, t0:t0 + T]),
                      'g_tm': asc(g_tm_g[b, t0:t0 + T])} for (_, b, t0) in _cores()],
                ['og'])
    og_g = gather_tm([o[0] for o in outs], VAL_DIM)
    if dbg is not None:
        dbg['og'] = og_g

    ogT = asc(og_g.transpose(0, 2, 1))
    WoT = asc(np.asarray(inp['L1_attn_Wo'], np.float32))
    onw = np.tile(np.asarray(inp['L1_attn_o_norm_w'], np.float32), H1).reshape(-1, 1)
    nc = build_linear(VAL_DIM, DM, T, residual=True, wscale=True, x_hbm_dtype=BF16)
    outs = _run(nc, [{'x': asc(ogT[b, :, t0:t0 + T]), 'w': WoT, 'wscale': onw,
                      'resid': asc(x2T[b, :, t0:t0 + T])} for (_, b, t0) in _cores()],
                ['out'])
    s3 = gather_cm([o[0] for o in outs], DM)
    s3_tm = asc(s3.transpose(0, 2, 1))
    nc = build_ln(T, DM)
    outs = _run(nc, [{'x': xc} for xc in shard_tm(s3_tm)], ['out'])
    x3_tm = gather_tm([o[0] for o in outs], DM)
    if dbg is not None:
        dbg['x3'] = x3_tm

    x3T = asc(x3_tm.transpose(0, 2, 1))
    nc = build_amax_part(DM, Tn=T)
    outs = _run(nc, [{'x': asc(x3T[b, :, t0:t0 + T]), 'identF': identF}
                     for (_, b, t0) in _cores()], ['part'])
    parts = asc(np.stack([o[0][0, 0] for o in outs])[None, :])
    w1f = asc(np.asarray(inp['L1_ffn_weight'], np.float32).T)
    b1f = np.asarray(inp['L1_ffn_bias'], np.float32).reshape(-1, 1)
    nc = build_ffn_fp8(Tn=T)
    outs = _run(nc, [{'parts': parts, 'x': asc(x3T[b, :, t0:t0 + T]), 'w': w1f,
                      'bias': b1f, 'resid': asc(x3T[b, :, t0:t0 + T]),
                      'ones1': ones1, 'identF': identF} for (_, b, t0) in _cores()],
                ['out'])
    s4 = gather_cm([o[0] for o in outs], DM)
    s4_tm = asc(s4.transpose(0, 2, 1))
    nc = build_ln(T, DM)
    outs = _run(nc, [{'x': xc} for xc in shard_tm(s4_tm)], ['out'])
    x4_tm = gather_tm([o[0] for o in outs], DM)
    return x4_tm


FUSED = True
FUSED_MODE = '1'


def _forward_fused(inputs, mode='5'):
    inp = {k: np.asarray(v) for k, v in inputs.items()}
    f32 = lambda k: np.asarray(inp[k], np.float32)
    per_b = NCORES // B

    u = f32('L0_attn_u')
    uT = asc(u.transpose(0, 2, 1))
    u_pad = np.concatenate([np.zeros((B, DM, 3), np.float32), uT], axis=2)

    common = {
        'W_inT': asc(f32('L0_attn_W_in').T),
        'b_in': f32('L0_attn_b_in').reshape(-1, 1),
        'conv_w': f32('L0_attn_conv_w'),
        'conv_b': f32('L0_attn_conv_b').reshape(-1, 1),
        'dt_bias': f32('L0_attn_dt_bias').reshape(-1, 1),
        'A_log': f32('L0_attn_A_log').reshape(-1, 1),
        'D_bc': np.broadcast_to(f32('L0_attn_D'), (128, NH0)).copy(),
        'mask0': asc(np.tril(np.ones((CH0, CH0), np.float32)).T),
        'ones1': np.ones((1, 128), np.float32),
        'identB': np.eye(128).astype(bf16),
        'identF': np.eye(128, dtype=np.float32),
        'norm_w': f32('L0_attn_norm_w').reshape(-1, 1),
        'W_outT': asc(f32('L0_attn_W_out').T),
        'b_out': f32('L0_attn_b_out').reshape(-1, 1),
        'w1': f32('L0_ffn_w1'),
        'b1': f32('L0_ffn_b1').reshape(-1, 1),
        'w2': f32('L0_ffn_w2'),
        'b2': f32('L0_ffn_b2').reshape(-1, 1),
        'Wcat': asc(np.concatenate([f32('L1_attn_Wq'), f32('L1_attn_Wk'),
                                    f32('L1_attn_Wv'), f32('L1_attn_Wa'),
                                    f32('L1_attn_Wb'), f32('L1_attn_Wg')], axis=1)),
        'convq_w': f32('L1_attn_conv_q'),
        'convk_w': f32('L1_attn_conv_k'),
        'convv_w': f32('L1_attn_conv_v'),
        'dt_bias1': f32('L1_attn_dt_bias').reshape(-1, 1),
        'A_log1': f32('L1_attn_A_log').reshape(-1, 1),
        'ident2': (2 * np.eye(CH1)).astype(np.float32),
        'mask_i1': asc(np.tril(np.ones((CH1, CH1), np.float32)).T),
        'mask_s1': asc(np.tril(np.ones((CH1, CH1), np.float32), -1).T),
        'onw': np.tile(f32('L1_attn_o_norm_w'), H1).reshape(-1, 1),
        'WoT': f32('L1_attn_Wo'),
        'w1f': asc(f32('L1_ffn_weight').T),
        'b1f': f32('L1_ffn_bias').reshape(-1, 1),
    }

    pool = []
    for (c, b, t0) in _cores():
        blk = c % per_b
        m = dict(common)
        if blk == 0:
            m['u_hal'] = asc(u_pad[b, :, t0:t0 + T + 3])
            m['hmask'] = np.zeros((128, 1), np.float32)
        else:
            m['u_hal'] = asc(uT[b, :, t0 - 3:t0 + T])
            m['hmask'] = np.ones((128, 1), np.float32)
        mm = np.zeros((128, per_b), np.float32)
        mm[:, :blk] = 1.0
        m['mam_mask'] = mm
        hs = np.zeros((128, per_b), np.float32)
        if blk > 0:
            hs[:, blk - 1] = 1.0
        m['halo_sel'] = hs
        pool.append(m)

    if mode == '1':
        nc = build_fused()
        res = run8(nc, [{n: pool[c][n] for n in _input_names(nc)}
                        for c in range(NCORES)])
        outs = res.results
    else:
        gathers = {
            1: [('ag_m_out', 'ag_m_in', 'group')],
            2: [('halo_ag', 'halo_pub', 'group')],
            3: [('ag_d_out', 'ag_d_in', 'group')],
            4: [('am_ag', 'am_part', 'all')],
        }
        for s in range(1, 6):
            nc = build_fused(seg=s)
            innames = _input_names(nc)
            outnames = _output_names(nc)
            res = run8(nc, [{n: pool[c][n] for n in innames} for c in range(NCORES)])
            for c in range(NCORES):
                for n in outnames:
                    pool[c][n] = np.asarray(res.results[c][n])
            for (dst, srcn, scope) in gathers.get(s, []):
                if scope == 'group':
                    for b in range(B):
                        cat = asc(np.concatenate(
                            [pool[b * per_b + i][srcn] for i in range(per_b)], axis=0))
                        for i in range(per_b):
                            pool[b * per_b + i][dst] = cat
                else:
                    cat = asc(np.stack([pool[c][srcn][0, 0] for c in range(NCORES)]
                                       )[:, None])
                    for c in range(NCORES):
                        pool[c][dst] = cat
        outs = [pool[c] for c in range(NCORES)]

    out = np.stack([np.concatenate(
        [np.asarray(outs[b * per_b + i]['out']) for i in range(per_b)], axis=0)
        for b in range(B)])
    return out


def kernel(**inputs):
    if FUSED:
        return _forward_fused(inputs, mode=FUSED_MODE)
    return _forward(inputs)


# version 13 (final)
# speedup vs baseline: 1.0547x; 1.0547x over previous
import numpy as np
import ml_dtypes
import concourse.bass as bass
import concourse.mybir as mybir
from concourse.bass_utils import run_bass_kernel_spmd

F32 = mybir.dt.float32
BF16 = mybir.dt.bfloat16
F8 = mybir.dt.float8e4
I32 = mybir.dt.int32
AF = mybir.ActivationFunctionType
ALU = mybir.AluOpType
AXL = mybir.AxisListType

NCORES = 8
B = 2
L = 8192
DM = 1024
T = 2048
MAGIC = 1.5 * 2.0 ** 23

DSSM = 2048
DST = 128
NH0 = 32
HD0 = 64
CONV_DIM = DSSM + 2 * DST
D_IN_PROJ = 2 * DSSM + 2 * DST + NH0
CH0 = 256
NCH0 = T // CH0

H1 = 8
KD1 = 128
VD1 = 256
KEY_DIM = H1 * KD1
VAL_DIM = H1 * VD1
CH1 = 64
NCH1 = T // CH1

bf16 = ml_dtypes.bfloat16


def cdiv(a, b):
    return (a + b - 1) // b


class Ctx:

    ENGS = ('v', 's', 't', 'g', 'y')
    PHASE_ATTRS = ('_pv', '_gelu_scr', '_gelu_guard', '_g16cc', '_scr', '_g', '_r2use0',
                   '_r2use1', '_sres2', '_r2ap', '_hl', '_po', '_pc', '_last', '_so2')

    def __init__(self, nc):
        self.nc = nc
        self.sem = {e: nc.alloc_semaphore('ctr_' + e) for e in self.ENGS}
        self.cnt = {e: 0 for e in self.ENGS}
        self.script = {e: [] for e in self.ENGS}
        self.dsem = {}
        self.dcnt = {}
        import contextlib
        self._stack = contextlib.ExitStack()
        self.pfx = ''

    def sb(self, name, shape, dtype):
        return self._stack.enter_context(self.nc.sbuf_tensor(self.pfx + name, shape, dtype))

    def ps(self, name, shape, dtype):
        return self._stack.enter_context(self.nc.psum_tensor(self.pfx + name, shape, dtype))

    def barrier(self):
        toks = [(self.sem[e], self.cnt[e]) for e in self.ENGS if self.cnt[e] > 0]
        toks += [(self.dsem[n], self.dcnt[n]) for n in self.dsem]
        for e in self.ENGS:
            def fn(eng, toks=toks):
                for (sem, val) in toks:
                    eng.wait_ge(sem, val)
            self.script[e].append(fn)

    def end_phase(self):
        self.barrier()
        self._stack.close()
        import contextlib
        self._stack = contextlib.ExitStack()
        for a in self.PHASE_ATTRS:
            if hasattr(self, a):
                delattr(self, a)

    def has_dma(self, name):
        return ('g.' + name) in self.dsem or ('h.' + name) in self.dsem

    def _dma_sem(self, name):
        if name not in self.dsem:
            self.dsem[name] = self.nc.alloc_semaphore('dma_' + name)
            self.dcnt[name] = 0
        return self.dsem[name]

    def _wrap(self, eng_key, emit, waits, inc):
        def fn(eng):
            for (sem, val) in waits:
                eng.wait_ge(sem, val)
            instr = emit(eng)
            if instr is not None and inc:
                instr.then_inc(self.sem[eng_key], 1)
        return fn

    def op(self, e, emit, waits=(), inc=True):
        wl = [w for w in waits if w is not None]
        self.script[e].append(self._wrap(e, emit, wl, inc))
        if inc:
            self.cnt[e] += 1
        return (self.sem[e], self.cnt[e])

    def dma(self, e, sem_name, out_ap, in_ap, waits=(), serial=False):
        sem_name = ('g.' if e == 'g' else 'h.') + sem_name
        sem = self._dma_sem(sem_name)
        wl = [w for w in waits if w is not None]
        if serial and self.dcnt[sem_name] > 0:
            wl = wl + [(sem, self.dcnt[sem_name])]
        def emit(eng):
            eng.dma_start(out=out_ap, in_=in_ap).then_inc(sem, 16)
            return None
        self.script[e].append(self._wrap(e, emit, wl, False))
        self.dcnt[sem_name] += 16
        return (sem, self.dcnt[sem_name])

    def dtok(self, name):
        if name in self.dsem:
            return (self.dsem[name], self.dcnt[name])
        g, h = 'g.' + name in self.dsem, 'h.' + name in self.dsem
        if g and h:
            raise KeyError('ambiguous dma sem ' + name)
        key = ('g.' if g else 'h.') + name
        return (self.dsem[key], self.dcnt[key])

    def tok(self, e):
        return (self.sem[e], self.cnt[e])

    def finish(self):
        for name in self.dsem:
            sem, cnt = self.dsem[name], self.dcnt[name]
            self.script['y'].append(lambda eng, s=sem, c=cnt: eng.wait_ge(s, c))

    def const_col(self, value, name=None):
        t = self.sb(name or ('c_%d' % len(self.script['v'])), [128, 1], F32)
        tok = self.op('v', lambda v: v.memset(t[:], value))
        return t, tok

    def emit(self):
        self.finish()
        with self.nc.Block() as block:
            for e, attr in (('y', 'sync'), ('g', 'gpsimd'), ('v', 'vector'),
                            ('s', 'scalar'), ('t', 'tensor')):
                fns = self.script[e]
                if not fns:
                    continue
                def body(eng, fns=fns):
                    for fn in fns:
                        fn(eng)
                getattr(block, attr)(body)



def build_linear(K, OD, Tn, bias=False, residual=False, wscale=False,
                 out_dtype=F32, x_hbm_dtype=F32, res_dtype=F32,
                 stage_dtype=None, post_od=None, residual2=False,
                 nc=None, ctx=None, io=None):
    own = nc is None
    if own:
        nc = bass.Bass('TRN2', target_bir_lowering=False)
    if ctx is None:
        ctx = Ctx(nc)
    x = io['x'] if io is not None else nc.dram_tensor('x', [K, Tn], x_hbm_dtype, kind='ExternalInput').ap()
    w = io['w'] if io is not None else nc.dram_tensor('w', [K, OD], F32, kind='ExternalInput').ap()
    b_ap = (io['bias'] if io is not None else nc.dram_tensor('bias', [OD, 1], F32, kind='ExternalInput').ap()) if bias else None
    r_ap = (io['resid'] if io is not None else nc.dram_tensor('resid', [OD, Tn], res_dtype, kind='ExternalInput').ap()) if residual else None
    ws_ap = (io['wscale'] if io is not None else nc.dram_tensor('wscale', [K, 1], F32, kind='ExternalInput').ap()) if wscale else None
    r2_ap = (io['resid2'] if io is not None else nc.dram_tensor('resid2', [OD, Tn], F32, kind='ExternalInput').ap()) if residual2 else None
    out = io['out'] if io is not None else nc.dram_tensor('out', [OD, Tn], out_dtype, kind='ExternalOutput').ap()

    nk = K // 128
    od_tiles = []
    od0 = 0
    while od0 < OD:
        odp = min(128, OD - od0)
        od_tiles.append((od0, odp))
        od0 += odp
    TOKBLK = 512
    ntok = cdiv(Tn, TOKBLK)

    sx = ctx.sb('sx', [128, nk * Tn], BF16)
    sw = ctx.sb('sw', [128, nk * OD], BF16)
    sbias = ctx.sb('sbias', [128, len(od_tiles)], F32) if bias else None
    sws = ctx.sb('sws', [128, nk], F32) if wscale else None
    sres = ctx.sb('sres', [128, nk_res * Tn], res_dtype) \
        if residual and (nk_res := len(od_tiles)) else None
    sres2 = ctx.sb('sres2', [128, 2 * Tn], F32) if residual2 else None
    sout = ctx.sb('sout', [128, 2 * Tn], stage_dtype or out_dtype)
    pz = [ctx.ps('pz%d' % i, [128, TOKBLK], F32) for i in range(2)]


    for k in range(nk):
        ctx.dma('g', 'ld', sx[:, k * Tn:(k + 1) * Tn], x[k * 128:(k + 1) * 128, :])
    for k in range(nk):
        ctx.dma('g', 'ld', sw[:, k * OD:(k + 1) * OD], w[k * 128:(k + 1) * 128, :])
    if bias:
        for i, (o0, op_) in enumerate(od_tiles):
            ctx.dma('y', 'ldh', sbias[0:op_, i:i + 1], b_ap[o0:o0 + op_, :])
    if wscale:
        for k in range(nk):
            ctx.dma('y', 'ldh', sws[:, k:k + 1], ws_ap[k * 128:(k + 1) * 128, :])
    if residual:
        for i, (o0, op_) in enumerate(od_tiles):
            ctx.dma('y', 'ldh', sres[0:op_, i * Tn:(i + 1) * Tn], r_ap[o0:o0 + op_, :])

    gl = ctx.dtok('ld')
    hl = ctx.dtok('ldh') if ctx.has_dma('ldh') else None

    wtok = None
    if wscale:
        for k in range(nk):
            wtok = ctx.op('v', lambda v, k=k: v.tensor_scalar(
                sw[:, k * OD:(k + 1) * OD], sw[:, k * OD:(k + 1) * OD],
                sws[:, k:k + 1], None, ALU.mult), waits=[gl, hl])

    prev_evac = {}
    store_cnt = {0: 0, 1: 0}
    for i, (o0, odp) in enumerate(od_tiles):
        par = i % 2
        ot = sout[0:odp, par * Tn:par * Tn + Tn]
        last_ev = None
        for j in range(ntok):
            t0 = j * TOKBLK
            tn = min(TOKBLK, Tn - t0)
            bank = (i * ntok + j) % 2
            ps = pz[bank]
            for k in range(nk):
                def mm(te, k=k, o0=o0, odp=odp, t0=t0, tn=tn, ps=ps):
                    return te.matmul(ps[0:odp, 0:tn],
                                     sw[:, k * OD + o0:k * OD + o0 + odp],
                                     sx[:, k * Tn + t0:k * Tn + t0 + tn],
                                     start=(k == 0), stop=(k == nk - 1))
                w8 = [gl, wtok, prev_evac.get(bank)] if k == 0 else []
                tk = ctx.op('t', mm, waits=w8)
            if False:
                pass
            else:
                def ev_fn(v, i=i, odp=odp, t0=t0, tn=tn, ps=ps, ot=ot):
                    if residual:
                        return v.scalar_tensor_tensor(
                            ot[:, t0:t0 + tn], ps[0:odp, 0:tn],
                            sbias[0:odp, i:i + 1] if bias else 0.0,
                            sres[0:odp, i * Tn + t0:i * Tn + t0 + tn],
                            ALU.add, ALU.add)
                    elif bias:
                        return v.tensor_scalar(ot[:, t0:t0 + tn], ps[0:odp, 0:tn],
                                               sbias[0:odp, i:i + 1], None, ALU.add)
                    else:
                        return v.tensor_copy(ot[:, t0:t0 + tn], ps[0:odp, 0:tn])
                wpar = ctx.dtok('st%d' % par) if ctx.has_dma('st%d' % par) else None
                ev = ctx.op('v', ev_fn, waits=[tk, hl, wpar])
            prev_evac[bank] = ev
            last_ev = ev
        if post_od is not None:
            ctx._sres2 = sres2
            ctx._r2ap = r2_ap
            ctx._hl = hl
            src_ap, last_ev = post_od(nc, ctx, i, o0, odp, ot, par, last_ev)
        else:
            src_ap = ot
        ctx.dma('y', 'st%d' % par, out[o0:o0 + odp, :], src_ap, waits=[last_ev],
                serial=True)

    if own:
        ctx.emit()
    return nc



def build_ln(R, D, eps=1e-5, out_dtype=F32, nc=None, ctx=None, io=None):
    own = nc is None
    if own:
        nc = bass.Bass('TRN2', target_bir_lowering=False)
    if ctx is None:
        ctx = Ctx(nc)
    x = io['x'] if io is not None else nc.dram_tensor('x', [R, D], F32, kind='ExternalInput').ap()
    out = io['out'] if io is not None else nc.dram_tensor('out', [R, D], out_dtype, kind='ExternalOutput').ap()
    nr = R // 128
    sx = ctx.sb('sx', [128, 2 * D], F32)
    sq = ctx.sb('sq', [128, 2 * D], F32)
    sm = ctx.sb('sm', [128, 2 * 8], F32)
    so = ctx.sb('so', [128, 2 * D], out_dtype)
    eps_ap, eps_tok = ctx.const_col(eps, 'eps')
    prev_use = {}
    for r in range(nr):
        par = r % 2
        xt = sx[:, par * D:(par + 1) * D]
        qt = sq[:, par * D:(par + 1) * D]
        mt = sm[:, par * 8:(par + 1) * 8]
        otile = so[:, par * D:(par + 1) * D]
        lt = ctx.dma('y', 'x%d' % par, xt, x[r * 128:(r + 1) * 128, :],
                     waits=[prev_use.get(par)], serial=True)
        t1 = ctx.op('v', lambda v, xt=xt, mt=mt: v.reduce_sum(mt[:, 0:1], xt, axis=AXL.X),
                    waits=[lt])
        t2 = ctx.op('s', lambda s, xt=xt, qt=qt, mt=mt: s.activation(
            qt, xt, AF.Square, accum_out=mt[:, 1:2]), waits=[lt])
        t3 = ctx.op('v', lambda v, mt=mt: v.tensor_scalar(mt[:, 0:1], mt[:, 0:1],
                                                          1.0 / D, None, ALU.mult),
                    waits=[t1])
        t4 = ctx.op('v', lambda v, mt=mt: v.tensor_tensor(mt[:, 2:3], mt[:, 0:1],
                                                          mt[:, 0:1], ALU.mult), waits=[t3])
        t5 = ctx.op('v', lambda v, mt=mt: v.tensor_scalar(mt[:, 1:2], mt[:, 1:2],
                                                          1.0 / D, None, ALU.mult),
                    waits=[t2])
        t6 = ctx.op('v', lambda v, mt=mt: v.tensor_tensor(mt[:, 3:4], mt[:, 1:2],
                                                          mt[:, 2:3], ALU.subtract),
                    waits=[t4, t5])
        t7 = ctx.op('s', lambda s, mt=mt: s.activation(mt[:, 4:5], mt[:, 3:4], AF.Sqrt,
                                                       bias=eps_ap[:, 0:1]), waits=[t6, eps_tok])
        t8 = ctx.op('v', lambda v, mt=mt: v.reciprocal(mt[:, 5:6], mt[:, 4:5]), waits=[t7])
        stw = ctx.dtok('o%d' % par) if ctx.has_dma('o%d' % par) else None
        t9 = ctx.op('v', lambda v, xt=xt, otile=otile, mt=mt: v.tensor_scalar(
            otile, xt, mt[:, 0:1], mt[:, 5:6], ALU.subtract, ALU.mult),
            waits=[t8, stw])
        prev_use[par] = t9
        ctx.dma('y', 'o%d' % par, out[r * 128:(r + 1) * 128, :], otile,
                waits=[t9], serial=True)
    if own:
        ctx.emit()
    return nc



def build_mamba_pre(Tn=T, nch=None, ch=CH0, nc=None, ctx=None, io=None, halo_mask=False):
    if nch is None:
        nch = Tn // ch
    own = nc is None
    if own:
        nc = bass.Bass('TRN2', target_bir_lowering=False)
    if ctx is None:
        ctx = Ctx(nc)
    xbc = io['xbc_pad'] if io is not None else nc.dram_tensor('xbc_pad', [CONV_DIM, Tn + 3], F32, kind='ExternalInput').ap()
    cw = io['conv_w'] if io is not None else nc.dram_tensor('conv_w', [CONV_DIM, 4], F32, kind='ExternalInput').ap()
    cb = io['conv_b'] if io is not None else nc.dram_tensor('conv_b', [CONV_DIM, 1], F32, kind='ExternalInput').ap()
    dtr = io['dt_raw'] if io is not None else nc.dram_tensor('dt_raw', [NH0, Tn], F32, kind='ExternalInput').ap()
    dtb = io['dt_bias'] if io is not None else nc.dram_tensor('dt_bias', [NH0, 1], F32, kind='ExternalInput').ap()
    alog = io['A_log'] if io is not None else nc.dram_tensor('A_log', [NH0, 1], F32, kind='ExternalInput').ap()
    convout = io['convout'] if io is not None else nc.dram_tensor('convout', [CONV_DIM, Tn], F32, kind='ExternalOutput').ap()
    dt_o = io['dt'] if io is not None else nc.dram_tensor('dt', [NH0, Tn], F32, kind='ExternalOutput').ap()
    cs_o = io['cs'] if io is not None else nc.dram_tensor('cs', [NH0, Tn], F32, kind='ExternalOutput').ap()
    expcs_o = io['expcs'] if io is not None else nc.dram_tensor('expcs', [NH0, Tn], F32, kind='ExternalOutput').ap()
    dsdt_o = io['dsdt'] if io is not None else nc.dram_tensor('dsdt', [NH0, Tn], F32, kind='ExternalOutput').ap()

    cslast_h = io.get('cslast') if io is not None else None
    hm_ap = io.get('hmask') if io is not None else None
    nct = CONV_DIM // 128
    sx = ctx.sb('sx', [128, 2 * (Tn + 3)], F32)
    swc = ctx.sb('swc', [128, nct * 4], F32)
    sbc = ctx.sb('sbc', [128, nct], F32)
    sacc = ctx.sb('sacc', [128, 2 * Tn], F32)
    sco = ctx.sb('sco', [128, 2 * Tn], F32)
    sdt = ctx.sb('sdt', [NH0, 8 * Tn], F32)
    scol = ctx.sb('scol', [NH0, 4], F32)

    one_ap, one_tok = ctx.const_col(1.0, 'one')

    U = sdt[:, 0 * Tn:1 * Tn]
    AB = sdt[:, 1 * Tn:2 * Tn]
    DT = sdt[:, 2 * Tn:3 * Tn]
    CS = sdt[:, 3 * Tn:4 * Tn]
    EC = sdt[:, 4 * Tn:5 * Tn]
    DS = sdt[:, 5 * Tn:6 * Tn]
    TMP = sdt[:, 6 * Tn:7 * Tn]
    ZERO = sdt[:, 7 * Tn:8 * Tn]

    for i in range(nct):
        ctx.dma('y', 'cw', swc[:, 4 * i:4 * i + 4], cw[i * 128:(i + 1) * 128, :])
        ctx.dma('y', 'cw', sbc[:, i:i + 1], cb[i * 128:(i + 1) * 128, :])
    ctx.dma('y', 'dt', sdt[:, 0:Tn], dtr[:, :])
    ctx.dma('y', 'dt', scol[:, 0:1], dtb[:, :])
    ctx.dma('y', 'dt', scol[:, 1:2], alog[:, :])
    shm = ctx.sb('shm', [128, 1], F32) if hm_ap is not None else None
    if hm_ap is not None:
        ctx.dma('y', 'cw', shm[:, :], hm_ap[:, :])
    cwl = ctx.dtok('cw')
    dtl = ctx.dtok('dt')

    z0 = ctx.op('v', lambda v: v.memset(ZERO, 0.0))
    t = ctx.op('v', lambda v: v.tensor_scalar(U, U, scol[:, 0:1], None, ALU.add),
               waits=[dtl])
    t = ctx.op('s', lambda s: s.activation(AB, U, AF.Abs), waits=[t])
    t = ctx.op('s', lambda s: s.activation(AB, AB, AF.Exp, scale=-1.0), waits=[t])
    t = ctx.op('s', lambda s: s.activation(AB, AB, AF.Ln, bias=one_ap[0:NH0, 0:1]), waits=[t, one_tok])
    t2 = ctx.op('v', lambda v: v.tensor_scalar(TMP, U, 0.0, None, ALU.max), waits=[t])
    tdt = ctx.op('v', lambda v: v.tensor_tensor(DT, AB, TMP, ALU.add), waits=[t, t2])
    t = ctx.op('s', lambda s: s.activation(scol[:, 2:3], scol[:, 1:2], AF.Exp), waits=[dtl])
    t = ctx.op('v', lambda v: v.tensor_scalar(scol[:, 2:3], scol[:, 2:3], -1.0, None,
                                              ALU.mult), waits=[t])
    tda = ctx.op('v', lambda v: v.tensor_scalar(TMP, DT, scol[:, 2:3], None, ALU.mult),
                 waits=[t, tdt])
    tcs = None
    for c in range(nch):
        sl = slice(c * ch, (c + 1) * ch)
        tcs = ctx.op('v', lambda v, sl=sl: v.tensor_tensor_scan(
            CS[:, sl], TMP[:, sl], ZERO[:, sl], 0.0, ALU.add, ALU.add),
            waits=[tda, z0])
    tec = ctx.op('s', lambda s: s.activation(EC, CS, AF.Exp), waits=[tcs])
    tds = None
    for c in range(nch):
        sl = slice(c * ch, (c + 1) * ch)
        last = slice((c + 1) * ch - 1, (c + 1) * ch)
        tt = ctx.op('s', lambda s, sl=sl, last=last: s.activation(
            DS[:, sl], CS[:, sl], AF.Exp, scale=-1.0, bias=CS[:, last]), waits=[tcs])
        tds = tt
    tds2 = ctx.op('v', lambda v: v.tensor_tensor(DS, DS, DT, ALU.mult), waits=[tds, tdt])
    ctx.dma('y', 'odt', dt_o[:, :], DT, waits=[tdt])
    ctx.dma('y', 'ocs', cs_o[:, :], CS, waits=[tcs])
    if cslast_h is not None:
        tcl = ctx.op('v', lambda v: v.tensor_copy(
            TMP[:, 0:nch],
            CS.rearrange('h (c w) -> h c w', w=ch)[:, :, ch - 1]), waits=[tcs])
        ctx.dma('y', 'ocs', cslast_h[:, :], TMP[:, 0:nch], waits=[tcl])
    ctx.dma('y', 'oec', expcs_o[:, :], EC, waits=[tec])
    ctx.dma('y', 'ods', dsdt_o[:, :], DS, waits=[tds2])

    prev_use = {}
    for i in range(nct):
        par = i % 2
        xt = sx[:, par * (Tn + 3):par * (Tn + 3) + Tn + 3]
        at = sacc[:, par * Tn:(par + 1) * Tn]
        ct = sco[:, par * Tn:(par + 1) * Tn]
        stw = ctx.dtok('co%d' % par) if ctx.has_dma('co%d' % par) else None
        lt = ctx.dma('g', 'x%d' % par, xt, xbc[i * 128:(i + 1) * 128, :], serial=True,
                     waits=[prev_use.get(par)])
        if hm_ap is not None:
            lt = ctx.op('v', lambda v, xt=xt: v.tensor_scalar(
                xt[:, 0:3], xt[:, 0:3], shm[:, 0:1], None, ALU.mult), waits=[lt, cwl])
        t = ctx.op('v', lambda v, xt=xt, at=at, i=i: v.tensor_scalar(
            at, xt[:, 0:Tn], swc[:, 4 * i:4 * i + 1], sbc[:, i:i + 1],
            ALU.mult, ALU.add), waits=[lt, cwl, prev_use.get(('silu', par))])
        for k in range(1, 4):
            t = ctx.op('v', lambda v, xt=xt, at=at, i=i, k=k: v.scalar_tensor_tensor(
                at, xt[:, k:k + Tn], swc[:, 4 * i + k:4 * i + k + 1], at,
                ALU.mult, ALU.add), waits=[t])
        prev_use[par] = t
        t = ctx.op('s', lambda s, at=at, ct=ct: s.activation(ct, at, AF.Silu),
                   waits=[t, stw])
        prev_use[('silu', par)] = t
        ctx.dma('y', 'co%d' % par, convout[i * 128:(i + 1) * 128, :], ct,
                waits=[t], serial=True)

    if own:
        ctx.emit()
    return nc



def build_ssd_local(Tn=T, ch=CH0, nc=None, ctx=None, io=None):
    nch = Tn // ch
    njt = ch // 128
    ncx = DSSM // 128
    own = nc is None
    if own:
        nc = bass.Bass('TRN2', target_bir_lowering=False)
    if ctx is None:
        ctx = Ctx(nc)
    x_cm = io['x_cm'] if io is not None else nc.dram_tensor('x_cm', [DSSM, Tn], F32, kind='ExternalInput').ap()
    b_cm = io['B_cm'] if io is not None else nc.dram_tensor('B_cm', [DST, Tn], F32, kind='ExternalInput').ap()
    c_cm = io['C_cm'] if io is not None else nc.dram_tensor('C_cm', [DST, Tn], F32, kind='ExternalInput').ap()
    dt_tm = io['dt_tm'] if io is not None else nc.dram_tensor('dt_tm', [Tn, NH0], F32, kind='ExternalInput').ap()
    cs_row = io['cs_row'] if io is not None else nc.dram_tensor('cs_row', [NH0, Tn], F32, kind='ExternalInput').ap()
    cs_tm = io['cs_tm'] if io is not None else nc.dram_tensor('cs_tm', [Tn, NH0], F32, kind='ExternalInput').ap()
    dsdt_tm = io['dsdt_tm'] if io is not None else nc.dram_tensor('dsdt_tm', [Tn, NH0], F32, kind='ExternalInput').ap()
    d_bc = io['D_bc'] if io is not None else nc.dram_tensor('D_bc', [128, NH0], F32, kind='ExternalInput').ap()
    mask_h = io['mask'] if io is not None else nc.dram_tensor('mask', [ch, ch], F32, kind='ExternalInput').ap()
    ones1_h = io['ones1'] if io is not None else nc.dram_tensor('ones1', [1, 128], F32, kind='ExternalInput').ap()
    ident_h = io['ident'] if io is not None else nc.dram_tensor('ident', [128, 128], BF16, kind='ExternalInput').ap()
    ydx_o = io['ydx'] if io is not None else nc.dram_tensor('ydx', [Tn, DSSM], F32, kind='ExternalOutput').ap()
    st_o = io['states'] if io is not None else nc.dram_tensor('states', [nch * DSSM, DST], F32, kind='ExternalOutput').ap()

    sxu = ctx.sb('sxu', [128, ncx * ch], BF16)
    sxt = ctx.sb('sxt', [128, njt * DSSM], BF16)
    sbu = ctx.sb('sbu', [128, ch], BF16)
    sbt = ctx.sb('sbt', [128, njt * DST], BF16)
    scu = ctx.sb('scu', [128, ch], BF16)
    s_dt = ctx.sb('s_dt', [128, njt * NH0], F32)
    s_cst = ctx.sb('s_cst', [128, njt * NH0], F32)
    s_dsd = ctx.sb('s_dsd', [128, njt * NH0], F32)
    s_csr = ctx.sb('s_csr', [1, NH0 * ch], F32)
    s_db = ctx.sb('s_db', [128, NH0], F32)
    s_mask = ctx.sb('s_mask', [128, njt * ch], F32)
    s_ones = ctx.sb('s_ones', [1, 128], F32)
    s_id = ctx.sb('s_id', [128, 128], BF16)
    s_sm = ctx.sb('s_sm', [128, njt * ch], BF16)
    s_csb = ctx.sb('s_csb', [128, ch], F32)
    s_g = ctx.sb('s_g', [128, njt * ch], F32)
    s_e = ctx.sb('s_e', [128, njt * ch], BF16)
    s_m = ctx.sb('s_m', [128, njt * ch], BF16)
    s_xdt = ctx.sb('s_xdt', [128, njt * HD0], BF16)
    s_bt2 = ctx.sb('s_bt2', [128, njt * DST], BF16)
    s_y = ctx.sb('s_y', [128, njt * DSSM], F32)
    s_st = ctx.sb('s_st', [128, ncx * DST], F32)

    p_tr = [ctx.ps('ptr%d' % i, [128, 128], BF16) for i in range(2)]
    p_s = [ctx.ps('p_s%d' % i, [128, ch], F32) for i in range(njt)]
    p_bc = ctx.ps('p_bc', [128, ch], F32)
    p_y = [ctx.ps('p_y%d' % i, [128, HD0], F32) for i in range(njt)]
    p_st = ctx.ps('p_st', [64, DST], F32)


    ctx.dma('y', 'const', s_db[:, :], d_bc[:, :])
    for jt in range(njt):
        ctx.dma('y', 'const', s_mask[:, jt * ch:(jt + 1) * ch],
                mask_h[jt * 128:(jt + 1) * 128, :])
    ctx.dma('y', 'const', s_ones[:, :], ones1_h[:, :])
    ctx.dma('y', 'const', s_id[:, :], ident_h[:, :])
    ctok = ctx.dtok('const')

    for u in range(nch):
        t0 = u * ch
        ub = [ctx.tok('v'), ctx.tok('t'), ctx.tok('s'),
              ctx.dtok('oy') if ctx.has_dma('oy') else None,
              ctx.dtok('ost') if ctx.has_dma('ost') else None]
        for i in range(ncx):
            ctx.dma('g', 'ld', sxu[:, i * ch:(i + 1) * ch],
                    x_cm[i * 128:(i + 1) * 128, t0:t0 + ch], waits=ub, serial=True)
        ctx.dma('g', 'ld', sbu[:, :], b_cm[:, t0:t0 + ch], waits=ub, serial=True)
        ctx.dma('g', 'ld', scu[:, :], c_cm[:, t0:t0 + ch], waits=ub, serial=True)
        for jt in range(njt):
            r0 = t0 + jt * 128
            ctx.dma('y', 'ldh', s_dt[:, jt * NH0:(jt + 1) * NH0], dt_tm[r0:r0 + 128, :],
                    waits=ub, serial=True)
            ctx.dma('y', 'ldh', s_cst[:, jt * NH0:(jt + 1) * NH0], cs_tm[r0:r0 + 128, :],
                    waits=ub, serial=True)
            ctx.dma('y', 'ldh', s_dsd[:, jt * NH0:(jt + 1) * NH0], dsdt_tm[r0:r0 + 128, :],
                    waits=ub, serial=True)
        ctx.dma('y', 'ldh', s_csr[:, :].rearrange('o (h w) -> o h w', w=ch),
                cs_row[:, t0:t0 + ch].unsqueeze(0), waits=ub, serial=True)
        ld = ctx.dtok('ld')
        ldh = ctx.dtok('ldh')

        prev_ptr = {0: None, 1: None}
        for idx in range(ncx * njt + njt):
            if idx < ncx * njt:
                i, jt = divmod(idx, njt)
                src_ap = sxu[:, i * ch + jt * 128:i * ch + (jt + 1) * 128]
                dst_ap = sxt[:, jt * DSSM + i * 128:jt * DSSM + (i + 1) * 128]
            else:
                jt = idx - ncx * njt
                src_ap = sbu[:, jt * 128:(jt + 1) * 128]
                dst_ap = sbt[:, jt * DST:(jt + 1) * DST]
            pk = idx % 2
            tk = ctx.op('t', lambda te, s=src_ap, pk=pk: te.transpose(
                p_tr[pk][:, :], s, s_id[:, :]), waits=[ld, ctok, prev_ptr[pk]])
            ev = ctx.op('v', lambda v, d=dst_ap, pk=pk: v.tensor_copy(d, p_tr[pk][:, :]),
                        waits=[tk])
            prev_ptr[pk] = ev
        ttr = ctx.tok('v')

        tsm = []
        for jt in range(njt):
            tk = ctx.op('t', lambda te, jt=jt: te.matmul(
                p_s[jt][:, :], sbu[:, jt * 128:(jt + 1) * 128], scu[:, :],
                start=True, stop=True), waits=[ld, ub[0]])
            ev = ctx.op('v', lambda v, jt=jt: v.scalar_tensor_tensor(
                s_sm[:, jt * ch:(jt + 1) * ch], p_s[jt][:, :], 0.0,
                s_mask[:, jt * ch:(jt + 1) * ch], ALU.add, ALU.mult),
                waits=[tk, ctok])
            tsm.append(ev)

        prev = {}
        for h in range(NH0):
            tbc = ctx.op('t', lambda te, h=h: te.matmul(
                p_bc[:, :], s_ones[:, :], s_csr[0:1, h * ch:(h + 1) * ch],
                start=True, stop=True),
                waits=[ldh, ctok, prev.get('csb')])
            ebc = ctx.op('v', lambda v: v.tensor_copy(s_csb[:, :], p_bc[:, :]),
                         waits=[tbc, prev.get('csb_use')])
            tg = []
            for jt in range(njt):
                t1 = ctx.op('v', lambda v, jt=jt, h=h: v.tensor_scalar(
                    s_g[:, jt * ch:(jt + 1) * ch], s_csb[:, :],
                    s_cst[:, jt * NH0 + h:jt * NH0 + h + 1], 0.0,
                    ALU.subtract, ALU.min), waits=[ebc, ldh, prev.get(('g', jt))])
                t2 = ctx.op('s', lambda s, jt=jt: s.activation(
                    s_e[:, jt * ch:(jt + 1) * ch], s_g[:, jt * ch:(jt + 1) * ch], AF.Exp),
                    waits=[t1, prev.get(('e', jt))])
                prev[('g', jt)] = t2
                t3 = ctx.op('v', lambda v, jt=jt: v.tensor_tensor(
                    s_m[:, jt * ch:(jt + 1) * ch], s_e[:, jt * ch:(jt + 1) * ch],
                    s_sm[:, jt * ch:(jt + 1) * ch], ALU.mult),
                    waits=[t2, tsm[jt], prev.get(('m', jt))])
                prev[('e', jt)] = t3
                tg.append(t3)
            prev['csb_use'] = tg[0]
            prev['csb'] = tg[njt - 1]
            txd = []
            for jt in range(njt):
                t4 = ctx.op('v', lambda v, jt=jt, h=h: v.tensor_scalar(
                    s_xdt[:, jt * HD0:(jt + 1) * HD0],
                    sxt[:, jt * DSSM + h * HD0:jt * DSSM + (h + 1) * HD0],
                    s_dt[:, jt * NH0 + h:jt * NH0 + h + 1], None, ALU.mult),
                    waits=[ttr, ldh, prev.get(('xdt', jt))])
                t5 = ctx.op('v', lambda v, jt=jt, h=h: v.tensor_scalar(
                    s_bt2[:, jt * DST:(jt + 1) * DST], sbt[:, jt * DST:(jt + 1) * DST],
                    s_dsd[:, jt * NH0 + h:jt * NH0 + h + 1], None, ALU.mult),
                    waits=[ttr, ldh, prev.get(('bt2', jt))])
                txd.append((t4, t5))
            tmy = []
            for it in range(njt):
                tk = None
                for jt in range(njt):
                    tk = ctx.op('t', lambda te, it=it, jt=jt: te.matmul(
                        p_y[it][:, :],
                        s_m[:, jt * ch + it * 128:jt * ch + (it + 1) * 128],
                        s_xdt[:, jt * HD0:(jt + 1) * HD0],
                        start=(jt == 0), stop=(jt == njt - 1)),
                        waits=[tg[jt], txd[jt][0],
                               prev.get(('py', it)) if jt == 0 else None])
                tmy.append(tk)
            tst = None
            for jt in range(njt):
                tst = ctx.op('t', lambda te, jt=jt, h=h: te.matmul(
                    p_st[0:64, :], sxt[:, jt * DSSM + h * HD0:jt * DSSM + (h + 1) * HD0],
                    s_bt2[:, jt * DST:(jt + 1) * DST],
                    start=(jt == 0), stop=(jt == njt - 1)),
                    waits=[txd[jt][0], txd[jt][1],
                           prev.get('pst') if jt == 0 else None])
            for jt in range(njt):
                prev[('m', jt)] = tmy[njt - 1]
                prev[('xdt', jt)] = ctx.tok('t')
                prev[('bt2', jt)] = tst
            yevs = []
            for it in range(njt):
                ev = ctx.op('v', lambda v, it=it, h=h: v.scalar_tensor_tensor(
                    s_y[:, it * DSSM + h * HD0:it * DSSM + (h + 1) * HD0],
                    sxt[:, it * DSSM + h * HD0:it * DSSM + (h + 1) * HD0],
                    s_db[:, h:h + 1], p_y[it][:, :], ALU.mult, ALU.add),
                    waits=[tmy[it], ctok, ub[3]])
                prev[('py', it)] = ev
                yevs.append(ev)
            hb = (h % 2) * 64
            hc = (h // 2) * DST
            ev2 = ctx.op('v', lambda v, hb=hb, hc=hc: v.tensor_copy(
                s_st[hb:hb + 64, hc:hc + DST], p_st[0:64, :]),
                waits=[tst, ub[4]])
            prev['pst'] = ev2
        vtok = ctx.tok('v')
        for it in range(njt):
            ctx.dma('y', 'oy', ydx_o[t0 + it * 128:t0 + (it + 1) * 128, :],
                    s_y[:, it * DSSM:(it + 1) * DSSM], waits=[vtok], serial=True)
        ctx.dma('y', 'ost', st_o[u * DSSM:(u + 1) * DSSM, :].rearrange(
            '(a p) d -> p a d', p=128),
            s_st[:, :].rearrange('p (a d) -> p a d', d=DST),
            waits=[vtok], serial=True)

    if own:
        ctx.emit()
    return nc



def build_state_scan(nchunks, rows, ndst=DST, nc=None, ctx=None, io=None):
    own = nc is None
    if own:
        nc = bass.Bass('TRN2', target_bir_lowering=False)
    if ctx is None:
        ctx = Ctx(nc)
    st_h = io['states'] if io is not None else nc.dram_tensor('states', [nchunks, rows, ndst], F32, kind='ExternalInput').ap()
    dc_h = io['decay_bc'] if io is not None else nc.dram_tensor(
        'decay_bc', [nchunks, 128, rows // 128], F32, kind='ExternalInput').ap()
    pv_h = io['prev'] if io is not None else nc.dram_tensor('prev', [nchunks, rows, ndst], F32, kind='ExternalOutput').ap()
    fin_h = io.get('final') if io is not None else None
    nrt = rows // 128
    s_pv = ctx.sb('s_pv', [128, nrt * ndst], F32)
    s_in = ctx.sb('s_in', [128, 2 * nrt * ndst], F32)
    s_dc = ctx.sb('s_dc', [128, 2 * nrt], F32)
    tz = ctx.op('v', lambda v: v.memset(s_pv[:, :], 0.0))
    upd = {i: tz for i in range(nrt)}
    for c in range(nchunks):
        par = c % 2
        ib = s_in[:, par * nrt * ndst:(par + 1) * nrt * ndst]
        db = s_dc[:, par * nrt:(par + 1) * nrt]
        lw = [upd[nrt - 1]] if c >= 2 else []
        lt = ctx.dma('g', 'ld%d' % par, ib.rearrange('p (a d) -> p a d', d=ndst),
                     st_h[c, :, :].rearrange('(a p) d -> p a d', p=128),
                     waits=lw, serial=True)
        lt2 = ctx.dma('g', 'ld%d' % par, db, dc_h[c, :, :], serial=True)
        stok = ctx.dma('y', 'st', pv_h[c, :, :].rearrange('(a p) d -> p a d', p=128),
                       s_pv[:, :].rearrange('p (a d) -> p a d', d=ndst),
                       waits=[upd[nrt - 1]], serial=True)
        for a in range(nrt):
            upd[a] = ctx.op('v', lambda v, a=a, ib=ib, db=db: v.scalar_tensor_tensor(
                s_pv[:, a * ndst:(a + 1) * ndst], s_pv[:, a * ndst:(a + 1) * ndst],
                db[:, a:a + 1], ib[:, a * ndst:(a + 1) * ndst], ALU.mult, ALU.add),
                waits=[lt2, stok, upd[a]])
    if fin_h is not None:
        ctx.dma('y', 'fin', fin_h[:, :].rearrange('(a p) d -> p a d', p=128),
                s_pv[:, :].rearrange('p (a d) -> p a d', d=ndst),
                waits=[upd[nrt - 1]], serial=True)
    if own:
        ctx.emit()
    return nc



def build_ssd_final(Tn=T, ch=CH0, nc=None, ctx=None, io=None):
    nch = Tn // ch
    njt = ch // 128
    own = nc is None
    if own:
        nc = bass.Bass('TRN2', target_bir_lowering=False)
    if ctx is None:
        ctx = Ctx(nc)
    ydx_h = io['ydx'] if io is not None else nc.dram_tensor('ydx', [Tn, DSSM], F32, kind='ExternalInput').ap()
    c_cm = io['C_cm'] if io is not None else nc.dram_tensor('C_cm', [DST, Tn], F32, kind='ExternalInput').ap()
    pvt_h = io['prevT'] if io is not None else nc.dram_tensor('prevT', [nch * DST, DSSM], F32, kind='ExternalInput').ap()
    ecs_h = io['expcs_tm'] if io is not None else nc.dram_tensor('expcs_tm', [Tn, NH0], F32, kind='ExternalInput').ap()
    z_h = (io.get('z_tm') if io is not None else None) if io is not None else None
    if io is None:
        z_h = nc.dram_tensor('z_tm', [Tn, DSSM], F32, kind='ExternalInput').ap()
    zcm_h = io.get('z_cm') if io is not None else None
    sint_h = io.get('s_inT') if io is not None else None
    sdr_h = io.get('sd_row') if io is not None else None
    on_h = io.get('ones1') if io is not None else None
    idf_h = io.get('identF') if io is not None else None
    yn_o = io['yn'] if io is not None else nc.dram_tensor('yn', [Tn, DSSM], BF16, kind='ExternalOutput').ap()

    s_y = ctx.sb('s_y', [128, njt * DSSM], F32)
    s_cu = ctx.sb('s_cu', [128, ch], BF16)
    s_pv = ctx.sb('s_pv', [128, DSSM], BF16)
    s_ec = ctx.sb('s_ec', [128, njt * NH0], F32)
    s_z = ctx.sb('s_z', [128, njt * DSSM], F32)
    s_sq = ctx.sb('s_sq', [128, DSSM], BF16)
    s_ms = ctx.sb('s_ms', [128, 8], F32)
    s_o = ctx.sb('s_o', [128, njt * DSSM], BF16)
    p_yo = [ctx.ps('p_yo%d' % i, [128, HD0], F32) for i in range(njt)]
    if zcm_h is not None:
        s_zc = ctx.sb('s_zc', [128, (DSSM // 128) * ch], F32)
        s_idf = ctx.sb('s_idf', [128, 128], F32)
        p_zt = ctx.ps('p_zt', [128, 128], F32)
    if sint_h is not None:
        s_sin = ctx.sb('s_sin', [DST, DSSM], F32)
        s_sdr = ctx.sb('s_sdr', [1, nch * NH0], F32)
        s_on1 = ctx.sb('s_on1', [1, 128], F32)
        s_sd = ctx.sb('s_sd', [128, NH0], F32)
        p_sd = ctx.ps('p_sd', [128, NH0], F32)

    eps_ap, eps_tok = ctx.const_col(1e-12, 'eps')
    corr0 = None
    if sint_h is not None:
        ctx.dma('y', 'corr', s_sin[:, :], sint_h[:, :])
        ctx.dma('y', 'corr', s_sdr[:, :], sdr_h[:, :])
        ctx.dma('y', 'corr', s_on1[:, :], on_h[:, :])
        corr0 = ctx.dtok('corr')
    if zcm_h is not None:
        ctx.dma('y', 'zid', s_idf[:, :], idf_h[:, :])
        corrz = ctx.dtok('zid')

    for u in range(nch):
        t0 = u * ch
        ub = [ctx.tok('v'), ctx.tok('t'), ctx.tok('s'),
              ctx.dtok('oy') if ctx.has_dma('oy') else None]
        for jt in range(njt):
            r0 = t0 + jt * 128
            ctx.dma('y', 'ldy', s_y[:, jt * DSSM:(jt + 1) * DSSM], ydx_h[r0:r0 + 128, :],
                    waits=ub, serial=True)
            ctx.dma('y', 'ldy', s_ec[:, jt * NH0:(jt + 1) * NH0], ecs_h[r0:r0 + 128, :],
                    waits=ub, serial=True)
            if z_h is not None:
                ctx.dma('y', 'ldy', s_z[:, jt * DSSM:(jt + 1) * DSSM],
                        z_h[r0:r0 + 128, :], waits=ub, serial=True)
        ctx.dma('g', 'ldg', s_cu[:, :], c_cm[:, t0:t0 + ch], waits=ub, serial=True)
        ctx.dma('g', 'ldg', s_pv[:, :], pvt_h[u * DST:(u + 1) * DST, :], waits=ub,
                serial=True)
        if zcm_h is not None:
            ctx.dma('g', 'ldg', s_zc[:, :].rearrange('p (a w) -> p a w', w=ch),
                    zcm_h[:, t0:t0 + ch].rearrange('(a p) w -> p a w', p=128),
                    waits=ub, serial=True)
        ld = ctx.dtok('ldy')
        ldg = ctx.dtok('ldg')
        if zcm_h is not None:
            pzt = None
            for a in range(DSSM // 128):
                for jt in range(njt):
                    tkz = ctx.op('t', lambda te, a=a, jt=jt: te.transpose(
                        p_zt[:, :], s_zc[:, a * ch + jt * 128:a * ch + (jt + 1) * 128],
                        s_idf[:, :]), waits=[ldg, corrz, pzt])
                    pzt = ctx.op('v', lambda v, a=a, jt=jt: v.tensor_copy(
                        s_z[:, jt * DSSM + a * 128:jt * DSSM + (a + 1) * 128],
                        p_zt[:, :]), waits=[tkz] + ub)
            ztok = pzt
        corr_w = None
        if sint_h is not None:
            msd = ctx.op('t', lambda te, u=u: te.matmul(
                p_sd[:, :], s_on1[0:1, :], s_sdr[0:1, u * NH0:(u + 1) * NH0],
                start=True, stop=True), waits=[corr0, ctx.tok('v')])
            esd = ctx.op('v', lambda v: v.tensor_copy(s_sd[:, :], p_sd[:, :]),
                         waits=[msd])
            ctok2 = esd
            for h in range(NH0):
                ctok2 = ctx.op('v', lambda v, h=h: v.scalar_tensor_tensor(
                    s_pv[:, h * HD0:(h + 1) * HD0], s_sin[:, h * HD0:(h + 1) * HD0],
                    s_sd[:, h:h + 1], s_pv[:, h * HD0:(h + 1) * HD0],
                    ALU.mult, ALU.add), waits=[ctok2, ldg, corr0])
        corr_w = ctx.tok('v') if sint_h is not None else None

        prev = {}
        for h in range(NH0):
            for it in range(njt):
                tk = ctx.op('t', lambda te, it=it, h=h: te.matmul(
                    p_yo[it][:, :], s_cu[:, it * 128:(it + 1) * 128],
                    s_pv[:, h * HD0:(h + 1) * HD0], start=True, stop=True),
                    waits=[ldg, corr_w, prev.get(('py', it))])
                ev = ctx.op('v', lambda v, it=it, h=h: v.scalar_tensor_tensor(
                    s_y[:, it * DSSM + h * HD0:it * DSSM + (h + 1) * HD0],
                    p_yo[it][:, :], s_ec[:, it * NH0 + h:it * NH0 + h + 1],
                    s_y[:, it * DSSM + h * HD0:it * DSSM + (h + 1) * HD0],
                    ALU.mult, ALU.add), waits=[tk, ld])
                prev[('py', it)] = ev
        for jt in range(njt):
            yt = s_y[:, jt * DSSM:(jt + 1) * DSSM]
            zt = s_z[:, jt * DSSM:(jt + 1) * DSSM]
            ot = s_o[:, jt * DSSM:(jt + 1) * DSSM]
            vt = ctx.tok('v')
            t1 = ctx.op('s', lambda s, zt=zt: s.activation(zt, zt, AF.Silu), waits=[vt, ld])
            t2 = ctx.op('v', lambda v, yt=yt, zt=zt: v.tensor_tensor(yt, yt, zt, ALU.mult),
                        waits=[t1, vt])
            t3 = ctx.op('s', lambda s, yt=yt: s.activation(s_sq[:, :], yt, AF.Square,
                                                           accum_out=s_ms[:, 0:1]),
                        waits=[t2])
            t4 = ctx.op('s', lambda s: s.activation(s_ms[:, 1:2], s_ms[:, 0:1], AF.Sqrt,
                                                    scale=1.0 / DSSM, bias=eps_ap[:, 0:1]),
                        waits=[t3, eps_tok])
            t5 = ctx.op('v', lambda v: v.reciprocal(s_ms[:, 2:3], s_ms[:, 1:2]), waits=[t4])
            t6 = ctx.op('v', lambda v, yt=yt, ot=ot: v.tensor_scalar(
                ot, yt, s_ms[:, 2:3], None, ALU.mult), waits=[t5, ub[3]])
            ctx.dma('y', 'oy', yn_o[t0 + jt * 128:t0 + (jt + 1) * 128, :], ot,
                    waits=[t6], serial=True)
    if own:
        ctx.emit()
    return nc


import math as _math

def _pwl_table():
    x = np.linspace(-4.0, 4.0, 33).astype(np.float32)
    erf = np.array([_math.erf(float(v) / _math.sqrt(2.0)) for v in x], np.float64)
    y = (np.float32(0.5) * x * (np.float32(1.0) + erf.astype(np.float32))).astype(np.float32)
    m = ((y[1:] - y[:-1]) / np.float32(0.25)).astype(np.float32)
    m = np.concatenate([m, np.float32([1.0])])
    c = np.empty(33, np.float32)
    c[0] = m[0]
    c[1:] = m[1:] - m[:-1]
    return x, y, c

_PWLX, _PWLY, _PWLC = _pwl_table()


def _pwl_table16():
    x = np.linspace(-4.0, 4.0, 33).astype(np.float32)
    erf = np.array([_math.erf(float(v) / _math.sqrt(2.0)) for v in x], np.float64)
    y = (np.float32(0.5) * x * (np.float32(1.0) + erf.astype(np.float32))).astype(np.float32)
    s = ((y[1:17] - y[:16]) / np.float32(0.25)).astype(np.float32)
    c = np.empty(16, np.float32)
    c[0] = s[0]
    c[1:] = s[1:] - s[:-1]
    kxu = (np.float32(256.0) * x[:16]).astype(np.float32)
    y0u = np.float32(256.0) * y[0]
    return kxu, c, y0u

_PWL16_KXU, _PWL16_C, _PWL16_Y0U = _pwl_table16()


def emit_round256(ctx, src_ap, dst_ap, tmp_ap, waits, tmp_guard=None):
    t1 = ctx.op('v', lambda v: v.tensor_scalar(tmp_ap, src_ap, 256.0, MAGIC,
                                               ALU.mult, ALU.add),
                waits=list(waits) + [tmp_guard])
    t2 = ctx.op('v', lambda v: v.tensor_scalar(dst_ap, tmp_ap, MAGIC, 1.0 / 256.0,
                                               ALU.subtract, ALU.mult), waits=[t1])
    return t2


GELU_MODE = 'pwl'


def emit_gelu_q(ctx, h_ap, out_ap, scr, waits):
    t1 = ctx.op('v', lambda v: v.tensor_scalar(scr['t'], h_ap, 256.0, MAGIC,
                                               ALU.mult, ALU.add),
                waits=list(waits) + [scr.get('guard')])
    if GELU_MODE == 'hw':
        t2 = ctx.op('s', lambda s: s.activation(scr['u'], scr['t'], AF.Gelu,
                                                scale=1.0 / 256.0,
                                                bias=-MAGIC / 256.0), waits=[t1])
        acc_tok = t2
        acc_ap = scr['u']
    else:
        if not hasattr(ctx, '_g16cc'):
            bias_aps = []
            for i in range(16):
                bap, cctok = ctx.const_col(float(-_PWL16_KXU[i]), 'g16b%d' % i)
                bias_aps.append(bap)
            ctx._g16cc = (bias_aps, cctok)
        bias_aps, cctok = ctx._g16cc
        t2 = ctx.op('v', lambda v: v.tensor_scalar(scr['u'], scr['t'], MAGIC, None,
                                                   ALU.subtract), waits=[t1])
        ta = ctx.op('s', lambda s: s.activation(scr['t'], scr['u'], AF.Abs),
                    waits=[t2])
        P_ = scr['acc'].shape[0]
        z0 = ctx.op('v', lambda v: v.tensor_scalar(
            scr['acc'], scr['u'], 0.0, float(_PWL16_Y0U), ALU.max, ALU.add),
            waits=[t2])
        rbuf = [scr['r'], scr['r2']]
        rfree = [None, None]
        acc_tok = z0
        for i in range(16):
            b = i % 2
            ci = float(_PWL16_C[i])
            tr = ctx.op('s', lambda s, b=b, i=i: s.activation(
                rbuf[b], scr['t'], AF.Relu, bias=bias_aps[i][0:P_, :], scale=-1.0),
                waits=[ta, cctok, rfree[b]])
            acc_tok = ctx.op('v', lambda v, b=b, ci=ci: v.scalar_tensor_tensor(
                scr['acc'], rbuf[b], ci, scr['acc'], ALU.mult, ALU.add),
                waits=[tr, acc_tok])
            rfree[b] = acc_tok
        t3 = ctx.op('v', lambda v: v.tensor_scalar(scr['t'], scr['acc'], MAGIC, None,
                                                   ALU.add), waits=[acc_tok])
        t4 = ctx.op('v', lambda v: v.tensor_scalar(out_ap, scr['t'], MAGIC, 1.0 / 256.0,
                                                   ALU.subtract, ALU.mult), waits=[t3])
        return t4
    t3 = ctx.op('v', lambda v: v.tensor_scalar(scr['t'], acc_ap, 256.0, MAGIC,
                                               ALU.mult, ALU.add), waits=[acc_tok])
    t4 = ctx.op('v', lambda v: v.tensor_scalar(out_ap, scr['t'], MAGIC, 1.0 / 256.0,
                                               ALU.subtract, ALU.mult), waits=[t3])
    return t4


def build_ffn1(Tn=T, nc=None, ctx=None, io=None):
    FF = 4096
    def post(nc, ctx, i, o0, odp, ot, par, tok):
        if not hasattr(ctx, '_gelu_scr'):
            scr_t = ctx.sb('g_t', [128, Tn], F32)
            scr_u = ctx.sb('g_u', [128, Tn], F32)
            scr_a = ctx.sb('g_a', [128, Tn], F32)
            scr_r = ctx.sb('g_r', [128, Tn], F32)
            scr_r2 = ctx.sb('g_r2', [128, Tn], F32)
            gout = ctx.sb('g_o', [128, 2 * Tn], BF16)
            ctx._gelu_scr = (scr_t, scr_u, scr_a, scr_r, scr_r2, gout)
            ctx._gelu_guard = None
        scr_t, scr_u, scr_a, scr_r, scr_r2, gout = ctx._gelu_scr
        ob = gout[:, par * Tn:(par + 1) * Tn]
        stw = ctx.dtok('st%d' % par) if ctx.has_dma('st%d' % par) else None
        scr = {'t': scr_t[:, :], 'u': scr_u[:, :], 'acc': scr_a[:, :], 'r': scr_r[:, :],
               'r2': scr_r2[:, :], 'guard': ctx._gelu_guard}
        tk = emit_gelu_q(ctx, ot, ob, scr, waits=[tok, stw])
        ctx._gelu_guard = tk
        return ob, tk
    return build_linear(DM, FF, Tn, bias=True, out_dtype=BF16, stage_dtype=F32,
                        post_od=post, nc=nc, ctx=ctx, io=io)


def build_f2b(Tn=T, nc=None, ctx=None, io=None):
    def post(nc, ctx, i, o0, odp, ot, par, tok):
        if not hasattr(ctx, '_scr'):
            ctx._scr = ctx.sb('r_t', [128, Tn], F32)
            ctx._g = None
        r2t = ctx._sres2[0:odp, par * Tn:par * Tn + Tn]
        lr = ctx.dma('y', 'r2%d' % par, r2t, ctx._r2ap[o0:o0 + odp, :], serial=True,
                     waits=[getattr(ctx, '_r2use%d' % par, None)])
        tk = emit_round256(ctx, ot, ot, ctx._scr[:, :], waits=[tok], tmp_guard=ctx._g)
        ctx._g = tk
        tk2 = ctx.op('v', lambda v, ot=ot, r2t=r2t: v.tensor_tensor(
            ot, ot, r2t, ALU.add), waits=[tk, lr])
        setattr(ctx, '_r2use%d' % par, tk2)
        return ot, tk2
    return build_linear(2048, DM, Tn, bias=True, residual=True, residual2=True,
                        x_hbm_dtype=BF16, post_od=post, nc=nc, ctx=ctx, io=io)



def build_conv_silu(C_, Tn=T, nc=None, ctx=None, io=None, halo_mask=False):
    own = nc is None
    if own:
        nc = bass.Bass('TRN2', target_bir_lowering=False)
    if ctx is None:
        ctx = Ctx(nc)
    xh = io['x_pad'] if io is not None else nc.dram_tensor('x_pad', [C_, Tn + 3], F32, kind='ExternalInput').ap()
    wh = io['w'] if io is not None else nc.dram_tensor('w', [C_, 4], F32, kind='ExternalInput').ap()
    oh = io['out'] if io is not None else nc.dram_tensor('out', [C_, Tn], F32, kind='ExternalOutput').ap()
    nct = C_ // 128
    sx = ctx.sb('sx', [128, 2 * (Tn + 3)], F32)
    sw = ctx.sb('sw', [128, nct * 4], F32)
    sa = ctx.sb('sa', [128, 2 * Tn], F32)
    so = ctx.sb('so', [128, 2 * Tn], F32)
    shm = ctx.sb('shm', [128, 1], F32) if halo_mask else None
    for i in range(nct):
        ctx.dma('y', 'w', sw[:, 4 * i:4 * i + 4], wh[i * 128:(i + 1) * 128, :])
    if halo_mask:
        hm_ap = io['hmask'] if io is not None else nc.dram_tensor(
            'hmask', [128, 1], F32, kind='ExternalInput').ap()
        ctx.dma('y', 'w', shm[:, :], hm_ap[:, :])
    wl = ctx.dtok('w')
    prev = {}
    for i in range(nct):
        par = i % 2
        xt = sx[:, par * (Tn + 3):par * (Tn + 3) + Tn + 3]
        at = sa[:, par * Tn:(par + 1) * Tn]
        ot = so[:, par * Tn:(par + 1) * Tn]
        stw = ctx.dtok('o%d' % par) if ctx.has_dma('o%d' % par) else None
        lt = ctx.dma('g', 'x%d' % par, xt, xh[i * 128:(i + 1) * 128, :], serial=True,
                     waits=[prev.get(('x', par))])
        if halo_mask:
            lt = ctx.op('v', lambda v, xt=xt: v.tensor_scalar(
                xt[:, 0:3], xt[:, 0:3], shm[:, 0:1], None, ALU.mult), waits=[lt, wl])
        t = ctx.op('v', lambda v, xt=xt, at=at, i=i: v.tensor_scalar(
            at, xt[:, 0:Tn], sw[:, 4 * i:4 * i + 1], None, ALU.mult),
            waits=[lt, wl, prev.get(('a', par))])
        for k in range(1, 4):
            t = ctx.op('v', lambda v, xt=xt, at=at, i=i, k=k: v.scalar_tensor_tensor(
                at, xt[:, k:k + Tn], sw[:, 4 * i + k:4 * i + k + 1], at,
                ALU.mult, ALU.add), waits=[t])
        prev[('x', par)] = t
        t2 = ctx.op('s', lambda s, at=at, ot=ot: s.activation(ot, at, AF.Silu),
                    waits=[t, stw])
        prev[('a', par)] = t2
        ctx.dma('y', 'o%d' % par, oh[i * 128:(i + 1) * 128, :], ot, waits=[t2],
                serial=True)
    if own:
        ctx.emit()
    return nc



def build_l2norm_qk(Tn=T, nc=None, ctx=None, io=None):
    own = nc is None
    if own:
        nc = bass.Bass('TRN2', target_bir_lowering=False)
    if ctx is None:
        ctx = Ctx(nc)
    qh = io['q_tm'] if io is not None else nc.dram_tensor('q_tm', [Tn, KEY_DIM], F32, kind='ExternalInput').ap()
    kh = io['k_tm'] if io is not None else nc.dram_tensor('k_tm', [Tn, KEY_DIM], F32, kind='ExternalInput').ap()
    qo = io['qn'] if io is not None else nc.dram_tensor('qn', [Tn, KEY_DIM], BF16, kind='ExternalOutput').ap()
    ko = io['kn'] if io is not None else nc.dram_tensor('kn', [Tn, KEY_DIM], BF16, kind='ExternalOutput').ap()
    nr = Tn // 128
    sx = ctx.sb('sx', [128, 2 * KEY_DIM], F32)
    sq = ctx.sb('sq', [128, KEY_DIM], BF16)
    sm = ctx.sb('sm', [128, 3 * H1], F32)
    so = ctx.sb('so', [128, 2 * KEY_DIM], BF16)
    eps_ap, ept = ctx.const_col(1e-6, 'epsn')
    prev = {}
    for idx in range(2 * nr):
        tensor_i = idx % 2
        r = idx // 2
        src_h = qh if tensor_i == 0 else kh
        dst_h = qo if tensor_i == 0 else ko
        par = idx % 2
        xt = sx[:, par * KEY_DIM:(par + 1) * KEY_DIM]
        ot = so[:, par * KEY_DIM:(par + 1) * KEY_DIM]
        stw = ctx.dtok('o%d' % par) if ctx.has_dma('o%d' % par) else None
        lt = ctx.dma('y', 'x%d' % par, xt, src_h[r * 128:(r + 1) * 128, :], serial=True,
                     waits=[prev.get(('u', par))])
        ts_ = []
        for h in range(H1):
            hs = slice(h * KD1, (h + 1) * KD1)
            t1 = ctx.op('s', lambda s, xt=xt, h=h, hs=hs: s.activation(
                sq[:, hs], xt[:, hs], AF.Square, accum_out=sm[:, h:h + 1]),
                waits=[lt, prev.get('sq')])
            t2 = ctx.op('s', lambda s, h=h: s.activation(
                sm[:, H1 + h:H1 + h + 1], sm[:, h:h + 1], AF.Sqrt), waits=[t1])
            t3 = ctx.op('v', lambda v, h=h: v.tensor_scalar(
                sm[:, H1 + h:H1 + h + 1], sm[:, H1 + h:H1 + h + 1], eps_ap[:, 0:1], None,
                ALU.max), waits=[t2, ept])
            t4 = ctx.op('v', lambda v, h=h: v.reciprocal(
                sm[:, 2 * H1 + h:2 * H1 + h + 1], sm[:, H1 + h:H1 + h + 1]), waits=[t3])
            t5 = ctx.op('v', lambda v, xt=xt, ot=ot, h=h, hs=hs: v.tensor_scalar(
                ot[:, hs], xt[:, hs], sm[:, 2 * H1 + h:2 * H1 + h + 1], None, ALU.mult),
                waits=[t4, stw])
            ts_.append(t5)
        prev[('u', par)] = ts_[-1]
        prev['sq'] = ts_[-1]
        ctx.dma('y', 'o%d' % par, dst_h[r * 128:(r + 1) * 128, :], ot,
                waits=[ts_[-1]], serial=True)
    if own:
        ctx.emit()
    return nc



def build_ab_rows(Tn=T, ch=CH1, nc=None, ctx=None, io=None):
    nch = Tn // ch
    own = nc is None
    if own:
        nc = bass.Bass('TRN2', target_bir_lowering=False)
    if ctx is None:
        ctx = Ctx(nc)
    ar = io['a_raw'] if io is not None else nc.dram_tensor('a_raw', [H1, Tn], F32, kind='ExternalInput').ap()
    br = io['b_raw'] if io is not None else nc.dram_tensor('b_raw', [H1, Tn], F32, kind='ExternalInput').ap()
    dtb = io['dt_bias'] if io is not None else nc.dram_tensor('dt_bias', [H1, 1], F32, kind='ExternalInput').ap()
    alog = io['A_log'] if io is not None else nc.dram_tensor('A_log', [H1, 1], F32, kind='ExternalInput').ap()
    lg_o = io['lg'] if io is not None else nc.dram_tensor('lg', [H1, Tn], F32, kind='ExternalOutput').ap()
    be_o = io['beta'] if io is not None else nc.dram_tensor('beta', [H1, Tn], F32, kind='ExternalOutput').ap()
    s = ctx.sb('s', [H1, 6 * Tn], F32)
    sc = ctx.sb('sc', [H1, 4], F32)
    U, AB, G, LG, BE, ZERO = (s[:, i * Tn:(i + 1) * Tn] for i in range(6))
    one_ap, one_tok = ctx.const_col(1.0, 'one')
    ctx.dma('y', 'ldy', U, ar[:, :])
    ctx.dma('y', 'ldy', BE, br[:, :])
    ctx.dma('y', 'ldy', sc[:, 0:1], dtb[:, :])
    ctx.dma('y', 'ldy', sc[:, 1:2], alog[:, :])
    ld = ctx.dtok('ldy')
    z0 = ctx.op('v', lambda v: v.memset(ZERO, 0.0))
    t = ctx.op('v', lambda v: v.tensor_scalar(U, U, sc[:, 0:1], None, ALU.add), waits=[ld])
    t = ctx.op('s', lambda s_: s_.activation(AB, U, AF.Abs), waits=[t])
    t = ctx.op('s', lambda s_: s_.activation(AB, AB, AF.Exp, scale=-1.0), waits=[t])
    t = ctx.op('s', lambda s_: s_.activation(AB, AB, AF.Ln, bias=one_ap[0:H1, 0:1]),
               waits=[t, one_tok])
    t2 = ctx.op('v', lambda v: v.tensor_scalar(G, U, 0.0, None, ALU.max), waits=[t])
    tsp = ctx.op('v', lambda v: v.tensor_tensor(G, AB, G, ALU.add), waits=[t, t2])
    te = ctx.op('s', lambda s_: s_.activation(sc[:, 2:3], sc[:, 1:2], AF.Exp), waits=[ld])
    tn = ctx.op('v', lambda v: v.tensor_scalar(sc[:, 2:3], sc[:, 2:3], -1.0, None,
                                               ALU.mult), waits=[te])
    tg = ctx.op('v', lambda v: v.tensor_scalar(G, G, sc[:, 2:3], None, ALU.mult),
                waits=[tsp, tn])
    tcs = None
    for c in range(nch):
        sl = slice(c * ch, (c + 1) * ch)
        tcs = ctx.op('v', lambda v, sl=sl: v.tensor_tensor_scan(
            LG[:, sl], G[:, sl], ZERO[:, sl], 0.0, ALU.add, ALU.add), waits=[tg, z0])
    tb = ctx.op('s', lambda s_: s_.activation(BE, BE, AF.Sigmoid), waits=[ld])
    tb = ctx.op('v', lambda v: v.tensor_scalar(BE, BE, 2.0, None, ALU.mult), waits=[tb])
    ctx.dma('y', 'o', lg_o[:, :], LG, waits=[tcs])
    ctx.dma('y', 'o', be_o[:, :], BE, waits=[tb])
    if own:
        ctx.emit()
    return nc



def build_delta_local(Tn=T, ch=CH1, nc=None, ctx=None, io=None):
    nch = Tn // ch
    own = nc is None
    if own:
        nc = bass.Bass('TRN2', target_bir_lowering=False)
    if ctx is None:
        ctx = Ctx(nc)
    qcm_h = io['q_cm'] if io is not None else nc.dram_tensor('q_cm', [KEY_DIM, Tn], BF16, kind='ExternalInput').ap()
    kcm_h = io['k_cm'] if io is not None else nc.dram_tensor('k_cm', [KEY_DIM, Tn], BF16, kind='ExternalInput').ap()
    qtm_h = io['q_tm'] if io is not None else nc.dram_tensor('q_tm', [Tn, KEY_DIM], BF16, kind='ExternalInput').ap()
    ktm_h = io['k_tm'] if io is not None else nc.dram_tensor('k_tm', [Tn, KEY_DIM], BF16, kind='ExternalInput').ap()
    vtm_h = io['v_tm'] if io is not None else nc.dram_tensor('v_tm', [Tn, VAL_DIM], F32, kind='ExternalInput').ap()
    lgr_h = io['lg_row'] if io is not None else nc.dram_tensor('lg_row', [H1, Tn], F32, kind='ExternalInput').ap()
    lgt_h = io['lg_tm'] if io is not None else nc.dram_tensor('lg_tm', [Tn, H1], F32, kind='ExternalInput').ap()
    bet_h = io['beta_tm'] if io is not None else nc.dram_tensor('beta_tm', [Tn, H1], F32, kind='ExternalInput').ap()
    idf_h = io['identF'] if io is not None else nc.dram_tensor('identF', [128, 128], F32, kind='ExternalInput').ap()
    id2_h = io['ident2'] if io is not None else nc.dram_tensor('ident2', [ch, ch], F32, kind='ExternalInput').ap()
    mi_h = io['mask_incl'] if io is not None else nc.dram_tensor('mask_incl', [ch, ch], F32, kind='ExternalInput').ap()
    ms_h = io['mask_strict'] if io is not None else nc.dram_tensor('mask_strict', [ch, ch], F32, kind='ExternalInput').ap()
    on_h = io['ones1'] if io is not None else nc.dram_tensor('ones1', [1, 128], F32, kind='ExternalInput').ap()
    olo_o = io['O_loc'] if io is not None else nc.dram_tensor('O_loc', [Tn, VAL_DIM], F32, kind='ExternalOutput').ap()
    chain = io is not None and 'PLT' in io
    mt_o = (io.get('Mt') if io is not None else None) if io is not None else None
    sl_o = (io.get('SlocT') if io is not None else None) if io is not None else None
    if io is None:
        mt_o = nc.dram_tensor('Mt', [nch * H1 * KD1, KD1], F32, kind='ExternalOutput').ap()
        sl_o = nc.dram_tensor('SlocT', [nch * H1 * KD1, VD1], F32, kind='ExternalOutput').ap()
    gt_o = io['GT'] if io is not None else nc.dram_tensor('GT', [nch * H1 * KD1, ch], F32, kind='ExternalOutput').ap()
    gc_o = (io.get('gamc') if io is not None else None) if io is not None else None
    if io is None:
        gc_o = nc.dram_tensor('gamc', [nch, H1], F32, kind='ExternalOutput').ap()
    plt_o = io.get('PLT') if io is not None else None
    ags_o = io.get('ag_S') if io is not None else None
    agp_o = io.get('ag_PL') if io is not None else None

    sq_cm = ctx.sb('sq_cm', [128, H1 * ch], BF16)
    sk_cm = ctx.sb('sk_cm', [128, H1 * ch], BF16)
    sq_tm = ctx.sb('sq_tm', [ch, KEY_DIM], F32)
    sk_tm = ctx.sb('sk_tm', [ch, KEY_DIM], F32)
    sv_tm = ctx.sb('sv_tm', [ch, VAL_DIM], F32)
    s_lgr = ctx.sb('s_lgr', [1, H1 * ch], F32)
    s_lgc = ctx.sb('s_lgc', [ch, H1], F32)
    s_bet = ctx.sb('s_bet', [ch, H1], F32)
    s_id = ctx.sb('s_id', [128, 128], F32)
    s_id2 = ctx.sb('s_id2', [ch, ch], F32)
    s_mi = ctx.sb('s_mi', [ch, ch], F32)
    s_ms = ctx.sb('s_ms', [ch, ch], F32)
    s_on = ctx.sb('s_on', [1, 128], F32)
    s_lgb = ctx.sb('s_lgb', [ch, ch], F32)
    s_E = ctx.sb('s_E', [ch, ch], F32)
    s_C = ctx.sb('s_C', [ch, ch], F32)
    s_A = ctx.sb('s_A', [ch, ch], F32)
    s_Dn = ctx.sb('s_Dn', [ch, ch], F32)
    s_DT = ctx.sb('s_DT', [ch, ch], F32)
    s_X = ctx.sb('s_X', [ch, ch], F32)
    s_XT = ctx.sb('s_XT', [ch, ch], F32)
    s_Z = ctx.sb('s_Z', [ch, ch], F32)
    s_Rv = ctx.sb('s_Rv', [ch, VD1], F32)
    s_Kp = ctx.sb('s_Kp', [ch, KD1], F32)
    s_Wl = ctx.sb('s_Wl', [ch, VD1], F32)
    s_Pl = ctx.sb('s_Pl', [ch, KD1], F32)
    s_Qt = ctx.sb('s_Qt', [ch, KD1], F32)
    s_G = ctx.sb('s_G', [ch, KD1], F32)
    s_GT = ctx.sb('s_GT', [KD1, ch], F32)
    s_LK = ctx.sb('s_LK', [ch, KD1], F32)
    s_Sl = ctx.sb('s_Sl', [KD1, VD1], F32)
    s_Mt = ctx.sb('s_Mt', [KD1, KD1], F32)
    s_col = ctx.sb('s_col', [ch, 4], F32)
    s_gr = ctx.sb('s_gr', [1, ch], F32)
    s_gc = ctx.sb('s_gc', [1, H1], F32)
    s_O = ctx.sb('s_O', [ch, VAL_DIM], F32)

    p1 = ctx.ps('p1', [ch, ch], F32)
    p2 = ctx.ps('p2', [ch, ch], F32)
    pP = ctx.ps('pP', [ch, KD1], F32)
    pT = ctx.ps('pT', [KD1, ch], F32)
    pW = ctx.ps('pW', [ch, VD1], F32)
    pS = ctx.ps('pS', [KD1, VD1], F32)
    pM = ctx.ps('pM', [KD1, KD1], F32)
    if chain:
        s_S0 = ctx.sb('s_S0', [128, H1 * VD1], F32)
        s_PLn = ctx.sb('s_PLn', [128, H1 * KD1], F32)
        s_PLt = ctx.sb('s_PLt', [128, H1 * KD1], F32)
        s_gI = ctx.sb('s_gI', [128, 128], F32)
        s_Lt = ctx.sb('s_Lt', [128, 128], F32)
        s_gcol = ctx.sb('s_gcol', [128, 1], F32)
        s_tmp3 = ctx.sb('s_tmp3', [128, VD1], F32)

    for src_h, dst in [(idf_h, s_id), (id2_h, s_id2), (mi_h, s_mi), (ms_h, s_ms),
                       (on_h, s_on)]:
        ctx.dma('y', 'const', dst[:, :], src_h[:, :])
    ctok = ctx.dtok('const')
    if chain:
        ztok = ctx.op('v', lambda v: v.memset(s_S0[:, :], 0.0))
        for h in range(H1):
            ztok = ctx.op('v', lambda v, h=h: v.tensor_copy(
                s_PLn[:, h * KD1:(h + 1) * KD1], s_id[:, :]), waits=[ctok, ztok])
            ztok = ctx.op('v', lambda v, h=h: v.tensor_copy(
                s_PLt[:, h * KD1:(h + 1) * KD1], s_id[:, :]), waits=[ztok])

    for u in range(nch):
        t0 = u * ch
        ub = [ctx.tok('v'), ctx.tok('t'), ctx.tok('s')] +              [ctx.dtok(n) for n in ('oO', 'oM', 'oS', 'oG', 'ogc') if ctx.has_dma(n)]
        ctx.dma('g', 'ld', sq_cm[:, :].rearrange('p (a w) -> p a w', w=ch),
                qcm_h[:, t0:t0 + ch].rearrange('(a p) w -> p a w', p=128),
                waits=ub, serial=True)
        ctx.dma('g', 'ld', sk_cm[:, :].rearrange('p (a w) -> p a w', w=ch),
                kcm_h[:, t0:t0 + ch].rearrange('(a p) w -> p a w', p=128),
                waits=ub, serial=True)
        ctx.dma('g', 'ld', sq_tm[:, :], qtm_h[t0:t0 + ch, :], waits=ub, serial=True)
        ctx.dma('g', 'ld', sk_tm[:, :], ktm_h[t0:t0 + ch, :], waits=ub, serial=True)
        ctx.dma('y', 'ldh', sv_tm[:, :], vtm_h[t0:t0 + ch, :], waits=ub, serial=True)
        ctx.dma('y', 'ldh', s_lgr[:, :].rearrange('o (a w) -> o a w', w=ch),
                lgr_h[:, t0:t0 + ch].unsqueeze(0), waits=ub, serial=True)
        ctx.dma('y', 'ldh', s_lgc[:, :], lgt_h[t0:t0 + ch, :], waits=ub, serial=True)
        ctx.dma('y', 'ldh', s_bet[:, :], bet_h[t0:t0 + ch, :], waits=ub, serial=True)
        ld = ctx.dtok('ld')
        ldh = ctx.dtok('ldh')

        def emit_head(h, u=u, ld=ld, ldh=ldh, ub=ub):
            hs = slice(h * ch, (h + 1) * ch)
            kcm = sk_cm[:, hs]
            qcm = sq_cm[:, hs]
            ktm = sk_tm[:, h * KD1:(h + 1) * KD1]
            qtm = sq_tm[:, h * KD1:(h + 1) * KD1]
            vtm = sv_tm[:, h * VD1:(h + 1) * VD1]
            lgc = s_lgc[:, h:h + 1]
            bet = s_bet[:, h:h + 1]

            def op(e, f, w=()):
                return ctx.op(e, f, waits=[x for x in w if x is not None])

            m1 = op('t', lambda te: te.matmul(p1[:, :], s_on[0:1, 0:ch],
                                              s_lgr[0:1, hs], start=True, stop=True),
                    [ldh, ctok, ctx._pv.get('p1') if hasattr(ctx, '_pv') else None])
            e1 = op('v', lambda v: v.tensor_copy(s_lgb[:, :], p1[:, :]),
                    [m1, ctx._pv.get('lgb') if hasattr(ctx, '_pv') else None,
                     ctx._pv.get('lgb_s') if hasattr(ctx, '_pv') else None])
            g1 = op('s', lambda s_: s_.activation(s_gr[:, :], s_lgb[0:1, :], AF.Exp), [e1])
            g2 = op('v', lambda v, h=h: v.tensor_copy(s_gc[:, h:h + 1], s_gr[:, ch - 1:ch]),
                    [g1])
            c1 = op('s', lambda s_: s_.activation(s_col[:, 0:1], lgc, AF.Exp), [ldh])
            c2 = op('s', lambda s_: s_.activation(s_col[:, 1:2], lgc, AF.Exp, scale=-1.0,
                                                  bias=s_lgb[:, ch - 1:ch]), [ldh, e1])
            c3 = op('v', lambda v: v.tensor_tensor(s_col[:, 2:3], s_col[:, 0:1], bet,
                                                   ALU.mult), [c1, ldh])
            e2 = op('v', lambda v: v.tensor_scalar(s_E[:, :], s_lgb[:, :], lgc, 0.0,
                                                   ALU.subtract, ALU.min), [e1, ldh])
            e3 = op('s', lambda s_: s_.activation(s_E[:, :], s_E[:, :], AF.Exp), [e2])
            m2 = op('t', lambda te: te.matmul(p2[:, :], kcm, kcm, start=True, stop=True),
                    [ld, ctx._pv.get('p2') if hasattr(ctx, '_pv') else None])
            e4 = op('v', lambda v: v.scalar_tensor_tensor(s_C[:, :], p2[:, :], 0.0,
                                                          s_ms[:, :], ALU.add, ALU.mult),
                    [m2, ctok])
            e5 = op('v', lambda v: v.tensor_tensor(s_C[:, :], s_C[:, :], s_E[:, :],
                                                   ALU.mult), [e4, e3])
            m3 = op('t', lambda te: te.matmul(p2[:, :], kcm, qcm, start=True, stop=True),
                    [ld, e4])
            e6 = op('v', lambda v: v.scalar_tensor_tensor(s_A[:, :], p2[:, :], 0.0,
                                                          s_mi[:, :], ALU.add, ALU.mult),
                    [m3, ctok])
            e7 = op('v', lambda v: v.tensor_tensor(s_A[:, :], s_A[:, :], s_E[:, :],
                                                   ALU.mult), [e6, e3])
            m4 = op('t', lambda te: te.transpose(p1[:, :], s_C[:, :], s_id[0:ch, 0:ch]),
                    [e5, e1, ctok])
            e8 = op('v', lambda v: v.scalar_tensor_tensor(s_Dn[:, :], p1[:, :], bet,
                                                          s_id[0:ch, 0:ch], ALU.mult,
                                                          ALU.add), [m4, ctok])
            m5 = op('t', lambda te: te.transpose(p1[:, :], s_Dn[:, :], s_id[0:ch, 0:ch]),
                    [e8])
            e9 = op('v', lambda v: v.tensor_copy(s_DT[:, :], p1[:, :]), [m5])
            x0 = op('v', lambda v: v.scalar_tensor_tensor(s_X[:, :], s_Dn[:, :], -1.0,
                                                          s_id2[:, :], ALU.mult, ALU.add),
                    [e8])
            xt_tok = None
            for itn in range(5):
                mA = op('t', lambda te: te.matmul(p2[:, :], s_DT[:, :], s_X[:, :],
                                                  start=True, stop=True),
                        [e9, x0 if itn == 0 else xt_tok, e6])
                zA = op('v', lambda v: v.scalar_tensor_tensor(s_Z[:, :], p2[:, :], -1.0,
                                                              s_id2[:, :], ALU.mult,
                                                              ALU.add), [mA])
                mB = op('t', lambda te: te.transpose(p1[:, :], s_X[:, :],
                                                     s_id[0:ch, 0:ch]),
                        [x0 if itn == 0 else xt_tok])
                zB = op('v', lambda v: v.tensor_copy(s_XT[:, :], p1[:, :]), [mB])
                mC = op('t', lambda te: te.matmul(p2[:, :], s_XT[:, :], s_Z[:, :],
                                                  start=True, stop=True), [zB, zA])
                xt_tok = op('v', lambda v: v.tensor_copy(s_X[:, :], p2[:, :]), [mC])
            mD = op('t', lambda te: te.transpose(p1[:, :], s_X[:, :], s_id[0:ch, 0:ch]),
                    [xt_tok])
            xT = op('v', lambda v: v.tensor_copy(s_XT[:, :], p1[:, :]), [mD])
            r1 = op('v', lambda v: v.tensor_scalar(s_Rv[:, :], vtm, bet, None, ALU.mult),
                    [ldh])
            r2 = op('v', lambda v: v.tensor_scalar(s_Kp[:, :], ktm, s_col[:, 2:3], None,
                                                   ALU.mult), [ld, c3])
            mW = op('t', lambda te: te.matmul(pW[:, :], s_XT[:, :], s_Rv[:, :],
                                              start=True, stop=True), [xT, r1])
            eW = op('v', lambda v: v.tensor_copy(s_Wl[:, :], pW[:, :]), [mW])
            mP = op('t', lambda te: te.matmul(pP[:, :], s_XT[:, :], s_Kp[:, :],
                                              start=True, stop=True), [xT, r2])
            eP = op('v', lambda v: v.tensor_copy(s_Pl[:, :], pP[:, :]), [mP])
            q1 = op('v', lambda v: v.tensor_scalar(s_Qt[:, :], qtm, s_col[:, 0:1], None,
                                                   ALU.mult), [ld, c1])
            mG = op('t', lambda te: te.matmul(pP[:, :], s_A[:, :], s_Pl[:, :],
                                              start=True, stop=True), [e7, eP])
            eG = op('v', lambda v: v.scalar_tensor_tensor(s_G[:, :], pP[:, :], -1.0,
                                                          s_Qt[:, :], ALU.mult, ALU.add),
                    [mG, q1])
            mGT = op('t', lambda te: te.transpose(pT[:, :], s_G[:, :], s_id[0:ch, 0:ch]),
                     [eG])
            eGT = op('v', lambda v: v.tensor_copy(s_GT[:, :], pT[:, :]),
                     [mGT, ctx.dtok('oG') if ctx.has_dma('oG') else None])
            mO = op('t', lambda te: te.matmul(pW[:, :], s_A[:, :], s_Wl[:, :],
                                              start=True, stop=not chain), [e7, eW])
            if chain:
                mO = op('t', lambda te, h=h: te.matmul(
                    pW[:, :], s_GT[:, :], s_S0[:, h * VD1:(h + 1) * VD1],
                    start=False, stop=True), [mO, eGT])
            eO = op('v', lambda v, h=h: v.tensor_copy(s_O[:, h * VD1:(h + 1) * VD1],
                                                      pW[:, :]), [mO] + ub)
            l1 = op('v', lambda v: v.tensor_scalar(s_LK[:, :], ktm, s_col[:, 1:2], None,
                                                   ALU.mult), [ld, c2])
            mS = op('t', lambda te: te.matmul(pS[:, :], s_LK[:, :], s_Wl[:, :],
                                              start=True, stop=True), [l1, eW, mO])
            eS = op('v', lambda v: v.tensor_copy(s_Sl[:, :], pS[:, :]),
                    [mS, ctx.dtok('oS') if ctx.has_dma('oS') else None])
            mM = op('t', lambda te: te.matmul(pM[:, :], s_Pl[:, :], s_LK[:, :],
                                              start=True, stop=True), [eP, l1])
            eM = op('v', lambda v: v.tensor_copy(s_Mt[:, :], pM[:, :]),
                    [mM, ctx.dtok('oM') if ctx.has_dma('oM') else None])
            r0 = (u * H1 + h) * KD1
            if sl_o is not None:
                ctx.dma('y', 'oS', sl_o[r0:r0 + KD1, :], s_Sl[:, :], waits=[eS],
                        serial=True)
            if mt_o is not None:
                ctx.dma('y', 'oM', mt_o[r0:r0 + KD1, :], s_Mt[:, :], waits=[eM],
                        serial=True)
            ctx.dma('y', 'oG', gt_o[r0:r0 + KD1, :], s_GT[:, :], waits=[eGT], serial=True)
            if chain:
                stP = ctx.dma('y', 'oP', plt_o[r0:r0 + KD1, :],
                              s_PLt[:, h * KD1:(h + 1) * KD1],
                              waits=[ctx.tok('v')], serial=True)
                mgc = op('t', lambda te: te.matmul(pT[:, 0:1], s_on[0:1, :],
                                                   s_gr[0:1, ch - 1:ch],
                                                   start=True, stop=True), [g1, eGT])
                egc = op('v', lambda v: v.tensor_copy(s_gcol[:, :], pT[:, 0:1]), [mgc])
                mC = op('t', lambda te, h=h: te.matmul(
                    pS[:, :], s_Mt[:, :], s_S0[:, h * VD1:(h + 1) * VD1],
                    start=True, stop=True), [eM, eS, mO])
                t1c = op('v', lambda v, h=h: v.scalar_tensor_tensor(
                    s_tmp3[:, :], s_S0[:, h * VD1:(h + 1) * VD1], s_gcol[:, 0:1],
                    pS[:, :], ALU.mult, ALU.subtract), [mC, egc])
                t2c = op('v', lambda v, h=h: v.tensor_tensor(
                    s_S0[:, h * VD1:(h + 1) * VD1], s_tmp3[:, :], s_Sl[:, :], ALU.add),
                    [t1c])
                gi = op('v', lambda v: v.tensor_scalar(s_gI[:, :], s_id[:, :],
                                                       s_gcol[:, 0:1], None, ALU.mult),
                        [egc])
                lt_ = op('v', lambda v: v.scalar_tensor_tensor(
                    s_Lt[:, :], s_Mt[:, :], -1.0, s_gI[:, :], ALU.mult, ALU.add),
                    [gi, eM])
                mPA = op('t', lambda te, h=h: te.matmul(
                    pM[:, :], s_Lt[:, :], s_PLn[:, h * KD1:(h + 1) * KD1],
                    start=True, stop=True), [lt_, eM])
                mPB = op('t', lambda te, h=h: te.matmul(
                    pS[:, 0:KD1], s_PLn[:, h * KD1:(h + 1) * KD1], s_Lt[:, :],
                    start=True, stop=True), [lt_, t1c, mPA])
                ePA = op('v', lambda v, h=h: v.tensor_copy(
                    s_PLn[:, h * KD1:(h + 1) * KD1], pM[:, :]), [mPA, mPB])
                ePB = op('v', lambda v, h=h: v.tensor_copy(
                    s_PLt[:, h * KD1:(h + 1) * KD1], pS[:, 0:KD1]),
                    [mPB, stP])
            if not hasattr(ctx, '_pv'):
                ctx._pv = {}
            ctx._pv['p1'] = xT
            ctx._pv['p2'] = xt_tok
            ctx._pv['lgb'] = ctx.tok('v')
            ctx._pv['lgb_s'] = ctx.tok('s')

        for h in range(H1):
            emit_head(h)
        vt = ctx.tok('v')
        ctx.dma('y', 'oO', olo_o[t0:t0 + ch, :], s_O[:, :], waits=[vt], serial=True)
        if gc_o is not None:
            ctx.dma('y', 'ogc', gc_o[u:u + 1, :], s_gc[:, :], waits=[vt], serial=True)

    if chain:
        vt = ctx.tok('v')
        for h in range(H1):
            ctx.dma('y', 'oAS', ags_o[h * KD1:(h + 1) * KD1, :],
                    s_S0[:, h * VD1:(h + 1) * VD1], waits=[vt], serial=True)
            ctx.dma('y', 'oAP', agp_o[h * KD1:(h + 1) * KD1, :],
                    s_PLt[:, h * KD1:(h + 1) * KD1], waits=[vt], serial=True)
    if own:
        ctx.emit()
    return nc



def build_delta_chain(nunits, nchg, ch=CH1, nc=None, ctx=None, io=None):
    own = nc is None
    if own:
        nc = bass.Bass('TRN2', target_bir_lowering=False)
    if ctx is None:
        ctx = Ctx(nc)
    mt_h = io['Mt'] if io is not None else nc.dram_tensor('Mt', [nunits * nchg * KD1, KD1], F32, kind='ExternalInput').ap()
    sl_h = io['SlocT'] if io is not None else nc.dram_tensor('SlocT', [nunits * nchg * KD1, VD1], F32, kind='ExternalInput').ap()
    gt_h = io['GT'] if io is not None else nc.dram_tensor('GT', [nunits * nchg * KD1, ch], F32, kind='ExternalInput').ap()
    gm_h = io['gam_bc'] if io is not None else nc.dram_tensor('gam_bc', [nunits * nchg, 128], F32, kind='ExternalInput').ap()
    oc_o = io['O_corr'] if io is not None else nc.dram_tensor('O_corr', [nunits * nchg * ch, VD1], F32, kind='ExternalOutput').ap()
    s_S0 = ctx.sb('s_S0', [128, VD1], F32)
    s_tmp = ctx.sb('s_tmp', [128, VD1], F32)
    s_mt = ctx.sb('s_mt', [128, 2 * KD1], F32)
    s_sl = ctx.sb('s_sl', [128, 2 * VD1], F32)
    s_gt = ctx.sb('s_gt', [128, 2 * ch], F32)
    s_gm = ctx.sb('s_gm', [128, 2], F32)
    s_oc = ctx.sb('s_oc', [ch, 2 * VD1], F32)
    pC = ctx.ps('pC', [128, VD1], F32)
    pO = ctx.ps('pO', [ch, VD1], F32)
    for un in range(nunits):
        z = ctx.op('v', lambda v: v.memset(s_S0[:, :], 0.0),
                   waits=[ctx._last] if hasattr(ctx, '_last') else [])
        upd = z
        for c in range(nchg):
            par = c % 2
            r0 = (un * nchg + c) * KD1
            lw = [ctx.tok('v')] if c >= 2 or un > 0 else []
            l1 = ctx.dma('g', 'ld%d' % par, s_mt[:, par * KD1:(par + 1) * KD1],
                         mt_h[r0:r0 + KD1, :], waits=lw, serial=True)
            l2 = ctx.dma('g', 'ld%d' % par, s_sl[:, par * VD1:(par + 1) * VD1],
                         sl_h[r0:r0 + KD1, :], serial=True)
            l3 = ctx.dma('g', 'ld%d' % par, s_gt[:, par * ch:(par + 1) * ch],
                         gt_h[r0:r0 + KD1, :], serial=True)
            l4 = ctx.dma('y', 'lh%d' % par, s_gm[:, par:par + 1],
                         gm_h[un * nchg + c:un * nchg + c + 1, :].transpose([1, 0]),
                         serial=True)
            mO = ctx.op('t', lambda te, par=par: te.matmul(
                pO[:, :], s_gt[:, par * ch:(par + 1) * ch], s_S0[:, :],
                start=True, stop=True),
                waits=[l3, upd, ctx._po if hasattr(ctx, '_po') else None])
            eO = ctx.op('v', lambda v, par=par: v.tensor_copy(
                s_oc[:, par * VD1:(par + 1) * VD1], pO[:, :]),
                waits=[mO, ctx.dtok('oc%d' % par) if ctx.has_dma('oc%d' % par) else None])
            ctx._po = eO
            ro = (un * nchg + c) * ch
            ctx.dma('y', 'oc%d' % par, oc_o[ro:ro + ch, :],
                    s_oc[:, par * VD1:(par + 1) * VD1], waits=[eO], serial=True)
            mC = ctx.op('t', lambda te, par=par: te.matmul(
                pC[:, :], s_mt[:, par * KD1:(par + 1) * KD1], s_S0[:, :],
                start=True, stop=True),
                waits=[l1, upd, ctx._pc if hasattr(ctx, '_pc') else None])
            t1 = ctx.op('v', lambda v, par=par: v.scalar_tensor_tensor(
                s_tmp[:, :], s_S0[:, :], s_gm[:, par:par + 1], pC[:, :],
                ALU.mult, ALU.subtract), waits=[mC, l4, mO])
            ctx._pc = t1
            upd = ctx.op('v', lambda v, par=par: v.tensor_tensor(
                s_S0[:, :], s_tmp[:, :], s_sl[:, par * VD1:(par + 1) * VD1], ALU.add),
                waits=[t1, l2])
        ctx._last = upd
    if own:
        ctx.emit()
    return nc



def build_gate_out(Tn=T, nc=None, ctx=None, io=None):
    own = nc is None
    if own:
        nc = bass.Bass('TRN2', target_bir_lowering=False)
    if ctx is None:
        ctx = Ctx(nc)
    ol_h = io['O_loc'] if io is not None else nc.dram_tensor('O_loc', [Tn, VAL_DIM], F32, kind='ExternalInput').ap()
    oc_h = io['O_corr'] if io is not None else nc.dram_tensor('O_corr', [Tn, VAL_DIM], F32, kind='ExternalInput').ap()
    g_h = io['g_tm'] if io is not None else nc.dram_tensor('g_tm', [Tn, VAL_DIM], F32, kind='ExternalInput').ap()
    og_o = io['og'] if io is not None else nc.dram_tensor('og', [Tn, VAL_DIM], BF16, kind='ExternalOutput').ap()
    nr = Tn // 128
    sA = ctx.sb('sA', [128, 2 * VAL_DIM], F32)
    sB = ctx.sb('sB', [128, VAL_DIM], F32)
    sG = ctx.sb('sG', [128, VAL_DIM], F32)
    sq = ctx.sb('sq', [128, VAL_DIM], BF16)
    sm = ctx.sb('sm', [128, 3 * H1], F32)
    so = ctx.sb('so', [128, 2 * VAL_DIM], BF16)
    eps_ap, ept = ctx.const_col(1e-5, 'epsr')
    prev = {}
    for r in range(nr):
        par = r % 2
        at = sA[:, par * VAL_DIM:(par + 1) * VAL_DIM]
        ot = so[:, par * VAL_DIM:(par + 1) * VAL_DIM]
        l1 = ctx.dma('y', 'lA%d' % par, at, ol_h[r * 128:(r + 1) * 128, :],
                     waits=[prev.get(('a', par))], serial=True)
        l2 = ctx.dma('y', 'lB', sB[:, :], oc_h[r * 128:(r + 1) * 128, :],
                     waits=[prev.get('b')], serial=True)
        l3 = ctx.dma('y', 'lG', sG[:, :], g_h[r * 128:(r + 1) * 128, :],
                     waits=[prev.get('g2')], serial=True)
        t1 = ctx.op('v', lambda v, at=at: v.tensor_tensor(at, at, sB[:, :], ALU.add),
                    waits=[l1, l2])
        prev['b'] = t1
        t2 = ctx.op('s', lambda s_: s_.activation(sG[:, :], sG[:, :], AF.Silu),
                    waits=[l3, prev.get('g2')])
        prev['g'] = t2
        prev['g2'] = None
        hts = []
        for h in range(H1):
            hs = slice(h * VD1, (h + 1) * VD1)
            q1 = ctx.op('s', lambda s_, at=at, hs=hs, h=h: s_.activation(
                sq[:, hs], at[:, hs], AF.Square, accum_out=sm[:, h:h + 1]),
                waits=[t1, prev.get('sq')])
            q2 = ctx.op('s', lambda s_, h=h: s_.activation(
                sm[:, H1 + h:H1 + h + 1], sm[:, h:h + 1], AF.Sqrt, scale=1.0 / VD1,
                bias=eps_ap[:, 0:1]), waits=[q1, ept])
            q3 = ctx.op('v', lambda v, h=h: v.reciprocal(
                sm[:, 2 * H1 + h:2 * H1 + h + 1], sm[:, H1 + h:H1 + h + 1]), waits=[q2])
            stw = ctx.dtok('o%d' % par) if ctx.has_dma('o%d' % par) else None
            q4 = ctx.op('v', lambda v, at=at, ot=ot, hs=hs, h=h: v.tensor_scalar(
                at[:, hs], at[:, hs], sm[:, 2 * H1 + h:2 * H1 + h + 1], None, ALU.mult),
                waits=[q3])
            q5 = ctx.op('v', lambda v, at=at, ot=ot, hs=hs: v.tensor_tensor(
                ot[:, hs], at[:, hs], sG[:, hs], ALU.mult), waits=[q4, t2, stw])
            hts.append(q5)
        prev['sq'] = hts[-1]
        prev[('a', par)] = hts[-1]
        prev['g2'] = hts[-1]
        ctx.dma('y', 'o%d' % par, og_o[r * 128:(r + 1) * 128, :], ot, waits=[hts[-1]],
                serial=True)
    if own:
        ctx.emit()
    return nc



def build_amax_part(C_, Tn=T, nc=None, ctx=None, io=None):
    own = nc is None
    if own:
        nc = bass.Bass('TRN2', target_bir_lowering=False)
    if ctx is None:
        ctx = Ctx(nc)
    x_h = io['x'] if io is not None else nc.dram_tensor('x', [C_, Tn], F32, kind='ExternalInput').ap()
    idf_h = io['identF'] if io is not None else nc.dram_tensor('identF', [128, 128], F32, kind='ExternalInput').ap()
    p_o = io['part'] if io is not None else nc.dram_tensor('part', [1, 1], F32, kind='ExternalOutput').ap()
    nk = C_ // 128
    sx = ctx.sb('sx', [128, 2 * Tn], F32)
    sc = ctx.sb('sc', [128, nk + 2], F32)
    sr = ctx.sb('sr', [1, 129], F32)
    s_id = ctx.sb('s_id', [128, 128], F32)
    pt = ctx.ps('pt', [1, 128], F32)
    ctx.dma('y', 'c', s_id[:, :], idf_h[:, :])
    ctok = ctx.dtok('c')
    prev = {}
    toks = []
    for k in range(nk):
        par = k % 2
        xt = sx[:, par * Tn:(par + 1) * Tn]
        lt = ctx.dma('g', 'x%d' % par, xt, x_h[k * 128:(k + 1) * 128, :],
                     waits=[prev.get(par)], serial=True)
        tk = ctx.op('v', lambda v, xt=xt, k=k: v.reduce_max(
            sc[:, k:k + 1], xt, axis=AXL.X, apply_absolute_value=True), waits=[lt])
        prev[par] = tk
        toks.append(tk)
    t1 = ctx.op('v', lambda v: v.reduce_max(sc[:, nk:nk + 1], sc[:, 0:nk], axis=AXL.X),
                waits=[toks[-1]])
    m1 = ctx.op('t', lambda te: te.transpose(pt[:, :], sc[:, nk:nk + 1], s_id[:, :]),
                waits=[t1, ctok])
    e1 = ctx.op('v', lambda v: v.tensor_copy(sr[:, 0:128], pt[:, :]), waits=[m1])
    t2 = ctx.op('v', lambda v: v.reduce_max(sr[:, 128:129], sr[:, 0:128], axis=AXL.X),
                waits=[e1])
    ctx.dma('y', 'o', p_o[:, :], sr[:, 128:129], waits=[t2])
    if own:
        ctx.emit()
    return nc



def build_ffn_fp8(Tn=T, nc=None, ctx=None, io=None):
    own = nc is None
    if own:
        nc = bass.Bass('TRN2', target_bir_lowering=False)
    if ctx is None:
        ctx = Ctx(nc)
    parts_h = io['parts'] if io is not None else nc.dram_tensor('parts', [1, NCORES], F32, kind='ExternalInput').ap()
    x_h = io['x'] if io is not None else nc.dram_tensor('x', [DM, Tn], F32, kind='ExternalInput').ap()
    w_h = io['w'] if io is not None else nc.dram_tensor('w', [DM, DM], F32, kind='ExternalInput').ap()
    b_h = io['bias'] if io is not None else nc.dram_tensor('bias', [DM, 1], F32, kind='ExternalInput').ap()
    r_h = io['resid'] if io is not None else nc.dram_tensor('resid', [DM, Tn], F32, kind='ExternalInput').ap()
    on_h = io['ones1'] if io is not None else nc.dram_tensor('ones1', [1, 128], F32, kind='ExternalInput').ap()
    idf_h = io['identF'] if io is not None else nc.dram_tensor('identF', [128, 128], F32, kind='ExternalInput').ap()
    out_o = io['out'] if io is not None else nc.dram_tensor('out', [DM, Tn], F32, kind='ExternalOutput').ap()
    nk = DM // 128
    sxf = ctx.sb('sxf', [128, 2 * Tn], F32)
    sx8 = ctx.sb('sx8', [128, Tn], F8)
    sxq = ctx.sb('sxq', [128, nk * Tn], BF16)
    swf = ctx.sb('swf', [128, nk * DM], F32)
    sw8 = ctx.sb('sw8', [128, DM], F8)
    swq = ctx.sb('swq', [128, nk * DM], BF16)
    sbias = ctx.sb('sbias', [128, nk], F32)
    sres = ctx.sb('sres', [128, nk * Tn], F32)
    s_sc = ctx.sb('s_sc', [1, 32], F32)
    s_cx = ctx.sb('s_cx', [128, 3], F32)
    s_id = ctx.sb('s_id', [128, 128], F32)
    s_on = ctx.sb('s_on', [1, 128], F32)
    s_mx = ctx.sb('s_mx', [128, nk + 2], F32)
    sr = ctx.sb('sr', [1, 129], F32)
    sout = ctx.sb('sout', [128, 2 * Tn], F32)
    pz = [ctx.ps('pz%d' % i, [128, 512], F32) for i in range(2)]
    pt = ctx.ps('pt', [1, 128], F32)
    pc = ctx.ps('pc', [128, 3], F32)
    ctx.dma('y', 'c', s_id[:, :], idf_h[:, :])
    ctx.dma('y', 'c', s_on[:, :], on_h[:, :])
    ctx.dma('y', 'c', s_sc[:, 0:NCORES], parts_h[:, :])
    for k in range(nk):
        ctx.dma('y', 'c', swf[:, k * DM:(k + 1) * DM], w_h[k * 128:(k + 1) * 128, :])
        ctx.dma('y', 'c', sbias[:, k:k + 1], b_h[k * 128:(k + 1) * 128, :])
        ctx.dma('y', 'c', sres[:, k * Tn:(k + 1) * Tn], r_h[k * 128:(k + 1) * 128, :])
    ctok = ctx.dtok('c')
    t1 = ctx.op('v', lambda v: v.reduce_max(s_sc[:, 8:9], s_sc[:, 0:NCORES], axis=AXL.X),
                waits=[ctok])
    wts = []
    for k in range(nk):
        wts.append(ctx.op('v', lambda v, k=k: v.reduce_max(
            s_mx[:, k:k + 1], swf[:, k * DM:(k + 1) * DM], axis=AXL.X,
            apply_absolute_value=True), waits=[ctok]))
    t2 = ctx.op('v', lambda v: v.reduce_max(s_mx[:, nk:nk + 1], s_mx[:, 0:nk],
                                            axis=AXL.X), waits=[wts[-1]])
    m1 = ctx.op('t', lambda te: te.transpose(pt[:, :], s_mx[:, nk:nk + 1], s_id[:, :]),
                waits=[t2, ctok])
    e1 = ctx.op('v', lambda v: v.tensor_copy(sr[:, 0:128], pt[:, :]), waits=[m1])
    t3 = ctx.op('v', lambda v: v.reduce_max(s_sc[:, 9:10], sr[:, 0:128], axis=AXL.X),
                waits=[e1])
    t4 = ctx.op('v', lambda v: v.tensor_scalar(s_sc[:, 10:12], s_sc[:, 8:10], 1e-12, None,
                                               ALU.max), waits=[t3, t1])
    t5 = ctx.op('v', lambda v: v.reciprocal(s_sc[:, 12:14], s_sc[:, 10:12]), waits=[t4])
    t6 = ctx.op('v', lambda v: v.tensor_scalar(s_sc[:, 14:16], s_sc[:, 12:14], 224.0,
                                               None, ALU.mult), waits=[t5])
    t7 = ctx.op('v', lambda v: v.tensor_tensor(s_sc[:, 16:17], s_sc[:, 14:15],
                                               s_sc[:, 15:16], ALU.mult), waits=[t6])
    t8 = ctx.op('v', lambda v: v.reciprocal(s_sc[:, 17:18], s_sc[:, 16:17]), waits=[t7])
    m2 = ctx.op('t', lambda te: te.matmul(pc[:, 0:3], s_on[0:1, :],
                                          s_sc[0:1, 14:17].copy() if False else
                                          s_sc[0:1, 14:17], start=True, stop=True),
                waits=[t8, ctok])
    e2 = ctx.op('v', lambda v: v.tensor_copy(s_cx[:, 0:3], pc[:, 0:3]), waits=[m2])
    e2b = ctx.op('v', lambda v: v.reciprocal(s_cx[:, 2:3], s_cx[:, 2:3]), waits=[e2])
    wq_toks = []
    for k in range(nk):
        a = ctx.op('v', lambda v, k=k: v.tensor_scalar(
            swf[:, k * DM:(k + 1) * DM], swf[:, k * DM:(k + 1) * DM], s_cx[:, 1:2], None,
            ALU.mult), waits=[e2, t2 if k == 0 else None])
        b_ = ctx.op('v', lambda v, k=k: v.tensor_copy(sw8[:, 0:DM],
                                                      swf[:, k * DM:(k + 1) * DM]),
                    waits=[a, wq_toks[-1] if wq_toks else None])
        c_ = ctx.op('v', lambda v, k=k: v.tensor_copy(swq[:, k * DM:(k + 1) * DM],
                                                      sw8[:, 0:DM]), waits=[b_])
        wq_toks.append(c_)
    xq_toks = []
    prevx = {}
    for k in range(nk):
        par = k % 2
        xt = sxf[:, par * Tn:(par + 1) * Tn]
        lt = ctx.dma('g', 'x%d' % par, xt, x_h[k * 128:(k + 1) * 128, :],
                     waits=[prevx.get(par)], serial=True)
        a = ctx.op('v', lambda v, xt=xt: v.tensor_scalar(xt, xt, s_cx[:, 0:1], None,
                                                         ALU.mult), waits=[lt, e2])
        b_ = ctx.op('v', lambda v, xt=xt: v.tensor_copy(sx8[:, :], xt),
                    waits=[a, xq_toks[-1] if xq_toks else None])
        prevx[par] = b_
        c_ = ctx.op('v', lambda v, k=k: v.tensor_copy(sxq[:, k * Tn:(k + 1) * Tn],
                                                      sx8[:, :]), waits=[b_])
        xq_toks.append(c_)
    xq = ctx.tok('v')
    prev_ev = {}
    for i in range(nk):
        par = i % 2
        ot = sout[:, par * Tn:(par + 1) * Tn]
        evs = []
        tbk = min(512, Tn)
        for j in range(cdiv(Tn, tbk)):
            t0j = j * tbk
            tn = min(tbk, Tn - t0j)
            bank = (i * cdiv(Tn, tbk) + j) % 2
            ps = pz[bank]
            for k in range(nk):
                def mm(te, k=k, i=i, t0j=t0j, tn=tn, ps=ps):
                    return te.matmul(ps[:, 0:tn],
                                     swq[:, k * DM + i * 128:k * DM + (i + 1) * 128],
                                     sxq[:, k * Tn + t0j:k * Tn + t0j + tn],
                                     start=(k == 0), stop=(k == nk - 1))
                w8 = [xq, prev_ev.get(bank)] if k == 0 else []
                tk = ctx.op('t', mm, waits=w8)
            stw = ctx.dtok('st%d' % par) if ctx.has_dma('st%d' % par) else None
            ev = ctx.op('v', lambda v, i=i, t0j=t0j, tn=tn, ps=ps, ot=ot: v.tensor_scalar(
                ot[:, t0j:t0j + tn], ps[:, 0:tn], s_cx[:, 2:3], sbias[:, i:i + 1],
                ALU.mult, ALU.add), waits=[tk, e2b, stw])
            ev = ctx.op('v', lambda v, i=i, t0j=t0j, tn=tn, ot=ot: v.tensor_tensor(
                ot[:, t0j:t0j + tn], ot[:, t0j:t0j + tn],
                sres[:, i * Tn + t0j:i * Tn + t0j + tn], ALU.add), waits=[ev])
            prev_ev[bank] = ev
            evs.append(ev)
        ctx.dma('y', 'st%d' % par, out_o[i * 128:(i + 1) * 128, :], ot,
                waits=[evs[-1]], serial=True)
    if own:
        ctx.emit()
    return nc



def emit_transpose(nc, ctx, src_ap, dst_ap, C_, Tn, dtype, ident_ap, itok):
    nr = cdiv(C_, 128)
    ntt = cdiv(Tn, 128)
    st_in = ctx.sb('tp_in', [128, Tn], dtype)
    st_out = ctx.sb('tp_out', [128, min(C_, 8192)], dtype)
    ptr = [ctx.ps('tp_ps%d' % i, [128, 128], dtype) for i in range(2)]
    prev = {0: None, 1: None}
    prev_row = {}
    for tt in range(ntt):
        tp = min(128, Tn - tt * 128)
        evs = []
        for cc in range(nr):
            cp = min(128, C_ - cc * 128)
            par = cc % 2
            lt = ctx.dma('g', 'tp%d' % par, st_in[0:cp, 0:Tn],
                         src_ap[cc * 128:cc * 128 + cp, :], serial=True,
                         waits=[prev.get(('l', par)), prev_row.get('store')])
            tk = ctx.op('t', lambda te, cp=cp, tt=tt, tp=tp, par=par: te.transpose(
                ptr[par][0:tp, 0:cp], st_in[0:cp, tt * 128:tt * 128 + tp],
                ident_ap[0:cp, 0:cp]), waits=[lt, itok, prev.get(par)])
            ev = ctx.op('v', lambda v, cp=cp, cc=cc, tp=tp, par=par: v.tensor_copy(
                st_out[0:tp, cc * 128:cc * 128 + cp], ptr[par][0:tp, 0:cp]),
                waits=[tk, prev_row.get('store')])
            prev[par] = ev
            prev[('l', par)] = tk
            evs.append(ev)
        prev_row['store'] = ctx.dma('y', 'tpo',
                                    dst_ap[tt * 128:tt * 128 + tp, :],
                                    st_out[0:tp, 0:C_], serial=True, waits=[evs[-1]])


def emit_transpose_blocked(nc, ctx, src_ap, dst_ap, C_, Tn, dtype, ident_ap, itok):
    nr = cdiv(C_, 128)
    ntt = cdiv(Tn, 128)
    st_in = ctx.sb('tq_in', [128, 2 * 128], dtype)
    st_out = ctx.sb('tq_out', [128, C_], dtype)
    ptr = [ctx.ps('tq_ps%d' % i, [128, 128], dtype) for i in range(2)]
    prev = {}
    for tt in range(ntt):
        tp = min(128, Tn - tt * 128)
        evs = []
        for cc in range(nr):
            cp = min(128, C_ - cc * 128)
            par = cc % 2
            xt = st_in[:, par * 128:par * 128 + 128]
            lt = ctx.dma('g', 'tq%d' % par, xt[0:cp, 0:tp],
                         src_ap[cc * 128:cc * 128 + cp, tt * 128:tt * 128 + tp],
                         serial=True, waits=[prev.get(('t', par))])
            tk = ctx.op('t', lambda te, xt=xt, cp=cp, tp=tp, par=par: te.transpose(
                ptr[par][0:tp, 0:cp], xt[0:cp, 0:tp], ident_ap[0:cp, 0:cp]),
                waits=[lt, itok, prev.get(par)])
            prev[('t', par)] = tk
            ev = ctx.op('v', lambda v, cc=cc, cp=cp, tp=tp, par=par: v.tensor_copy(
                st_out[0:tp, cc * 128:cc * 128 + cp], ptr[par][0:tp, 0:cp]),
                waits=[tk, prev.get('store')])
            prev[par] = ev
            evs.append(ev)
        prev['store'] = ctx.dma('y', 'tqo', dst_ap[tt * 128:tt * 128 + tp, :],
                                st_out[0:tp, 0:C_], serial=True, waits=[evs[-1]])


def emit_allgather(nc, ctx, name, in_ap, out_ap, groups):
    sem = nc.alloc_semaphore('cc_' + name)
    def fn(eng):
        eng.collective_compute('AllGather', ALU.bypass, replica_groups=groups,
                               ins=[in_ap.opt()], outs=[out_ap.opt()]).then_inc(sem, 1)
    ctx.script['g'].append(fn)
    ctx.dsem['cc_' + name] = sem
    ctx.dcnt['cc_' + name] = 1
    return (sem, 1)


def build_fused(seg=None):
    nc = bass.Bass('TRN2', target_bir_lowering=False)
    ctx = Ctx(nc)
    Tn = T
    nch0 = Tn // CH0
    nch1 = Tn // CH1
    EXI = 'ExternalInput'
    per_b = NCORES // B

    def inp(name, shape, dtype=F32, segs=None):
        if seg is not None and segs is not None and seg not in segs:
            return None
        return nc.dram_tensor(name, shape, dtype, kind=EXI).ap()

    def scr(name, shape, dtype=F32, segs=None):
        if seg is not None and segs is not None and seg not in segs:
            return None
        return nc.dram_tensor('scr_' + name, shape, dtype, kind='Internal').ap()

    def bnd(name, shape, dtype, prod, cons):
        if seg is None:
            return nc.dram_tensor('scr_' + name, shape, dtype, kind='Internal').ap()
        if seg == prod:
            return nc.dram_tensor(name, shape, dtype, kind='ExternalOutput').ap()
        if seg in cons:
            return nc.dram_tensor(name, shape, dtype, kind=EXI).ap()
        return None

    def in_seg(k):
        return seg is None or seg == k

    u_hal = inp('u_hal', [DM, Tn + 3], F32, segs=(1, 2))
    W_inT = inp('W_inT', [DM, D_IN_PROJ], F32, segs=(1,))
    b_in = inp('b_in', [D_IN_PROJ, 1], F32, segs=(1,))
    conv_w = inp('conv_w', [CONV_DIM, 4], F32, segs=(1,))
    conv_b = inp('conv_b', [CONV_DIM, 1], F32, segs=(1,))
    dt_bias = inp('dt_bias', [NH0, 1], F32, segs=(1,))
    A_log = inp('A_log', [NH0, 1], F32, segs=(1,))
    D_bc = inp('D_bc', [128, NH0], F32, segs=(1,))
    mask0 = inp('mask0', [CH0, CH0], F32, segs=(1,))
    ones1 = inp('ones1', [1, 128])
    identB = inp('identB', [128, 128], BF16)
    identF = inp('identF', [128, 128])
    norm_w = inp('norm_w', [DSSM, 1], F32, segs=(2,))
    W_outT = inp('W_outT', [DSSM, DM], F32, segs=(2,))
    b_out = inp('b_out', [DM, 1], F32, segs=(2,))
    w1 = inp('w1', [DM, 4096], F32, segs=(2,))
    b1 = inp('b1', [4096, 1], F32, segs=(2,))
    w2 = inp('w2', [4096, DM], F32, segs=(2,))
    b2 = inp('b2', [DM, 1], F32, segs=(2,))
    Wcat = inp('Wcat', [DM, 6160], F32, segs=(3,))
    convq_w = inp('convq_w', [KEY_DIM, 4], F32, segs=(3,))
    convk_w = inp('convk_w', [KEY_DIM, 4], F32, segs=(3,))
    convv_w = inp('convv_w', [VAL_DIM, 4], F32, segs=(3,))
    dt_bias1 = inp('dt_bias1', [H1, 1], F32, segs=(3,))
    A_log1 = inp('A_log1', [H1, 1], F32, segs=(3,))
    ident2 = inp('ident2', [CH1, CH1], F32, segs=(3,))
    mask_i1 = inp('mask_i1', [CH1, CH1], F32, segs=(3,))
    mask_s1 = inp('mask_s1', [CH1, CH1], F32, segs=(3,))
    onw = inp('onw', [VAL_DIM, 1], F32, segs=(4,))
    WoT = inp('WoT', [VAL_DIM, DM], F32, segs=(4,))
    w1f = inp('w1f', [DM, DM], F32, segs=(5,))
    b1f = inp('b1f', [DM, 1], F32, segs=(5,))
    hmask = inp('hmask', [128, 1], F32, segs=(1, 3))
    mam_mask = inp('mam_mask', [128, per_b], F32, segs=(2, 4))
    halo_sel = inp('halo_sel', [128, per_b], F32, segs=(3,))
    out_ap = (nc.dram_tensor('out', [Tn, DM], F32, kind='ExternalOutput').ap()
              if in_seg(5) else None)

    groups_b = [list(range(b * per_b, (b + 1) * per_b)) for b in range(B)]
    groups_all = [list(range(NCORES))]

    zx = scr('zx', [D_IN_PROJ, Tn + 3], F32, segs=(1,))
    z_x = bnd('z_x', [DSSM, Tn], F32, 1, (2,))
    c_x = bnd('c_x', [DST, Tn], F32, 1, (2,))
    convout = scr('convout', [CONV_DIM, Tn], F32, segs=(1,))
    rows = {n: scr(n, [NH0, Tn], F32, segs=(1,)) for n in ('dt', 'cs', 'expcs', 'dsdt')}
    cslast = scr('cslast', [NH0, nch0], F32, segs=(1,))
    rows_tm = {n: scr(n + '_tm', [Tn, NH0], F32, segs=(1,))
               for n in ('dt', 'cs', 'dsdt')}
    rows_tm['expcs'] = bnd('expcs_tm', [Tn, NH0], F32, 1, (2,))
    xcl8 = bnd('xcl8', [nch0, NH0], F32, 1, (2,))
    ydx = bnd('ydx', [Tn, DSSM], F32, 1, (2,))
    states = scr('states', [nch0 * DSSM, DST], F32, segs=(1,))
    prev_s = scr('prev', [nch0, DSSM, DST], F32, segs=(1,))
    s_fin = scr('s_fin', [DSSM, DST], F32, segs=(1,))
    s_finT = scr('s_finT', [DST, DSSM], F32, segs=(1,))
    prevT = bnd('prevT', [nch0 * DST, DSSM], F32, 1, (2,))
    decay_bc = scr('decay_bc', [nch0, 128, DSSM // 128], F32, segs=(1,))
    drow8 = scr('drow8', [nch0, NH0], F32, segs=(1,))
    ag_m_in = bnd('ag_m_in', [128, DSSM + 32], F32, 1, ())
    ag_m_out = bnd('ag_m_out', [per_b * 128, DSSM + 32], F32, -1, (2,))
    s_inT = scr('s_inT', [DST, DSSM], F32, segs=(2,))
    yn = scr('yn', [Tn, DSSM], BF16, segs=(2,))
    ynT = scr('ynT', [DSSM, Tn], BF16, segs=(2,))
    s1 = scr('s1', [DM, Tn], F32, segs=(2,))
    s1_tm = scr('s1_tm', [Tn, DM], F32, segs=(2,))
    x1_tm = scr('x1_tm', [Tn, DM], F32, segs=(2,))
    x1 = scr('x1', [DM, Tn], F32, segs=(2,))
    g_s = scr('g_s', [4096, Tn], BF16, segs=(2,))
    part = scr('part', [DM, Tn], F32, segs=(2,))
    s2 = scr('s2', [DM, Tn], F32, segs=(2,))
    s2_tm = scr('s2_tm', [Tn, DM], F32, segs=(2,))
    x2_tm = scr('x2_tm', [Tn, DM], F32, segs=(2,))
    x2 = bnd('x2', [DM, Tn], F32, 2, (3, 4))
    halo_pub = bnd('halo_pub', [DM, 3], F32, 2, ())
    halo_ag = bnd('halo_ag', [per_b * DM, 3], F32, -1, (3,))
    x2h = scr('x2h', [DM, Tn + 3], F32, segs=(3,))
    proj = scr('proj', [6160, Tn + 3], F32, segs=(3,))
    qc = scr('qc', [KEY_DIM, Tn], F32, segs=(3,))
    kc = scr('kc', [KEY_DIM, Tn], F32, segs=(3,))
    vc = scr('vc', [VAL_DIM, Tn], F32, segs=(3,))
    q_tm = scr('q_tm', [Tn, KEY_DIM], F32, segs=(3,))
    k_tm = scr('k_tm', [Tn, KEY_DIM], F32, segs=(3,))
    v_tm = scr('v_tm', [Tn, VAL_DIM], F32, segs=(3,))
    qn = scr('qn', [Tn, KEY_DIM], BF16, segs=(3,))
    kn = scr('kn', [Tn, KEY_DIM], BF16, segs=(3,))
    qn_cm = scr('qn_cm', [KEY_DIM, Tn], BF16, segs=(3,))
    kn_cm = scr('kn_cm', [KEY_DIM, Tn], BF16, segs=(3,))
    lg_r = scr('lg_r', [H1, Tn], F32, segs=(3,))
    beta_r = scr('beta_r', [H1, Tn], F32, segs=(3,))
    lg_tm = scr('lg_tm', [Tn, H1], F32, segs=(3,))
    beta_tm = scr('beta_tm', [Tn, H1], F32, segs=(3,))
    O_loc = bnd('O_loc', [Tn, VAL_DIM], F32, 3, (4,))
    GT_s = bnd('GT_s', [nch1 * H1 * KD1, CH1], F32, 3, (4,))
    PLT_s = bnd('PLT_s', [nch1 * H1 * KD1, KD1], F32, 3, (4,))
    ag_d_in = bnd('ag_d_in', [H1 * KD1, VD1 + KD1], F32, 3, ())
    ag_d_out = bnd('ag_d_out', [per_b * H1 * KD1, VD1 + KD1], F32, -1, (4,))
    g_tm_s = bnd('g_tm_s', [Tn, VAL_DIM], F32, 3, (4,))
    s_inT_d = scr('s_inT_d', [H1 * KD1, VD1], F32, segs=(4,))
    O_corr = scr('O_corr', [Tn, VAL_DIM], F32, segs=(4,))
    og = scr('og', [Tn, VAL_DIM], BF16, segs=(4,))
    ogT = scr('ogT', [VAL_DIM, Tn], BF16, segs=(4,))
    s3 = scr('s3', [DM, Tn], F32, segs=(4,))
    s3_tm = scr('s3_tm', [Tn, DM], F32, segs=(4,))
    x3_tm = scr('x3_tm', [Tn, DM], F32, segs=(4,))
    x3 = bnd('x3', [DM, Tn], F32, 4, (5,))
    am_part = bnd('am_part', [1, 1], F32, 4, ())
    am_ag = bnd('am_ag', [NCORES, 1], F32, -1, (5,))
    s4 = scr('s4', [DM, Tn], F32, segs=(5,))
    s4_tm = scr('s4_tm', [Tn, DM], F32, segs=(5,))

    def phase(pfx):
        ctx.end_phase()
        ctx.pfx = pfx + '_'

    def ldid(dtype):
        t = ctx.sb('ident_l', [128, 128], dtype)
        tok = ctx.dma('y', 'idl', t[:, :], identB[:, :] if dtype == BF16 else identF[:, :])
        return t, tok

    if in_seg(1):
        ctx.pfx = 'p00_'
        build_linear(DM, D_IN_PROJ, Tn + 3, bias=True, nc=nc, ctx=ctx,
                     io={'x': u_hal, 'w': W_inT, 'bias': b_in, 'out': zx})
        phase('p01')
        build_mamba_pre(Tn=Tn, nc=nc, ctx=ctx, io={
            'xbc_pad': zx[DSSM:DSSM + CONV_DIM, :],
            'conv_w': conv_w, 'conv_b': conv_b,
            'dt_raw': zx[DSSM + CONV_DIM:, 3:], 'dt_bias': dt_bias, 'A_log': A_log,
            'convout': convout, 'dt': rows['dt'], 'cs': rows['cs'],
            'expcs': rows['expcs'], 'dsdt': rows['dsdt'], 'cslast': cslast,
            'hmask': hmask})
        phase('p01b')
        ctx.dma('y', 'cp1', z_x[:, :], zx[0:DSSM, 3:])
        ctx.dma('y', 'cp1', c_x[:, :], convout[DSSM + DST:, :])
        for ip, n in enumerate(('dt', 'cs', 'expcs', 'dsdt')):
            phase('p02%d' % ip)
            idF, itok = ldid(F32)
            emit_transpose_blocked(nc, ctx, rows[n], rows_tm[n], NH0, Tn, F32,
                                   idF[:, :], itok)
        phase('p03')
        s_cl = ctx.sb('cl', [NH0, nch0], F32)
        s_ecl = ctx.sb('ecl', [NH0, nch0], F32)
        s_xcl = ctx.sb('xcl', [NH0, nch0 + 1], F32)
        s_z8 = ctx.sb('z8', [NH0, nch0], F32)
        s_tr = ctx.sb('tr8', [nch0, NH0], F32)
        s_dx = ctx.sb('dx', [16, 128], F32)
        s_dxt = ctx.sb('dxt', [128, DSSM // 128], F32)
        p_t8 = ctx.ps('pt8', [nch0, NH0], F32)
        p_dx = ctx.ps('pdx', [128, 16], F32)
        idF, itok = ldid(F32)
        l1 = ctx.dma('y', 'cl', s_cl[:, :], cslast[:, :])
        z1 = ctx.op('v', lambda v: v.memset(s_z8[:, :], 0.0))
        t1 = ctx.op('v', lambda v: v.tensor_tensor_scan(
            s_xcl[:, 1:nch0 + 1], s_cl[:, :], s_z8[:, :], 0.0, ALU.add, ALU.add),
            waits=[l1, z1])
        t1b = ctx.op('v', lambda v: v.memset(s_xcl[:, 0:1], 0.0))
        t2 = ctx.op('s', lambda s_: s_.activation(s_ecl[:, :], s_cl[:, :], AF.Exp),
                    waits=[l1])
        t3 = ctx.op('s', lambda s_: s_.activation(s_xcl[:, 0:nch0], s_xcl[:, 0:nch0],
                                                  AF.Exp), waits=[t1, t1b])
        tr1 = ctx.op('t', lambda te, idF=idF: te.transpose(
            p_t8[:, :], s_ecl[:, :], idF[0:NH0, 0:NH0]), waits=[t2, itok])
        e1 = ctx.op('v', lambda v: v.tensor_copy(s_tr[:, :], p_t8[:, :]), waits=[tr1])
        st1 = ctx.dma('y', 'dr8', drow8[:, :], s_tr[:, :], waits=[e1])
        tr2 = ctx.op('t', lambda te, idF=idF: te.transpose(
            p_t8[:, :], s_xcl[:, 0:nch0], idF[0:NH0, 0:NH0]), waits=[t3, itok, e1])
        e2 = ctx.op('v', lambda v: v.tensor_copy(s_tr[:, :], p_t8[:, :]), waits=[tr2, st1])
        st2 = ctx.dma('y', 'dr8', xcl8[:, :], s_tr[:, :], waits=[e2])
        s_r32 = ctx.sb('r32', [1, NH0], F32)
        s_r2k = ctx.sb('r2k', [1, DSSM], F32)
        dscr = scr('d_exp_row', [1, DSSM], F32, segs=(1,))
        pv = None
        for c in range(nch0):
            lr = ctx.dma('y', 'r32', s_r32[:, :], drow8[c:c + 1, :],
                         waits=[st1, pv], serial=True)
            cp = ctx.op('v', lambda v: v.tensor_copy(
                s_r2k[:, :].rearrange('o (h r) -> o h r', r=HD0),
                s_r32[:, :].unsqueeze(2).broadcast_to([1, NH0, HD0])), waits=[lr])
            sd = ctx.dma('y', 'r32o', dscr[:, :], s_r2k[:, :], waits=[cp], serial=True)
            ld2 = ctx.dma('y', 'r32i', s_dx[:, :],
                          dscr[:, :].rearrange('o (a p) -> (o a) p', p=128),
                          waits=[sd], serial=True)
            tr = ctx.op('t', lambda te, idF=idF: te.transpose(
                p_dx[:, 0:16], s_dx[:, :], idF[0:16, 0:16]), waits=[ld2, itok, pv])
            ev = ctx.op('v', lambda v: v.tensor_copy(s_dxt[:, :], p_dx[:, 0:16]),
                        waits=[tr])
            pv = ev
            ctx.dma('y', 'dbc', decay_bc[c, :, :], s_dxt[:, :], waits=[ev], serial=True)

        phase('p04')
        build_ssd_local(Tn=Tn, ch=CH0, nc=nc, ctx=ctx, io={
            'x_cm': convout[0:DSSM, :], 'B_cm': convout[DSSM:DSSM + DST, :],
            'C_cm': convout[DSSM + DST:, :], 'dt_tm': rows_tm['dt'],
            'cs_row': rows['cs'], 'cs_tm': rows_tm['cs'], 'dsdt_tm': rows_tm['dsdt'],
            'D_bc': D_bc, 'mask': mask0, 'ones1': ones1, 'ident': identB,
            'ydx': ydx, 'states': states})
        phase('p05')
        build_state_scan(nch0, DSSM, nc=nc, ctx=ctx, io={
            'states': states.rearrange('(c r) d -> c r d', c=nch0),
            'decay_bc': decay_bc, 'prev': prev_s, 'final': s_fin})
        phase('p06')
        idF, itok = ldid(F32)
        emit_transpose_blocked(nc, ctx, s_fin, s_finT, DSSM, DST, F32, idF[:, :], itok)
        for c in range(nch0):
            phase('p06%d' % c)
            idF, itok = ldid(F32)
            emit_transpose_blocked(nc, ctx, prev_s[c, :, :],
                                   prevT[c * DST:(c + 1) * DST, :], DSSM, DST, F32,
                                   idF[:, :], itok)
        phase('p07')
        s_cl2 = ctx.sb('cl2', [NH0, nch0], F32)
        s_tt = ctx.sb('tt', [NH0, 2], F32)
        s_ttr = ctx.sb('ttr', [1, NH0], F32)
        p_tt = ctx.ps('ptt', [1, NH0], F32)
        idF, itok = ldid(F32)
        l1 = ctx.dma('y', 'cl2', s_cl2[:, :], cslast[:, :])
        t1 = ctx.op('v', lambda v: v.reduce_sum(s_tt[:, 0:1], s_cl2[:, :], axis=AXL.X),
                    waits=[l1])
        t2 = ctx.op('s', lambda s_: s_.activation(s_tt[:, 1:2], s_tt[:, 0:1], AF.Exp),
                    waits=[t1])
        tr = ctx.op('t', lambda te, idF=idF: te.transpose(
            p_tt[:, :], s_tt[:, 1:2], idF[0:NH0, 0:NH0]), waits=[t2, itok])
        e1 = ctx.op('v', lambda v: v.tensor_copy(s_ttr[:, :], p_tt[:, :]), waits=[tr])
        s_zpad = ctx.sb('zpad', [128, 32], F32)
        zp = ctx.op('v', lambda v: v.memset(s_zpad[:, :], 0.0))
        ctx.dma('y', 'agm', ag_m_in[:, DSSM:DSSM + 32], s_zpad[:, :], waits=[zp],
                serial=True)
        ctx.dma('y', 'agm', ag_m_in[0:1, DSSM:DSSM + 32], s_ttr[:, :], waits=[e1],
                serial=True)
        ctx.dma('y', 'agm', ag_m_in[:, 0:DSSM], s_finT[:, :], serial=True)

    cc1 = None
    if seg is None:
        ctx.end_phase()
        ctx.pfx = 'p07b_'
        cc1 = emit_allgather(nc, ctx, 'mam', ag_m_in, ag_m_out, groups_b)

    if in_seg(2):
        ctx.end_phase()
        ctx.pfx = 'p07c_'
        s_ST = ctx.sb('ST', [DST, DSSM], F32)
        s_SJ = ctx.sb('SJ', [DST, DSSM], F32)
        s_dj = ctx.sb('dj', [1, NH0], F32)
        s_dex = ctx.sb('dex', [1, DSSM], F32)
        s_dbc = ctx.sb('dbc', [128, DSSM], F32)
        s_mm = ctx.sb('mmk', [128, per_b], F32)
        s_on2 = ctx.sb('on2', [1, 128], F32)
        s_t1 = ctx.sb('ft1', [DST, DSSM], F32)
        p_bc2 = ctx.ps('pbc2', [128, 512], F32)
        ctx.dma('y', 'fm', s_mm[:, :], mam_mask[:, :])
        ctx.dma('y', 'fm', s_on2[:, :], ones1[:, :])
        lm = lo = ctx.dtok('fm')
        z1 = ctx.op('v', lambda v: v.memset(s_ST[:, :], 0.0))
        fold_tok = z1
        for j in range(per_b):
            lj = ctx.dma('y', 'fj', s_SJ[:, :], ag_m_out[j * 128:j * 128 + DST, 0:DSSM],
                         serial=True, waits=[cc1, fold_tok])
            ldj = ctx.dma('y', 'fj', s_dj[:, :],
                          ag_m_out[j * 128:j * 128 + 1, DSSM:DSSM + 32], serial=True)
            cp = ctx.op('v', lambda v: v.tensor_copy(
                s_dex[:, :].rearrange('o (h r) -> o h r', r=HD0),
                s_dj[:, :].unsqueeze(2).broadcast_to([1, NH0, HD0])),
                waits=[ldj, fold_tok])
            bts = []
            for blk in range(DSSM // 512):
                tb = ctx.op('t', lambda te, blk=blk: te.matmul(
                    p_bc2[:, :], s_on2[0:1, :], s_dex[0:1, blk * 512:(blk + 1) * 512],
                    start=True, stop=True), waits=[cp, lo, bts[-1] if bts else fold_tok])
                eb = ctx.op('v', lambda v, blk=blk: v.tensor_copy(
                    s_dbc[:, blk * 512:(blk + 1) * 512], p_bc2[:, :]), waits=[tb])
                bts.append(eb)
            t1 = ctx.op('v', lambda v: v.tensor_tensor(s_t1[:, :], s_dbc[0:DST, :],
                                                       s_ST[:, :], ALU.mult),
                        waits=[bts[-1], fold_tok])
            t2 = ctx.op('v', lambda v: v.tensor_tensor(s_t1[:, :], s_t1[:, :],
                                                       s_SJ[:, :], ALU.add),
                        waits=[t1, lj])
            t3 = ctx.op('v', lambda v: v.tensor_tensor(s_t1[:, :], s_t1[:, :],
                                                       s_ST[:, :], ALU.subtract),
                        waits=[t2])
            fold_tok = ctx.op('v', lambda v, j=j: v.scalar_tensor_tensor(
                s_ST[:, :], s_t1[:, :], s_mm[0:DST, j:j + 1], s_ST[:, :],
                ALU.mult, ALU.add), waits=[t3, lm])
        ctx.dma('y', 'fst', s_inT[:, :], s_ST[:, :], waits=[fold_tok])

        phase('p08')
        build_ssd_final(Tn=Tn, ch=CH0, nc=nc, ctx=ctx, io={
            'ydx': ydx, 'C_cm': c_x, 'prevT': prevT,
            'expcs_tm': rows_tm['expcs'], 'yn': yn,
            's_inT': s_inT,
            'sd_row': xcl8.rearrange('c h -> (c h)').unsqueeze(0), 'ones1': ones1,
            'z_cm': z_x, 'identF': identF})
        phase('p09')
        idB, itok = ldid(BF16)
        emit_transpose_blocked(nc, ctx, yn, ynT, Tn, DSSM, BF16, idB[:, :], itok)
        phase('p10')
        build_linear(DSSM, DM, Tn, bias=True, residual=True, wscale=True,
                     x_hbm_dtype=BF16, nc=nc, ctx=ctx,
                     io={'x': ynT, 'w': W_outT, 'bias': b_out, 'wscale': norm_w,
                         'resid': u_hal[:, 3:], 'out': s1})
        phase('p11')
        idF, itok = ldid(F32)
        emit_transpose_blocked(nc, ctx, s1, s1_tm, DM, Tn, F32, idF[:, :], itok)
        phase('p12')
        build_ln(Tn, DM, nc=nc, ctx=ctx, io={'x': s1_tm, 'out': x1_tm})
        phase('p13')
        idF, itok = ldid(F32)
        emit_transpose_blocked(nc, ctx, x1_tm, x1, Tn, DM, F32, idF[:, :], itok)
        phase('p14')
        build_ffn1(Tn=Tn, nc=nc, ctx=ctx, io={'x': x1, 'w': w1, 'bias': b1, 'out': g_s})
        phase('p15')
        build_linear(2048, DM, Tn, bias=False, x_hbm_dtype=BF16, nc=nc, ctx=ctx,
                     io={'x': g_s[0:2048, :], 'w': w2[0:2048, :], 'out': part})
        phase('p16')
        build_f2b(Tn=Tn, nc=nc, ctx=ctx, io={'x': g_s[2048:, :], 'w': w2[2048:, :],
                                             'bias': b2, 'resid': part, 'resid2': x1,
                                             'out': s2})
        phase('p17')
        idF, itok = ldid(F32)
        emit_transpose_blocked(nc, ctx, s2, s2_tm, DM, Tn, F32, idF[:, :], itok)
        phase('p18')
        build_ln(Tn, DM, nc=nc, ctx=ctx, io={'x': s2_tm, 'out': x2_tm})
        phase('p19')
        idF, itok = ldid(F32)
        emit_transpose_blocked(nc, ctx, x2_tm, x2, Tn, DM, F32, idF[:, :], itok)
        phase('p20')
        s_hp = ctx.sb('hp', [128, 3], F32)
        prevh = None
        for kk in range(DM // 128):
            lt = ctx.dma('y', 'hpl', s_hp[:, :], x2[kk * 128:(kk + 1) * 128, Tn - 3:Tn],
                         serial=True, waits=[prevh])
            prevh = ctx.dma('y', 'hps', halo_pub[kk * 128:(kk + 1) * 128, :],
                            s_hp[:, :], serial=True, waits=[lt])

    cc2 = None
    if seg is None:
        ctx.end_phase()
        ctx.pfx = 'p20b_'
        cc2 = emit_allgather(nc, ctx, 'halo', halo_pub, halo_ag, groups_b)

    if in_seg(3):
        ctx.end_phase()
        ctx.pfx = 'p20c_'
        s_hs = ctx.sb('hs', [128, per_b], F32)
        s_hj = ctx.sb('hj', [128, 3], F32)
        s_ha = ctx.sb('ha', [128, 3], F32)
        lhs = ctx.dma('y', 'hsl', s_hs[:, :], halo_sel[:, :])
        prevh = None
        for kk in range(DM // 128):
            z = ctx.op('v', lambda v: v.memset(s_ha[:, :], 0.0), waits=[prevh])
            tacc = z
            for j in range(per_b):
                lj = ctx.dma('y', 'hjl', s_hj[:, :],
                             halo_ag[j * DM + kk * 128:j * DM + (kk + 1) * 128, :],
                             serial=True, waits=[cc2, tacc])
                tacc = ctx.op('v', lambda v, j=j: v.scalar_tensor_tensor(
                    s_ha[:, :], s_hj[:, :], s_hs[:, j:j + 1], s_ha[:, :], ALU.mult,
                    ALU.add), waits=[lj, lhs])
            prevh = ctx.dma('y', 'hao', x2h[kk * 128:(kk + 1) * 128, 0:3], s_ha[:, :],
                            waits=[tacc], serial=True)
        ctx.dma('y', 'hax', x2h[:, 3:], x2[:, :])
        phase('p21')
        build_linear(DM, 6160, Tn + 3, bias=False, nc=nc, ctx=ctx,
                     io={'x': x2h, 'w': Wcat, 'out': proj})
        phase('p22')
        build_conv_silu(KEY_DIM, Tn=Tn, nc=nc, ctx=ctx, halo_mask=True,
                        io={'x_pad': proj[0:1024, :], 'w': convq_w, 'out': qc,
                            'hmask': hmask})
        phase('p23')
        build_conv_silu(KEY_DIM, Tn=Tn, nc=nc, ctx=ctx, halo_mask=True,
                        io={'x_pad': proj[1024:2048, :], 'w': convk_w, 'out': kc,
                            'hmask': hmask})
        phase('p24')
        build_conv_silu(VAL_DIM, Tn=Tn, nc=nc, ctx=ctx, halo_mask=True,
                        io={'x_pad': proj[2048:4096, :], 'w': convv_w, 'out': vc,
                            'hmask': hmask})
        phase('p25')
        idF, itok = ldid(F32)
        emit_transpose_blocked(nc, ctx, qc, q_tm, KEY_DIM, Tn, F32, idF[:, :], itok)
        phase('p25b')
        idF, itok = ldid(F32)
        emit_transpose_blocked(nc, ctx, kc, k_tm, KEY_DIM, Tn, F32, idF[:, :], itok)
        phase('p25c')
        idF, itok = ldid(F32)
        emit_transpose_blocked(nc, ctx, vc, v_tm, VAL_DIM, Tn, F32, idF[:, :], itok)
        phase('p26')
        build_l2norm_qk(Tn=Tn, nc=nc, ctx=ctx, io={'q_tm': q_tm, 'k_tm': k_tm,
                                                   'qn': qn, 'kn': kn})
        phase('p27')
        idB, itok = ldid(BF16)
        emit_transpose_blocked(nc, ctx, qn, qn_cm, Tn, KEY_DIM, BF16, idB[:, :], itok)
        phase('p27b')
        idB, itok = ldid(BF16)
        emit_transpose_blocked(nc, ctx, kn, kn_cm, Tn, KEY_DIM, BF16, idB[:, :], itok)
        phase('p28')
        build_ab_rows(Tn=Tn, ch=CH1, nc=nc, ctx=ctx, io={
            'a_raw': proj[4096:4104, 3:], 'b_raw': proj[4104:4112, 3:],
            'dt_bias': dt_bias1, 'A_log': A_log1, 'lg': lg_r, 'beta': beta_r})
        phase('p29')
        idF, itok = ldid(F32)
        emit_transpose_blocked(nc, ctx, lg_r, lg_tm, H1, Tn, F32, idF[:, :], itok)
        phase('p29b')
        idF, itok = ldid(F32)
        emit_transpose_blocked(nc, ctx, beta_r, beta_tm, H1, Tn, F32, idF[:, :], itok)
        phase('p30')
        build_delta_local(Tn=Tn, ch=CH1, nc=nc, ctx=ctx, io={
            'q_cm': qn_cm, 'k_cm': kn_cm, 'q_tm': qn, 'k_tm': kn, 'v_tm': v_tm,
            'lg_row': lg_r, 'lg_tm': lg_tm, 'beta_tm': beta_tm,
            'identF': identF, 'ident2': ident2, 'mask_incl': mask_i1,
            'mask_strict': mask_s1, 'ones1': ones1,
            'O_loc': O_loc, 'GT': GT_s, 'PLT': PLT_s,
            'ag_S': ag_d_in[:, 0:VD1], 'ag_PL': ag_d_in[:, VD1:VD1 + KD1]})
        phase('p33')
        idF, itok = ldid(F32)
        emit_transpose_blocked(nc, ctx, proj[4112:6160, 3:], g_tm_s, VAL_DIM, Tn, F32,
                               idF[:, :], itok)

    cc3 = None
    if seg is None:
        ctx.end_phase()
        ctx.pfx = 'p31_'
        cc3 = emit_allgather(nc, ctx, 'delta', ag_d_in, ag_d_out, groups_b)

    if in_seg(4):
        ctx.end_phase()
        ctx.pfx = 'p31b_'
        s_STd = ctx.sb('STd', [128, H1 * VD1], F32)
        s_SJd = ctx.sb('SJd', [128, VD1], F32)
        s_PJ = ctx.sb('PJ', [128, KD1], F32)
        s_md = ctx.sb('md', [128, per_b], F32)
        s_tmp2 = ctx.sb('tmp2', [128, VD1], F32)
        p_f = ctx.ps('pf', [128, VD1], F32)
        lmd = ctx.dma('y', 'md', s_md[:, :], mam_mask[:, :])
        zz = ctx.op('v', lambda v: v.memset(s_STd[:, :], 0.0))
        fold_tok = zz
        for j in range(per_b):
            for h in range(H1):
                r0 = j * H1 * KD1 + h * KD1
                lj = ctx.dma('y', 'fdj', s_SJd[:, :], ag_d_out[r0:r0 + KD1, 0:VD1],
                             serial=True, waits=[cc3, fold_tok])
                lp = ctx.dma('y', 'fdj', s_PJ[:, :],
                             ag_d_out[r0:r0 + KD1, VD1:VD1 + KD1], serial=True)
                mm = ctx.op('t', lambda te, h=h: te.matmul(
                    p_f[:, :], s_PJ[:, :], s_STd[:, h * VD1:(h + 1) * VD1],
                    start=True, stop=True), waits=[lp, fold_tok])
                t1 = ctx.op('v', lambda v, h=h: v.scalar_tensor_tensor(
                    s_tmp2[:, :], p_f[:, :], 0.0, s_SJd[:, :], ALU.add, ALU.add),
                    waits=[mm, lj])
                t2 = ctx.op('v', lambda v, h=h: v.tensor_tensor(
                    s_tmp2[:, :], s_tmp2[:, :], s_STd[:, h * VD1:(h + 1) * VD1],
                    ALU.subtract), waits=[t1])
                fold_tok = ctx.op('v', lambda v, h=h, j=j: v.scalar_tensor_tensor(
                    s_STd[:, h * VD1:(h + 1) * VD1], s_tmp2[:, :], s_md[:, j:j + 1],
                    s_STd[:, h * VD1:(h + 1) * VD1], ALU.mult, ALU.add),
                    waits=[t2, lmd])
        ctx.dma('y', 'fso', s_inT_d[:, :].rearrange('(h p) d -> p h d', p=128),
                s_STd[:, :].rearrange('p (h d) -> p h d', d=VD1), waits=[fold_tok])
        phase('p32')
        s_PLT = ctx.sb('cPLT', [128, 2 * KD1], F32)
        s_GTl = ctx.sb('cGT', [128, 2 * CH1], F32)
        s_Sc = ctx.sb('cS', [128, H1 * VD1], F32)
        s_tc = ctx.sb('ctmp', [128, VD1], F32)
        s_oc = ctx.sb('cO', [CH1, 2 * VD1], F32)
        p_c1 = ctx.ps('pc1', [128, VD1], F32)
        p_c2 = ctx.ps('pc2', [CH1, VD1], F32)
        lS = ctx.dma('y', 'cS', s_Sc[:, :].rearrange('p (h d) -> p h d', d=VD1),
                     s_inT_d[:, :].rearrange('(h p) d -> p h d', p=128))
        prevc = {}
        for u in range(nch1):
            for h in range(H1):
                par = (u * H1 + h) % 2
                r0 = (u * H1 + h) * KD1
                l1 = ctx.dma('g', 'cl%d' % par, s_PLT[:, par * KD1:(par + 1) * KD1],
                             PLT_s[r0:r0 + KD1, :], serial=True,
                             waits=[prevc.get(('m', par))])
                l2 = ctx.dma('g', 'cl%d' % par, s_GTl[:, par * CH1:(par + 1) * CH1],
                             GT_s[r0:r0 + KD1, :], serial=True)
                m1 = ctx.op('t', lambda te, par=par, h=h: te.matmul(
                    p_c1[:, :], s_PLT[:, par * KD1:(par + 1) * KD1],
                    s_Sc[:, h * VD1:(h + 1) * VD1], start=True, stop=True),
                    waits=[l1, lS, prevc.get('e1')])
                e1 = ctx.op('v', lambda v: v.tensor_copy(s_tc[:, :], p_c1[:, :]),
                            waits=[m1, prevc.get('m2')])
                prevc['e1'] = e1
                m2 = ctx.op('t', lambda te, par=par: te.matmul(
                    p_c2[:, :], s_GTl[:, par * CH1:(par + 1) * CH1], s_tc[:, :],
                    start=True, stop=True), waits=[l2, e1, prevc.get('e2')])
                prevc['m2'] = m2
                e2 = ctx.op('v', lambda v, par=par: v.tensor_copy(
                    s_oc[:, par * VD1:(par + 1) * VD1], p_c2[:, :]),
                    waits=[m2, ctx.dtok('co%d' % par) if ctx.has_dma('co%d' % par)
                           else None])
                prevc['e2'] = e2
                prevc[('m', par)] = e2
                ro = u * CH1
                ctx.dma('y', 'co%d' % par,
                        O_corr[ro:ro + CH1, h * VD1:(h + 1) * VD1],
                        s_oc[:, par * VD1:(par + 1) * VD1], waits=[e2], serial=True)
        phase('p34')
        build_gate_out(Tn=Tn, nc=nc, ctx=ctx, io={'O_loc': O_loc, 'O_corr': O_corr,
                                                  'g_tm': g_tm_s, 'og': og})
        phase('p35')
        idB, itok = ldid(BF16)
        emit_transpose_blocked(nc, ctx, og, ogT, Tn, VAL_DIM, BF16, idB[:, :], itok)
        phase('p36')
        build_linear(VAL_DIM, DM, Tn, residual=True, wscale=True, x_hbm_dtype=BF16,
                     nc=nc, ctx=ctx, io={'x': ogT, 'w': WoT, 'wscale': onw,
                                         'resid': x2, 'out': s3})
        phase('p37')
        idF, itok = ldid(F32)
        emit_transpose_blocked(nc, ctx, s3, s3_tm, DM, Tn, F32, idF[:, :], itok)
        phase('p38')
        build_ln(Tn, DM, nc=nc, ctx=ctx, io={'x': s3_tm, 'out': x3_tm})
        phase('p39')
        idF, itok = ldid(F32)
        emit_transpose_blocked(nc, ctx, x3_tm, x3, Tn, DM, F32, idF[:, :], itok)
        phase('p40')
        build_amax_part(DM, Tn=Tn, nc=nc, ctx=ctx, io={'x': x3, 'identF': identF,
                                                       'part': am_part})

    cc4 = None
    if seg is None:
        ctx.end_phase()
        ctx.pfx = 'p40b_'
        cc4 = emit_allgather(nc, ctx, 'amax', am_part, am_ag, groups_all)

    if in_seg(5):
        ctx.end_phase()
        ctx.pfx = 'p41_'
        build_ffn_fp8(Tn=Tn, nc=nc, ctx=ctx, io={
            'parts': am_ag.rearrange('a b -> (a b)').unsqueeze(0), 'x': x3, 'w': w1f,
            'bias': b1f, 'resid': x3, 'ones1': ones1, 'identF': identF, 'out': s4})
        phase('p42')
        idF, itok = ldid(F32)
        emit_transpose_blocked(nc, ctx, s4, s4_tm, DM, Tn, F32, idF[:, :], itok)
        phase('p43')
        build_ln(Tn, DM, nc=nc, ctx=ctx, io={'x': s4_tm, 'out': out_ap})
    ctx.end_phase()
    ctx.emit()
    return nc


def _input_names(nc):
    import concourse.mybir as _mb
    return [a.memorylocations[0].name for a in nc.m.functions[0].allocations
            if isinstance(a, _mb.MemoryLocationSet) and a.kind == 'ExternalInput'
            and a.memorylocations[0].name != 'partition_id']


def _output_names(nc):
    import concourse.mybir as _mb
    return [a.memorylocations[0].name for a in nc.m.functions[0].allocations
            if isinstance(a, _mb.MemoryLocationSet) and a.kind == 'ExternalOutput']


def run8(nc, in_maps):
    return run_bass_kernel_spmd(nc, in_maps, core_ids=list(range(NCORES)))


def asc(x):
    return np.ascontiguousarray(x)


def _cores():
    per_b = NCORES // B
    return [(c, c // per_b, (c % per_b) * T) for c in range(NCORES)]


def shard_cm(xg):
    return [asc(xg[b, :, t0:t0 + T]) for (_, b, t0) in _cores()]


def shard_tm(xg):
    return [asc(xg[b, t0:t0 + T, :]) for (_, b, t0) in _cores()]


def gather_cm(outs, C_):
    per_b = NCORES // B
    return np.stack([np.concatenate(outs[b * per_b:(b + 1) * per_b], axis=1)
                     for b in range(B)])


def gather_tm(outs, C_):
    per_b = NCORES // B
    return np.stack([np.concatenate(outs[b * per_b:(b + 1) * per_b], axis=0)
                     for b in range(B)])


def _run(nc, in_maps, out_names):
    res = run8(nc, in_maps)
    return [[np.asarray(r[nm]) for nm in out_names] for r in res.results]


def _forward(inputs, dbg=None):
    inp = {k: np.asarray(v, dtype=np.float32) if np.asarray(v).dtype == np.float32
           else np.asarray(v) for k, v in inputs.items()}
    nch_tot = L // CH0
    per_b = NCORES // B

    u = np.asarray(inp['L0_attn_u'], np.float32)
    uT = asc(u.transpose(0, 2, 1))
    W_inT = asc(np.asarray(inp['L0_attn_W_in'], np.float32).T)
    b_in = np.asarray(inp['L0_attn_b_in'], np.float32).reshape(-1, 1)

    nc = build_linear(DM, D_IN_PROJ, T, bias=True)
    outs = _run(nc, [{'x': xc, 'w': W_inT, 'bias': b_in} for xc in shard_cm(uT)], ['out'])
    zx = gather_cm([o[0] for o in outs], D_IN_PROJ)
    if dbg is not None:
        dbg['zx'] = zx

    xbc = zx[:, DSSM:DSSM + CONV_DIM, :]
    dt_raw = zx[:, DSSM + CONV_DIM:, :]
    conv_w = np.asarray(inp['L0_attn_conv_w'], np.float32)
    conv_b = np.asarray(inp['L0_attn_conv_b'], np.float32).reshape(-1, 1)
    dt_bias = np.asarray(inp['L0_attn_dt_bias'], np.float32).reshape(-1, 1)
    A_log = np.asarray(inp['L0_attn_A_log'], np.float32).reshape(-1, 1)
    xbc_pad_g = np.concatenate([np.zeros((B, CONV_DIM, 3), np.float32), xbc], axis=2)
    maps = []
    for (_, b, t0) in _cores():
        maps.append({'xbc_pad': asc(xbc_pad_g[b, :, t0:t0 + T + 3]),
                     'conv_w': conv_w, 'conv_b': conv_b,
                     'dt_raw': asc(dt_raw[b, :, t0:t0 + T]),
                     'dt_bias': dt_bias, 'A_log': A_log})
    nc = build_mamba_pre(Tn=T)
    outs = _run(nc, maps, ['convout', 'dt', 'cs', 'expcs', 'dsdt'])
    convout = gather_cm([o[0] for o in outs], CONV_DIM)
    dt_g = gather_cm([o[1] for o in outs], NH0)
    cs_g = gather_cm([o[2] for o in outs], NH0)
    expcs_g = gather_cm([o[3] for o in outs], NH0)
    dsdt_g = gather_cm([o[4] for o in outs], NH0)
    if dbg is not None:
        dbg.update(convout=convout, dt=dt_g, cs=cs_g, expcs=expcs_g, dsdt=dsdt_g)

    x_cm_g = convout[:, :DSSM, :]
    B_cm_g = convout[:, DSSM:DSSM + DST, :]
    C_cm_g = convout[:, DSSM + DST:, :]
    dt_tm_g = asc(dt_g.transpose(0, 2, 1))
    cs_tm_g = asc(cs_g.transpose(0, 2, 1))
    dsdt_tm_g = asc(dsdt_g.transpose(0, 2, 1))
    D_bc = np.broadcast_to(np.asarray(inp['L0_attn_D'], np.float32), (128, NH0)).copy()
    mask_ji = asc(np.tril(np.ones((CH0, CH0), np.float32)).T)
    ones1 = np.ones((1, 128), np.float32)
    ident = np.eye(128).astype(bf16)
    maps = []
    for (_, b, t0) in _cores():
        maps.append({'x_cm': asc(x_cm_g[b, :, t0:t0 + T]),
                     'B_cm': asc(B_cm_g[b, :, t0:t0 + T]),
                     'C_cm': asc(C_cm_g[b, :, t0:t0 + T]),
                     'dt_tm': asc(dt_tm_g[b, t0:t0 + T]),
                     'cs_row': asc(cs_g[b, :, t0:t0 + T]),
                     'cs_tm': asc(cs_tm_g[b, t0:t0 + T]),
                     'dsdt_tm': asc(dsdt_tm_g[b, t0:t0 + T]),
                     'D_bc': D_bc, 'mask': mask_ji, 'ones1': ones1, 'ident': ident})
    nc = build_ssd_local(Tn=T, ch=CH0)
    outs = _run(nc, maps, ['ydx', 'states'])
    ydx_g = gather_tm([o[0] for o in outs], DSSM)
    states_g = np.stack([np.concatenate(
        [outs[b * per_b + i][1].reshape(NCH0, DSSM, DST) for i in range(per_b)], axis=0)
        for b in range(B)])
    if dbg is not None:
        dbg['ydx'] = ydx_g
        dbg['states'] = states_g

    HQ = NH0 // (NCORES // B)
    rows = HQ * HD0
    decay = expcs_g[:, :, CH0 - 1::CH0]
    maps = []
    for c in range(NCORES):
        b, hq = c // per_b, c % per_b
        st = asc(states_g[b, :, hq * rows:(hq + 1) * rows, :])
        dcr = np.repeat(decay[b, hq * HQ:(hq + 1) * HQ, :].T, HD0, axis=1)
        dc = asc(dcr.reshape(nch_tot, rows // 128, 128).transpose(0, 2, 1))
        maps.append({'states': st, 'decay_bc': dc})
    nc = build_state_scan(nch_tot, rows)
    outs = _run(nc, maps, ['prev'])
    prev_g = np.stack([np.concatenate([outs[b * per_b + i][0] for i in range(per_b)],
                                      axis=1) for b in range(B)])
    prevT_g = asc(prev_g.transpose(0, 1, 3, 2))
    if dbg is not None:
        dbg['prev'] = prev_g

    expcs_tm_g = asc(expcs_g.transpose(0, 2, 1))
    z_tm_g = asc(zx[:, :DSSM, :].transpose(0, 2, 1))
    maps = []
    for (_, b, t0) in _cores():
        c0 = t0 // CH0
        maps.append({'ydx': asc(ydx_g[b, t0:t0 + T]),
                     'C_cm': asc(C_cm_g[b, :, t0:t0 + T]),
                     'prevT': asc(prevT_g[b, c0:c0 + NCH0].reshape(NCH0 * DST, DSSM)),
                     'expcs_tm': asc(expcs_tm_g[b, t0:t0 + T]),
                     'z_tm': asc(z_tm_g[b, t0:t0 + T])})
    nc = build_ssd_final(Tn=T, ch=CH0)
    outs = _run(nc, maps, ['yn'])
    yn_g = gather_tm([o[0] for o in outs], DSSM)
    if dbg is not None:
        dbg['yn'] = yn_g

    ynT = asc(yn_g.transpose(0, 2, 1))
    W_outT = asc(np.asarray(inp['L0_attn_W_out'], np.float32).T)
    b_out = np.asarray(inp['L0_attn_b_out'], np.float32).reshape(-1, 1)
    nw = np.asarray(inp['L0_attn_norm_w'], np.float32).reshape(-1, 1)
    maps = []
    for i, (_, b, t0) in enumerate(_cores()):
        maps.append({'x': asc(ynT[b, :, t0:t0 + T]), 'w': W_outT, 'bias': b_out,
                     'resid': asc(uT[b, :, t0:t0 + T]), 'wscale': nw})
    nc = build_linear(DSSM, DM, T, bias=True, residual=True, wscale=True,
                      x_hbm_dtype=BF16)
    outs = _run(nc, maps, ['out'])
    s1 = gather_cm([o[0] for o in outs], DM)
    if dbg is not None:
        dbg['s1'] = s1

    s1_tm = asc(s1.transpose(0, 2, 1))
    nc = build_ln(T, DM)
    outs = _run(nc, [{'x': xc} for xc in shard_tm(s1_tm)], ['out'])
    x1_tm = gather_tm([o[0] for o in outs], DM)
    if dbg is not None:
        dbg['x1'] = x1_tm

    x1T = asc(x1_tm.transpose(0, 2, 1))
    w1 = np.asarray(inp['L0_ffn_w1'], np.float32)
    b1 = np.asarray(inp['L0_ffn_b1'], np.float32).reshape(-1, 1)
    w2 = np.asarray(inp['L0_ffn_w2'], np.float32)
    b2 = np.asarray(inp['L0_ffn_b2'], np.float32).reshape(-1, 1)
    nc = build_ffn1(Tn=T)
    outs = _run(nc, [{'x': xc, 'w': w1, 'bias': b1} for xc in shard_cm(x1T)], ['out'])
    g_g = gather_cm([o[0] for o in outs], 4096)
    if dbg is not None:
        dbg['g'] = g_g

    nc = build_linear(2048, DM, T, bias=False, x_hbm_dtype=BF16)
    outs = _run(nc, [{'x': asc(g_g[b, :2048, t0:t0 + T]), 'w': asc(w2[:2048])}
                     for (_, b, t0) in _cores()], ['out'])
    part = gather_cm([o[0] for o in outs], DM)
    nc = build_f2b(Tn=T)
    outs = _run(nc, [{'x': asc(g_g[b, 2048:, t0:t0 + T]), 'w': asc(w2[2048:]),
                      'bias': b2, 'resid': asc(part[b, :, t0:t0 + T]),
                      'resid2': asc(x1T[b, :, t0:t0 + T])}
                     for (_, b, t0) in _cores()], ['out'])
    s2 = gather_cm([o[0] for o in outs], DM)
    s2_tm = asc(s2.transpose(0, 2, 1))
    nc = build_ln(T, DM)
    outs = _run(nc, [{'x': xc} for xc in shard_tm(s2_tm)], ['out'])
    x2_tm = gather_tm([o[0] for o in outs], DM)
    if dbg is not None:
        dbg['x2'] = x2_tm

    x2T = asc(x2_tm.transpose(0, 2, 1))
    Wq = np.asarray(inp['L1_attn_Wq'], np.float32)
    Wk = np.asarray(inp['L1_attn_Wk'], np.float32)
    Wv = np.asarray(inp['L1_attn_Wv'], np.float32)
    Wa = np.asarray(inp['L1_attn_Wa'], np.float32)
    Wb = np.asarray(inp['L1_attn_Wb'], np.float32)
    Wg = np.asarray(inp['L1_attn_Wg'], np.float32)
    Wcat = asc(np.concatenate([Wq, Wk, Wv, Wa, Wb, Wg], axis=1))
    nc = build_linear(DM, 6160, T)
    outs = _run(nc, [{'x': xc, 'w': Wcat} for xc in shard_cm(x2T)], ['out'])
    proj = gather_cm([o[0] for o in outs], 6160)
    q_raw = proj[:, 0:1024, :]
    k_raw = proj[:, 1024:2048, :]
    v_raw = proj[:, 2048:4096, :]
    a_raw = proj[:, 4096:4104, :]
    b_raw = proj[:, 4104:4112, :]
    g_raw = proj[:, 4112:6160, :]
    if dbg is not None:
        dbg['proj'] = proj

    convs = {}
    for nm, arr, wkey in [('q', q_raw, 'L1_attn_conv_q'), ('k', k_raw, 'L1_attn_conv_k'),
                          ('v', v_raw, 'L1_attn_conv_v')]:
        C_ = arr.shape[1]
        pad = np.concatenate([np.zeros((B, C_, 3), np.float32), arr], axis=2)
        wc = np.asarray(inp[wkey], np.float32)
        nc = build_conv_silu(C_, Tn=T)
        outs = _run(nc, [{'x_pad': asc(pad[b, :, t0:t0 + T + 3]), 'w': wc}
                         for (_, b, t0) in _cores()], ['out'])
        convs[nm] = gather_cm([o[0] for o in outs], C_)
    if dbg is not None:
        dbg.update(qc=convs['q'], kc=convs['k'], vc=convs['v'])

    q_tm = asc(convs['q'].transpose(0, 2, 1))
    k_tm = asc(convs['k'].transpose(0, 2, 1))
    nc = build_l2norm_qk(Tn=T)
    outs = _run(nc, [{'q_tm': asc(q_tm[b, t0:t0 + T]), 'k_tm': asc(k_tm[b, t0:t0 + T])}
                     for (_, b, t0) in _cores()], ['qn', 'kn'])
    qn_g = gather_tm([o[0] for o in outs], KEY_DIM)
    kn_g = gather_tm([o[1] for o in outs], KEY_DIM)
    if dbg is not None:
        dbg.update(qn=qn_g, kn=kn_g)

    dtb1 = np.asarray(inp['L1_attn_dt_bias'], np.float32).reshape(-1, 1)
    alog1 = np.asarray(inp['L1_attn_A_log'], np.float32).reshape(-1, 1)
    nc = build_ab_rows(Tn=T, ch=CH1)
    outs = _run(nc, [{'a_raw': asc(a_raw[b, :, t0:t0 + T]),
                      'b_raw': asc(b_raw[b, :, t0:t0 + T]),
                      'dt_bias': dtb1, 'A_log': alog1} for (_, b, t0) in _cores()],
                ['lg', 'beta'])
    lg_g = gather_cm([o[0] for o in outs], H1)
    beta_g = gather_cm([o[1] for o in outs], H1)
    if dbg is not None:
        dbg.update(lg=lg_g, beta=beta_g)

    qn_cm = asc(qn_g.transpose(0, 2, 1))
    kn_cm = asc(kn_g.transpose(0, 2, 1))
    v_tm_g = asc(convs['v'].transpose(0, 2, 1))
    lg_tm_g = asc(lg_g.transpose(0, 2, 1))
    beta_tm_g = asc(beta_g.transpose(0, 2, 1))
    identF = np.eye(128, dtype=np.float32)
    ident2 = (2 * np.eye(CH1)).astype(np.float32)
    mask_incl = asc(np.tril(np.ones((CH1, CH1), np.float32)).T)
    mask_strict = asc(np.tril(np.ones((CH1, CH1), np.float32), -1).T)
    ones1 = np.ones((1, 128), np.float32)
    nc = build_delta_local(Tn=T, ch=CH1)
    maps = []
    for (_, b, t0) in _cores():
        maps.append({'q_cm': asc(qn_cm[b, :, t0:t0 + T]), 'k_cm': asc(kn_cm[b, :, t0:t0 + T]),
                     'q_tm': asc(qn_g[b, t0:t0 + T]), 'k_tm': asc(kn_g[b, t0:t0 + T]),
                     'v_tm': asc(v_tm_g[b, t0:t0 + T]),
                     'lg_row': asc(lg_g[b, :, t0:t0 + T]),
                     'lg_tm': asc(lg_tm_g[b, t0:t0 + T]),
                     'beta_tm': asc(beta_tm_g[b, t0:t0 + T]),
                     'identF': identF, 'ident2': ident2, 'mask_incl': mask_incl,
                     'mask_strict': mask_strict, 'ones1': ones1})
    outs = _run(nc, maps, ['O_loc', 'Mt', 'SlocT', 'GT', 'gamc'])
    O_loc_g = gather_tm([o[0] for o in outs], VAL_DIM)
    nchg = L // CH1
    Mt_g = np.stack([np.concatenate(
        [outs[b * per_b + i][1].reshape(NCH1, H1, KD1, KD1) for i in range(per_b)], axis=0)
        for b in range(B)])
    Sl_g = np.stack([np.concatenate(
        [outs[b * per_b + i][2].reshape(NCH1, H1, KD1, VD1) for i in range(per_b)], axis=0)
        for b in range(B)])
    GT_g = np.stack([np.concatenate(
        [outs[b * per_b + i][3].reshape(NCH1, H1, KD1, CH1) for i in range(per_b)], axis=0)
        for b in range(B)])
    gamc_g = np.stack([np.concatenate(
        [outs[b * per_b + i][4] for i in range(per_b)], axis=0) for b in range(B)])
    if dbg is not None:
        dbg.update(O_loc=O_loc_g, Mt=Mt_g, Sl=Sl_g, GT=GT_g, gamc=gamc_g)

    units = [(b, h) for b in range(B) for h in range(H1)]
    upc = len(units) // NCORES
    maps = []
    for c in range(NCORES):
        us = units[c * upc:(c + 1) * upc]
        maps.append({
            'Mt': asc(np.concatenate([Mt_g[b, :, h] for (b, h) in us], axis=0)
                      ).reshape(upc * nchg * KD1, KD1),
            'SlocT': asc(np.concatenate([Sl_g[b, :, h] for (b, h) in us], axis=0)
                         ).reshape(upc * nchg * KD1, VD1),
            'GT': asc(np.concatenate([GT_g[b, :, h] for (b, h) in us], axis=0)
                      ).reshape(upc * nchg * KD1, CH1),
            'gam_bc': asc(np.broadcast_to(
                np.concatenate([gamc_g[b, :, h] for (b, h) in us])[:, None],
                (upc * nchg, 128)).copy()),
        })
    nc = build_delta_chain(upc, nchg, ch=CH1)
    outs = _run(nc, maps, ['O_corr'])
    O_corr_g = np.zeros((B, L, VAL_DIM), np.float32)
    for c in range(NCORES):
        oc = outs[c][0].reshape(upc, nchg, CH1, VD1)
        for iu, (b, h) in enumerate(units[c * upc:(c + 1) * upc]):
            O_corr_g[b, :, h * VD1:(h + 1) * VD1] = oc[iu].reshape(L, VD1)
    if dbg is not None:
        dbg['O_corr'] = O_corr_g

    g_tm_g = asc(g_raw.transpose(0, 2, 1))
    nc = build_gate_out(Tn=T)
    outs = _run(nc, [{'O_loc': asc(O_loc_g[b, t0:t0 + T]),
                      'O_corr': asc(O_corr_g[b, t0:t0 + T]),
                      'g_tm': asc(g_tm_g[b, t0:t0 + T])} for (_, b, t0) in _cores()],
                ['og'])
    og_g = gather_tm([o[0] for o in outs], VAL_DIM)
    if dbg is not None:
        dbg['og'] = og_g

    ogT = asc(og_g.transpose(0, 2, 1))
    WoT = asc(np.asarray(inp['L1_attn_Wo'], np.float32))
    onw = np.tile(np.asarray(inp['L1_attn_o_norm_w'], np.float32), H1).reshape(-1, 1)
    nc = build_linear(VAL_DIM, DM, T, residual=True, wscale=True, x_hbm_dtype=BF16)
    outs = _run(nc, [{'x': asc(ogT[b, :, t0:t0 + T]), 'w': WoT, 'wscale': onw,
                      'resid': asc(x2T[b, :, t0:t0 + T])} for (_, b, t0) in _cores()],
                ['out'])
    s3 = gather_cm([o[0] for o in outs], DM)
    s3_tm = asc(s3.transpose(0, 2, 1))
    nc = build_ln(T, DM)
    outs = _run(nc, [{'x': xc} for xc in shard_tm(s3_tm)], ['out'])
    x3_tm = gather_tm([o[0] for o in outs], DM)
    if dbg is not None:
        dbg['x3'] = x3_tm

    x3T = asc(x3_tm.transpose(0, 2, 1))
    nc = build_amax_part(DM, Tn=T)
    outs = _run(nc, [{'x': asc(x3T[b, :, t0:t0 + T]), 'identF': identF}
                     for (_, b, t0) in _cores()], ['part'])
    parts = asc(np.stack([o[0][0, 0] for o in outs])[None, :])
    w1f = asc(np.asarray(inp['L1_ffn_weight'], np.float32).T)
    b1f = np.asarray(inp['L1_ffn_bias'], np.float32).reshape(-1, 1)
    nc = build_ffn_fp8(Tn=T)
    outs = _run(nc, [{'parts': parts, 'x': asc(x3T[b, :, t0:t0 + T]), 'w': w1f,
                      'bias': b1f, 'resid': asc(x3T[b, :, t0:t0 + T]),
                      'ones1': ones1, 'identF': identF} for (_, b, t0) in _cores()],
                ['out'])
    s4 = gather_cm([o[0] for o in outs], DM)
    s4_tm = asc(s4.transpose(0, 2, 1))
    nc = build_ln(T, DM)
    outs = _run(nc, [{'x': xc} for xc in shard_tm(s4_tm)], ['out'])
    x4_tm = gather_tm([o[0] for o in outs], DM)
    return x4_tm


FUSED = True
FUSED_MODE = '1'


def _forward_fused(inputs, mode='5'):
    inp = {k: np.asarray(v) for k, v in inputs.items()}
    f32 = lambda k: np.asarray(inp[k], np.float32)
    per_b = NCORES // B

    u = f32('L0_attn_u')
    uT = asc(u.transpose(0, 2, 1))
    u_pad = np.concatenate([np.zeros((B, DM, 3), np.float32), uT], axis=2)

    common = {
        'W_inT': asc(f32('L0_attn_W_in').T),
        'b_in': f32('L0_attn_b_in').reshape(-1, 1),
        'conv_w': f32('L0_attn_conv_w'),
        'conv_b': f32('L0_attn_conv_b').reshape(-1, 1),
        'dt_bias': f32('L0_attn_dt_bias').reshape(-1, 1),
        'A_log': f32('L0_attn_A_log').reshape(-1, 1),
        'D_bc': np.broadcast_to(f32('L0_attn_D'), (128, NH0)).copy(),
        'mask0': asc(np.tril(np.ones((CH0, CH0), np.float32)).T),
        'ones1': np.ones((1, 128), np.float32),
        'identB': np.eye(128).astype(bf16),
        'identF': np.eye(128, dtype=np.float32),
        'norm_w': f32('L0_attn_norm_w').reshape(-1, 1),
        'W_outT': asc(f32('L0_attn_W_out').T),
        'b_out': f32('L0_attn_b_out').reshape(-1, 1),
        'w1': f32('L0_ffn_w1'),
        'b1': f32('L0_ffn_b1').reshape(-1, 1),
        'w2': f32('L0_ffn_w2'),
        'b2': f32('L0_ffn_b2').reshape(-1, 1),
        'Wcat': asc(np.concatenate([f32('L1_attn_Wq'), f32('L1_attn_Wk'),
                                    f32('L1_attn_Wv'), f32('L1_attn_Wa'),
                                    f32('L1_attn_Wb'), f32('L1_attn_Wg')], axis=1)),
        'convq_w': f32('L1_attn_conv_q'),
        'convk_w': f32('L1_attn_conv_k'),
        'convv_w': f32('L1_attn_conv_v'),
        'dt_bias1': f32('L1_attn_dt_bias').reshape(-1, 1),
        'A_log1': f32('L1_attn_A_log').reshape(-1, 1),
        'ident2': (2 * np.eye(CH1)).astype(np.float32),
        'mask_i1': asc(np.tril(np.ones((CH1, CH1), np.float32)).T),
        'mask_s1': asc(np.tril(np.ones((CH1, CH1), np.float32), -1).T),
        'onw': np.tile(f32('L1_attn_o_norm_w'), H1).reshape(-1, 1),
        'WoT': f32('L1_attn_Wo'),
        'w1f': asc(f32('L1_ffn_weight').T),
        'b1f': f32('L1_ffn_bias').reshape(-1, 1),
    }

    pool = []
    for (c, b, t0) in _cores():
        blk = c % per_b
        m = dict(common)
        if blk == 0:
            m['u_hal'] = asc(u_pad[b, :, t0:t0 + T + 3])
            m['hmask'] = np.zeros((128, 1), np.float32)
        else:
            m['u_hal'] = asc(uT[b, :, t0 - 3:t0 + T])
            m['hmask'] = np.ones((128, 1), np.float32)
        mm = np.zeros((128, per_b), np.float32)
        mm[:, :blk] = 1.0
        m['mam_mask'] = mm
        hs = np.zeros((128, per_b), np.float32)
        if blk > 0:
            hs[:, blk - 1] = 1.0
        m['halo_sel'] = hs
        pool.append(m)

    if mode == '1':
        nc = build_fused()
        res = run8(nc, [{n: pool[c][n] for n in _input_names(nc)}
                        for c in range(NCORES)])
        outs = res.results
    else:
        gathers = {
            1: [('ag_m_out', 'ag_m_in', 'group')],
            2: [('halo_ag', 'halo_pub', 'group')],
            3: [('ag_d_out', 'ag_d_in', 'group')],
            4: [('am_ag', 'am_part', 'all')],
        }
        for s in range(1, 6):
            nc = build_fused(seg=s)
            innames = _input_names(nc)
            outnames = _output_names(nc)
            res = run8(nc, [{n: pool[c][n] for n in innames} for c in range(NCORES)])
            for c in range(NCORES):
                for n in outnames:
                    pool[c][n] = np.asarray(res.results[c][n])
            for (dst, srcn, scope) in gathers.get(s, []):
                if scope == 'group':
                    for b in range(B):
                        cat = asc(np.concatenate(
                            [pool[b * per_b + i][srcn] for i in range(per_b)], axis=0))
                        for i in range(per_b):
                            pool[b * per_b + i][dst] = cat
                else:
                    cat = asc(np.stack([pool[c][srcn][0, 0] for c in range(NCORES)]
                                       )[:, None])
                    for c in range(NCORES):
                        pool[c][dst] = cat
        outs = [pool[c] for c in range(NCORES)]

    out = np.stack([np.concatenate(
        [np.asarray(outs[b * per_b + i]['out']) for i in range(per_b)], axis=0)
        for b in range(B)])
    return out


def kernel(**inputs):
    if FUSED:
        return _forward_fused(inputs, mode=FUSED_MODE)
    return _forward(inputs)
